# version 28
# baseline (speedup 1.0000x reference)
import numpy as np
import ml_dtypes
import concourse.bass as bass
import concourse.mybir as mybir
from concourse.bass_utils import run_bass_kernel_spmd

F32 = mybir.dt.float32
BF16 = mybir.dt.bfloat16
AF = mybir.ActivationFunctionType
ALU = mybir.AluOpType
AX = mybir.AxisListType

B, T, D = 8, 2048, 1024
DT = D // 128
NT = T // 512
ENGINES = ("sync", "vector", "scalar", "tensor", "gpsimd")

ALL_PHASES = ("l0_mix", "l0_attn", "l0_glu", "l0_ffn", "l1_mamba", "l1_moe")


class Buf:
    __slots__ = ("name", "handle", "parts", "free", "dtype", "space")

    def __init__(self, name, handle, parts, free, dtype, space):
        self.name, self.handle = name, handle
        self.parts, self.free, self.dtype, self.space = parts, free, dtype, space

    @property
    def tv(self):
        return TV(self, 0, self.parts, 0, self.free)

    def __getitem__(self, sl):
        return self.tv[sl]


class TV:
    __slots__ = ("buf", "p0", "p1", "f0", "f1")

    def __init__(self, buf, p0, p1, f0, f1):
        self.buf, self.p0, self.p1, self.f0, self.f1 = buf, p0, p1, f0, f1
        assert 0 <= p0 < p1 <= buf.parts and 0 <= f0 < f1 <= buf.free, \
            f"bad TV {buf.name}[{p0}:{p1},{f0}:{f1}] of [{buf.parts},{buf.free}]"

    @property
    def ap(self):
        return self.buf.handle[self.p0:self.p1, self.f0:self.f1]

    @property
    def P(self):
        return self.p1 - self.p0

    @property
    def F(self):
        return self.f1 - self.f0

    def __getitem__(self, sl):
        psl, fsl = sl if isinstance(sl, tuple) else (sl, slice(None))
        p0, p1, st = psl.indices(self.P)
        assert st == 1
        f0, f1, st = fsl.indices(self.F)
        assert st == 1
        return TV(self.buf, self.p0 + p0, self.p0 + p1, self.f0 + f0, self.f0 + f1)

    def overlaps(self, o):
        return (self.buf is o.buf and self.p0 < o.p1 and o.p0 < self.p1
                and self.f0 < o.f1 and o.f0 < self.f1)

    def covers(self, o):
        return (self.buf is o.buf and self.p0 <= o.p0 and self.p1 >= o.p1
                and self.f0 <= o.f0 and self.f1 >= o.f1)


class Op:
    __slots__ = ("i", "eng", "fn", "reads", "writes", "dma_sem", "need_inc", "tag")

    def __init__(self, i, eng, fn, reads, writes, dma_sem, tag):
        self.i, self.eng, self.fn = i, eng, fn
        self.reads, self.writes, self.dma_sem = reads, writes, dma_sem
        self.need_inc = False
        self.tag = tag


class Arena:

    def __init__(self, buf, align=16):
        self.buf = buf
        self.align = align
        self.free_list = [(0, buf.free)]
        self.live = {}

    def alloc(self, name, elems):
        a = self.align
        elems = -(-elems // a) * a
        for idx, (lo, hi) in enumerate(self.free_list):
            lo_a = -(-lo // a) * a
            if hi - lo_a >= elems:
                if lo_a > lo:
                    self.free_list[idx] = (lo, lo_a)
                    self.free_list.insert(idx + 1, (lo_a + elems, hi))
                else:
                    self.free_list[idx] = (lo_a + elems, hi)
                    if self.free_list[idx][0] >= self.free_list[idx][1]:
                        del self.free_list[idx]
                self.live[name] = (lo_a, lo_a + elems)
                return TV(self.buf, 0, self.buf.parts, lo_a, lo_a + elems)
        raise MemoryError(
            f"arena {self.buf.name}: cannot alloc {elems} for {name}; "
            f"free={self.free_list} live={self.live}")

    def free(self, name):
        lo, hi = self.live.pop(name)
        self.free_list.append((lo, hi))
        self.free_list.sort()
        merged = []
        for seg in self.free_list:
            if merged and merged[-1][1] >= seg[0]:
                merged[-1] = (merged[-1][0], max(merged[-1][1], seg[1]))
            else:
                merged.append(list(seg))
        self.free_list = [tuple(s) for s in merged]


class Prog:
    def __init__(self, nc):
        self.nc = nc
        self.ops = []
        self.deps = []
        self.records = {}
        self._ctxs = []

    def sbuf(self, name, parts, free, dtype):
        cm = self.nc.sbuf_tensor(name, [parts, free], dtype)
        h = cm.__enter__()
        self._ctxs.append(cm)
        return Buf(name, h, parts, free, dtype, "sbuf")

    def psum(self, name, parts, free, dtype):
        cm = self.nc.psum_tensor(name, [parts, free], dtype)
        h = cm.__enter__()
        self._ctxs.append(cm)
        return Buf(name, h, parts, free, dtype, "psum")

    def close_bufs(self):
        for cm in reversed(self._ctxs):
            cm.__exit__(None, None, None)
        self._ctxs = []

    def op(self, eng, fn, reads=(), writes=(), dma=None, tag=""):
        i = len(self.ops)
        reads = tuple(tv.buf.tv if tv.buf.space == "psum" else tv for tv in reads)
        writes = tuple(tv.buf.tv if tv.buf.space == "psum" else tv for tv in writes)
        o = Op(i, eng, fn, tuple(reads), tuple(writes), dma, tag)
        dep = set()
        for tv in o.reads:
            for (rtv, w, _rd) in self.records.get(tv.buf.name, ()):
                if w is not None and tv.overlaps(rtv):
                    dep.add(w)
        for tv in o.writes:
            for (rtv, w, rd) in self.records.get(tv.buf.name, ()):
                if tv.overlaps(rtv):
                    if w is not None:
                        dep.add(w)
                    dep.update(rd)
        for tv in o.reads:
            recs = self.records.setdefault(tv.buf.name, [])
            hit = False
            for r in recs:
                if r[0].covers(tv) and r[1] is not None:
                    r[2].append(i)
                    hit = True
            if not hit:
                recs.append([tv, None, [i]])
        for tv in o.writes:
            recs = self.records.setdefault(tv.buf.name, [])
            recs[:] = [r for r in recs if not tv.covers(r[0])]
            recs.append([tv, i, []])
        dep.discard(i)
        self.ops.append(o)
        self.deps.append(dep)
        return i

    def dep(self, a, b):
        self.deps[b].add(a)

    def barrier_read(self, tvs, eng="sync"):
        return self.op(eng, lambda e: e.nop(), reads=tvs, tag="barrier")

    def emit(self, block, sems):
        ops, deps = self.ops, self.deps
        for o in ops:
            for d in deps[o.i]:
                p = ops[d]
                if p.dma_sem is None and p.eng == "tensor" and o.eng == "tensor":
                    continue
                if p.dma_sem is None:
                    p.need_inc = True
        inc_count = {e: 0 for e in ENGINES}
        dma_count = {}
        prod_val = {}
        for o in ops:
            if o.dma_sem is not None:
                c = dma_count.get(o.dma_sem, 0) + 1
                dma_count[o.dma_sem] = c
                prod_val[o.i] = (o.dma_sem, 16 * c)
            elif o.need_inc:
                inc_count[o.eng] += 1
                prod_val[o.i] = (o.eng, inc_count[o.eng])
            else:
                prod_val[o.i] = None
        waits = []
        for o in ops:
            w = {}
            for d in deps[o.i]:
                p = ops[d]
                if p.dma_sem is None and p.eng == "tensor" and o.eng == "tensor":
                    continue
                s, v = prod_val[d]
                if w.get(s, -1) < v:
                    w[s] = v
            waits.append(w)
        self.n_incs = dict(inc_count)
        self.n_dma = dict(dma_count)

        def body_for(engine):
            def body(eng):
                hw = {}
                for o in ops:
                    if o.eng != engine:
                        continue
                    for s, v in sorted(waits[o.i].items()):
                        if hw.get(s, 0) >= v:
                            continue
                        eng.wait_ge(sems[s], v)
                        hw[s] = v
                    inst = o.fn(eng)
                    if o.dma_sem is not None:
                        inst.then_inc(sems[o.dma_sem], 16)
                    elif o.need_inc:
                        inst.then_inc(sems[o.eng], 1)
            return body

        for e in ENGINES:
            getattr(block, e)(body_for(e))

    def dma_sems_used(self):
        return sorted({o.dma_sem for o in self.ops if o.dma_sem is not None})


def lhsT_panels(w_out_in):
    O, I = w_out_in.shape
    assert O % 128 == 0 and I % 128 == 0, (O, I)
    MT, KT = O // 128, I // 128
    a = w_out_in.reshape(MT, 128, KT, 128).transpose(0, 3, 2, 1)
    return np.ascontiguousarray(a.reshape(MT, 128, KT * 128), dtype=np.float32)


def lhsT_panels_rows(w_out_in, row_lo, row_hi):
    w = w_out_in[row_lo:row_hi]
    M, I = w.shape
    KT = I // 128
    a = w.reshape(M, KT, 128).transpose(2, 1, 0)
    return np.ascontiguousarray(a.reshape(1, 128, KT * M), dtype=np.float32)


def rhs_panels(w_out_in):
    O, I = w_out_in.shape
    assert I % 128 == 0
    KT = I // 128
    return np.ascontiguousarray(w_out_in.T.reshape(KT, 128, O), dtype=np.float32)


def vec_image(v):
    v = np.asarray(v, dtype=np.float32).ravel()
    n = v.size // 128
    assert v.size == n * 128, v.size
    return np.ascontiguousarray(v.reshape(n, 128).T)


class VecPack:
    def __init__(self):
        self.cols = []
        self.index = {}
        self.n = 0

    def add(self, name, v):
        img = vec_image(v)
        self.index[name] = (self.n, img.shape[1])
        self.cols.append(img)
        self.n += img.shape[1]

    def add_image(self, name, img):
        img = np.ascontiguousarray(np.asarray(img, np.float32))
        assert img.shape[0] == 128
        self.index[name] = (self.n, img.shape[1])
        self.cols.append(img)
        self.n += img.shape[1]

    def add_scalar_slot(self, name):
        self.index[name] = (self.n, 1)
        self.cols.append(np.zeros((128, 1), np.float32))
        self.n += 1

    def array(self):
        return np.ascontiguousarray(np.concatenate(self.cols, axis=1))


class Ctx:
    def __init__(self, prog):
        self.p = prog
        self.af = None
        self.ab = None
        self.banks_f32 = []
        self.banks_bf = []
        self._rrf = 0
        self._rrb = 0
        self.vp = None
        self.of32 = None
        self.if32 = None
        self.sel16 = None
        self.masks = None
        self.vp_index = {}
        self.ones = None
        self.ident = None
        self.rowb = None
        self.rowf = None
        self.drams = {}

    def bank(self):
        b = self.banks_f32[self._rrf % len(self.banks_f32)]
        self._rrf += 1
        return b

    def bankbf(self):
        b = self.banks_bf[self._rrb % len(self.banks_bf)]
        self._rrb += 1
        return b

    def vcol(self, name, j=0):
        lo, n = self.vp_index[name]
        assert j < n, (name, j, n)
        return self.vp[:, lo + j:lo + j + 1]

    def vcols(self, name, j0, j1):
        lo, n = self.vp_index[name]
        assert j1 <= n
        return self.vp[:, lo + j0:lo + j1]


def emit_consts(ctx):
    p = ctx.p
    ones = p.sbuf("c_ones", 128, 128, BF16)
    ident = p.sbuf("c_ident", 128, 128, BF16)
    of32 = p.sbuf("c_of32", 128, 128, F32)
    ctx.ones, ctx.ident, ctx.of32 = ones, ident, of32.tv
    p.op("vector", lambda e: e.memset(ones.tv.ap, 1.0), writes=[ones.tv])
    p.op("vector", lambda e: e.memset(of32.tv.ap, 1.0), writes=[of32.tv])
    p.op("vector", lambda e: e.memset(ident.tv.ap, 1.0), writes=[ident.tv])
    p.op("gpsimd",
         lambda e: e.affine_select(ident.tv.ap, ident.tv.ap, [[1, 128]],
                                   ALU.is_equal, 0.0, base=0,
                                   channel_multiplier=-1),
         reads=[ident.tv], writes=[ident.tv])
    if32 = p.sbuf("c_if32", 128, 128, F32)
    ctx.if32 = if32.tv
    p.op("vector", lambda e: e.memset(if32.tv.ap, 1.0), writes=[if32.tv])
    p.op("gpsimd",
         lambda e: e.affine_select(if32.tv.ap, if32.tv.ap, [[1, 128]],
                                   ALU.is_equal, 0.0, base=0,
                                   channel_multiplier=-1),
         reads=[if32.tv], writes=[if32.tv])


class WStream:

    def __init__(self, ctx, name, dram_ap, n_panels, panel_free, nslots=4):
        self.ctx, self.name = ctx, name
        self.dram = dram_ap
        self.n_panels, self.pf = n_panels, panel_free
        self.slots = [ctx.ab.alloc(f"{name}_s{i}", panel_free) for i in range(nslots)]
        self.nslots = nslots
        self.count = 0

    def panel(self, i):
        return self.load(self.dram[i])

    def load(self, src_ap):
        ctx = self.ctx
        slot = self.slots[self.count % self.nslots]
        sem = f"w_{self.name}_{self.count % self.nslots}"
        self.count += 1
        shp = list(src_ap.shape)
        fw = int(np.prod(shp[1:])) if len(shp) > 1 else int(shp[0])
        assert fw <= slot.F, (self.name, fw, slot.F)
        dst = slot[:, 0:fw] if fw < slot.F else slot
        ctx.p.op("gpsimd", lambda e, d=dst, s=src_ap: e.dma_start(out=d.ap, in_=s),
                 writes=[dst], dma=sem, tag="wload")
        return dst

    def release(self):
        for i in range(self.nslots):
            self.ctx.ab.free(f"{self.name}_s{i}")


def matmul_acc(ctx, psum_tv, pairs, tag="mm"):
    p = ctx.p
    n = len(pairs)
    for k, (lt, rt) in enumerate(pairs):
        p.op("tensor",
             lambda e, o=psum_tv, a=lt, b=rt, s=(k == 0), t=(k == n - 1):
                 e.matmul(o.ap, a.ap, b.ap, start=s, stop=t),
             reads=[lt, rt], writes=[psum_tv], tag=tag)


def transpose128(ctx, psum_bf_tv, src_tv):
    ident = ctx.ident
    ctx.p.op("tensor",
             lambda e, o=psum_bf_tv, i=src_tv, idt=ident.tv:
                 e.transpose(o.ap, i.ap, idt.ap),
             reads=[src_tv, ident.tv], writes=[psum_bf_tv], tag="transp")


def bcast_row(ctx, row_tv, out_psum_tv, tag="bcast"):
    lhsT = ctx.ones[0:1, 0:out_psum_tv.P]
    ctx.p.op("tensor",
             lambda e, o=out_psum_tv, l=lhsT, r=row_tv:
                 e.matmul(o.ap, l.ap, r.ap, start=True, stop=True),
             reads=[row_tv, lhsT], writes=[out_psum_tv], tag=tag)


def emit_norm(ctx, src_tiles, out_tiles, kind, gamma=None, eps=1e-5,
              out2_tiles=None, eng_apply="vector", exact=False, keep_rows=None,
              keep_cols=None):
    p = ctx.p
    DTl = len(src_tiles)
    Dl = DTl * 128
    src_bf = src_tiles[0].buf.dtype == BF16
    tb = [ctx.ab.alloc(f"nrm_tb{i}", 512) for i in range(2)]
    if exact:
        sq = [ctx.af.alloc(f"nrm_sq{i}", 512) for i in range(2)]
        onecol = ctx.of32[:, 0:1]
    else:
        sq = [ctx.ab.alloc(f"nrm_sq{i}", 512) for i in range(2)]
        onecol = ctx.ones[:, 0:1]
    d1 = [ctx.af.alloc(f"nrm_d1{i}", 512) for i in range(2)]
    rowb, rowf = ctx.rowb, ctx.rowf
    ebias = ctx.vcol("eps_1e5" if eps == 1e-5 else "eps_1e6")[0:1, :]
    Tl = src_tiles[0].F
    NTl = Tl // 512
    for n in range(NTl):
        t0, t1 = n * 512, (n + 1) * 512
        ps_sum = ctx.bank()
        ps_sumsq = ctx.bank()
        sum_tv = ps_sum.tv[0:1, 0:512]
        sumsq_tv = ps_sumsq.tv[0:1, 0:512]
        for dt_ in range(DTl):
            s = src_tiles[dt_][:, t0:t1]
            if kind == "ln":
                if exact:
                    xb = s
                elif src_bf:
                    xb = s
                else:
                    xb = tb[dt_ % 2]
                    p.op("vector", lambda e, o=xb, i=s: e.tensor_copy(o.ap, i.ap),
                         reads=[s], writes=[xb], tag="nrm_cast")
                p.op("tensor",
                     lambda e, o=sum_tv, a=onecol, b=xb, s_=(dt_ == 0),
                            t_=(dt_ == DTl - 1):
                         e.matmul(o.ap, a.ap, b.ap, start=s_, stop=t_),
                     reads=[onecol, xb], writes=[sum_tv], tag="nrm_mmx")
            sqt = sq[dt_ % 2]
            p.op("scalar", lambda e, o=sqt, i=s: e.activation(o.ap, i.ap, AF.Square),
                 reads=[s], writes=[sqt], tag="nrm_sq")
            p.op("tensor",
                 lambda e, o=sumsq_tv, a=onecol, b=sqt, s_=(dt_ == 0),
                        t_=(dt_ == DTl - 1):
                     e.matmul(o.ap, a.ap, b.ap, start=s_, stop=t_),
                 reads=[onecol, sqt], writes=[sumsq_tv], tag="nrm_mmsq")
        r0 = rowf.tv[0:1, 0:512]
        r1 = rowf.tv[0:1, 512:1024]
        r2 = rowf.tv[0:1, 1024:1536]
        r3 = rowf.tv[0:1, 1536:2048]
        if kind == "ln":
            p.op("vector", lambda e, o=r2, i=sum_tv:
                 e.tensor_scalar(o.ap, i.ap, 1.0 / Dl, None, ALU.mult),
                 reads=[sum_tv], writes=[r2], tag="nrm_m")
            p.op("scalar", lambda e, o=r3, i=r2: e.activation(o.ap, i.ap, AF.Square),
                 reads=[r2], writes=[r3], tag="nrm_msq")
            p.op("vector", lambda e, o=r0, i=sumsq_tv, s2=r3:
                 e.scalar_tensor_tensor(o.ap, i.ap, 1.0 / Dl, s2.ap,
                                        ALU.mult, ALU.subtract),
                 reads=[sumsq_tv, r3], writes=[r0], tag="nrm_var")
        else:
            p.op("vector", lambda e, o=r0, i=sumsq_tv:
                 e.tensor_scalar(o.ap, i.ap, 1.0 / Dl, None, ALU.mult),
                 reads=[sumsq_tv], writes=[r0], tag="nrm_var")
        p.op("scalar", lambda e, o=r1, i=r0, b=ebias:
             e.activation(o.ap, i.ap, AF.Sqrt, bias=b.ap),
             reads=[r0, ebias], writes=[r1], tag="nrm_sd")
        p.op("vector", lambda e, o=r3, i=r1: e.reciprocal(o.ap, i.ap),
             reads=[r1], writes=[r3], tag="nrm_rstd")
        rs_b = rowb.tv[0:1, 0:512]
        p.op("vector", lambda e, o=rs_b, i=r3: e.tensor_copy(o.ap, i.ap),
             reads=[r3], writes=[rs_b], tag="nrm_rb")
        if keep_rows is not None:
            km, kr = keep_rows
            o_ = kr[0:1, t0:t1]
            p.op("vector", lambda e, o=o_, i=r3: e.tensor_copy(o.ap, i.ap),
                 reads=[r3], writes=[o_], tag="nrm_keepr")
            o_ = km[0:1, t0:t1]
            p.op("vector", lambda e, o=o_, i=r2: e.tensor_copy(o.ap, i.ap),
                 reads=[r2], writes=[o_], tag="nrm_keepm")
        if keep_cols is not None:
            kmc, krc = keep_cols
            for ti in range(4):
                for (row, cols) in ((r2, kmc), (r3, krc)):
                    psy = ctx.bank()
                    pc = psy.tv[:, 0:1]
                    lt = row[0:1, ti * 128:(ti + 1) * 128]
                    p.op("tensor", lambda e, o=pc, a=lt, b=ctx.of32[0:1, 0:1]:
                         e.matmul(o.ap, a.ap, b.ap, start=True, stop=True),
                         reads=[lt, ctx.of32[0:1, 0:1]], writes=[pc], tag="mm_kc")
                    o_ = cols[:, n * 4 + ti:n * 4 + ti + 1]
                    p.op("vector", lambda e, o=o_, i=pc: e.tensor_copy(o.ap, i.ap),
                         reads=[pc], writes=[o_], tag="ev_kc")
        ps_r = ctx.bank()
        rB = ps_r.tv[:, 0:512]
        bcast_row(ctx, rs_b, rB)
        if kind == "ln":
            m_b = rowb.tv[0:1, 512:1024]
            p.op("vector", lambda e, o=m_b, i=r2: e.tensor_copy(o.ap, i.ap),
                 reads=[r2], writes=[m_b], tag="nrm_mb")
            ps_m = ctx.bank()
            mB = ps_m.tv[:, 0:512]
            bcast_row(ctx, m_b, mB)
        for dt_ in range(DTl):
            s = src_tiles[dt_][:, t0:t1]
            o = out_tiles[dt_][:, t0:t1]
            if kind == "ln":
                dd = d1[dt_ % 2]
                p.op(eng_apply, lambda e, a=s, b=mB, oo=dd:
                     e.tensor_tensor(oo.ap, a.ap, b.ap, ALU.subtract),
                     reads=[s, mB], writes=[dd], tag="nrm_sub")
                p.op(eng_apply, lambda e, a=dd, b=rB, oo=o:
                     e.tensor_tensor(oo.ap, a.ap, b.ap, ALU.mult),
                     reads=[dd, rB], writes=[o], tag="nrm_mul")
                if out2_tiles is not None:
                    o2 = out2_tiles[dt_][:, t0:t1]
                    p.op(eng_apply, lambda e, a=dd, b=rB, oo=o2:
                         e.tensor_tensor(oo.ap, a.ap, b.ap, ALU.mult),
                         reads=[dd, rB], writes=[o2], tag="nrm_mul2")
            else:
                g = ctx.vcol(gamma, dt_)
                p.op(eng_apply, lambda e, a=s, sc=g, b=rB, oo=o:
                     e.scalar_tensor_tensor(oo.ap, a.ap, sc.ap, b.ap,
                                            ALU.mult, ALU.mult),
                     reads=[s, rB, g], writes=[o], tag="nrm_app")
                if out2_tiles is not None:
                    o2 = out2_tiles[dt_][:, t0:t1]
                    p.op(eng_apply, lambda e, a=s, sc=g, b=rB, oo=o2:
                         e.scalar_tensor_tensor(oo.ap, a.ap, sc.ap, b.ap,
                                                ALU.mult, ALU.mult),
                         reads=[s, rB, g], writes=[o2], tag="nrm_app2")
    for i in range(2):
        ctx.ab.free(f"nrm_tb{i}")
        (ctx.af if exact else ctx.ab).free(f"nrm_sq{i}")
        ctx.af.free(f"nrm_d1{i}")


def proj_fm(ctx, in_tiles, wstream, MT, evac, panel_order=None, tag="proj"):
    KT = len(in_tiles)
    order = panel_order if panel_order is not None else range(MT)
    for mt in order:
        w = wstream.panel(mt)
        for n in range(NT):
            ps = ctx.bank()
            pv = ps.tv[:, 0:512]
            pairs = [(w[:, kt * 128:(kt + 1) * 128],
                      in_tiles[kt][:, n * 512:(n + 1) * 512]) for kt in range(KT)]
            matmul_acc(ctx, pv, pairs, tag=tag)
            evac(mt, n, pv)


def emit_l0_mix(ctx, st):
    p = ctx.p
    xres = st["xres"]
    xinner = [ctx.ab.alloc(f"xi{i}", T) for i in range(DT)]
    st["xinner"] = xinner
    emit_norm(ctx, xres, xinner, "ln", eps=1e-5)

    xnb = [ctx.ab.alloc(f"xnb{i}", T) for i in range(DT)]
    emit_norm(ctx, xinner, xnb, "rms", gamma="w_rms_mix", eps=1e-6)

    sig = ctx.ab.alloc("sig_t", T)
    tnh = ctx.ab.alloc("tanh_t", T)
    xs = [ctx.ab.alloc(f"xs{i}", T) for i in range(2)]
    a_f = [ctx.af.alloc("a0", T)]
    h = [ctx.ab.alloc(f"h{i}", T) for i in range(2)]
    ws = WStream(ctx, "ws", None, DT, 3 * D, nslots=2)

    def mm_evac(wpanel, src_tiles, n, act, bname, mt, out_tile):
        ps = ctx.bank()
        pv = ps.tv[:, 0:512]
        pairs = [(wpanel[:, kt * 128:(kt + 1) * 128],
                  src_tiles[kt][:, n * 512:(n + 1) * 512]) for kt in range(DT)]
        matmul_acc(ctx, pv, pairs)
        o = out_tile[:, n * 512:(n + 1) * 512]
        bcol = ctx.vcol(bname, mt)
        p.op("scalar", lambda e, oo=o, i=pv, b=bcol:
             e.activation(oo.ap, i.ap, act, bias=b.ap),
             reads=[pv, bcol], writes=[o], tag="evac")

    for mt in range(DT):
        wall = ws.load(ctx.drams["l0_gvd"][mt])
        wgp, wvp, wdp = wall[:, 0:D], wall[:, D:2 * D], wall[:, 2 * D:3 * D]
        for n in range(NT):
            mm_evac(wgp, xnb, n, AF.Sigmoid, "bg", mt, sig)
        for n in range(NT):
            mm_evac(wvp, xnb, n, AF.Tanh, "bv", mt, tnh)
        xst = xs[mt % 2]
        p.op("vector", lambda e, oo=xst, x1=sig, x2=tnh:
             e.tensor_tensor(oo.ap, x1.ap, x2.ap, ALU.mult),
             reads=[sig, tnh], writes=[xst], tag="xs_mul")
        for n in range(NT):
            mm_evac(wdp, xnb, n, AF.Sigmoid, "bd", mt, sig)
        at = a_f[0]
        p.op("vector", lambda e, oo=at, i=sig:
             e.tensor_scalar(oo.ap, i.ap, 0.998, 0.001, ALU.mult, ALU.add),
             reads=[sig], writes=[at], tag="a_ts")
        ht = h[mt % 2]
        p.op("vector", lambda e, o=ht, a=at, x=xst:
             e.tensor_tensor_scan(o.ap, a.ap, x.ap, 0.0, ALU.mult, ALU.add),
             reads=[at, xst], writes=[ht], tag="scan")
        xi = xinner[mt]
        p.op("vector", lambda e, o=xi, a=xi, b=ht:
             e.tensor_tensor(o.ap, a.ap, b.ap, ALU.add),
             reads=[xi, ht], writes=[xi], tag="res_add")
    ws.release()
    ctx.ab.free("sig_t")
    ctx.ab.free("tanh_t")
    ctx.af.free("a0")
    for i in range(2):
        ctx.ab.free(f"xs{i}")
        ctx.ab.free(f"h{i}")
    for i in range(DT):
        ctx.ab.free(f"xnb{i}")


def emit_l0_attn(ctx, st):
    p = ctx.p
    xinner = st["xinner"]
    xnb = [ctx.ab.alloc(f"xnb{i}", T) for i in range(DT)]
    emit_norm(ctx, xinner, xnb, "rms", gamma="w_rms_attn", eps=1e-6)

    qb2 = [ctx.ab.alloc(f"at_q{i}", 512) for i in range(2)]
    kb2 = [ctx.ab.alloc(f"at_k{i}", 640) for i in range(2)]
    vb2 = [ctx.ab.alloc(f"at_v{i}", 640) for i in range(2)]
    prob = [ctx.ab.alloc(f"at_pr{i}", 256) for i in range(4)]
    attT = [ctx.ab.alloc(f"at_aT{i}", 256) for i in range(4)]
    ao_tm = ctx.ab.alloc("at_ao", 4096)
    ao_fm = ctx.ab.alloc("at_af", 4096)
    acol = ctx.af.alloc("at_acol", 64)
    wqkv = WStream(ctx, "wqkv", None, 8, 3 * D, nslots=2)
    wos = WStream(ctx, "wos", None, 8, D, nslots=2)
    masks = ctx.masks

    for tg in range(NT):
        t0 = tg * 512
        for hp in range(8):
            qb, kb, vb = qb2[hp % 2], kb2[hp % 2], vb2[hp % 2]
            w3 = wqkv.load(ctx.drams["l0_qkv3"][hp])
            wq = w3[:, 0:D]
            ps = ctx.bank()
            pv = ps.tv[:, 0:512]
            pairs = [(wq[:, kt * 128:(kt + 1) * 128],
                      xnb[kt][:, t0:t0 + 512]) for kt in range(DT)]
            matmul_acc(ctx, pv, pairs, tag="mm_q")
            p.op("vector", lambda e, o=qb, i=pv: e.tensor_copy(o.ap, i.ap),
                 reads=[pv], writes=[qb], tag="ev_q")
            wk = w3[:, D:2 * D]
            if tg == 0:
                p.op("vector", lambda e, o=kb[:, 0:128]: e.memset(o.ap, 0.0),
                     writes=[kb[:, 0:128]], tag="kpad0")
                kranges = [(128, t0, 512)]
            else:
                kranges = [(0, t0 - 128, 512), (512, t0 + 384, 128)]
            for (df, src_t, ln) in kranges:
                ps = ctx.bank()
                pv = ps.tv[:, 0:ln]
                pairs = [(wk[:, kt * 128:(kt + 1) * 128],
                          xnb[kt][:, src_t:src_t + ln]) for kt in range(DT)]
                matmul_acc(ctx, pv, pairs, tag="mm_k")
                o = kb[:, df:df + ln]
                p.op("vector", lambda e, o=o, i=pv: e.tensor_copy(o.ap, i.ap),
                     reads=[pv], writes=[o], tag="ev_k")
            wv = w3[:, 2 * D:3 * D]
            if tg == 0:
                p.op("vector", lambda e, o=vb[:, 0:128]: e.memset(o.ap, 0.0),
                     writes=[vb[:, 0:128]], tag="vpad0")
                vtis = range(1, 5)
            else:
                vtis = range(0, 5)
            for vi in vtis:
                tt = t0 - 128 + vi * 128
                ps = ctx.bank()
                pv = ps.tv[:, 0:128]
                pairs = [(xnb[kt][:, tt:tt + 128],
                          wv[:, kt * 128:(kt + 1) * 128]) for kt in range(DT)]
                matmul_acc(ctx, pv, pairs, tag="mm_v")
                o = vb[:, vi * 128:(vi + 1) * 128]
                p.op("vector", lambda e, o=o, i=pv: e.tensor_copy(o.ap, i.ap),
                     reads=[pv], writes=[o], tag="ev_v")
            chains = [(cl, hh) for cl in range(4) for hh in range(2)]
            DEPTH = 5
            scs = {}

            def stage_A(ci):
                cl, hh = chains[ci]
                h0 = hh * 64
                q_tv = qb[h0:h0 + 64, cl * 128:(cl + 1) * 128]
                k_tv = kb[h0:h0 + 64, cl * 128:cl * 128 + 256]
                ps = ctx.banks_f32[ci % DEPTH]
                sc = ps.tv[:, 0:256]
                scs[ci] = sc
                p.op("tensor", lambda e, o=sc, a=q_tv, b=k_tv:
                     e.matmul(o.ap, a.ap, b.ap, start=True, stop=True),
                     reads=[q_tv, k_tv], writes=[sc], tag="mm_sc")

            def stage_BC(ci):
                cl, hh = chains[ci]
                h0 = hh * 64
                gi = ci % 4
                sc = scs[ci]
                nmax = acol[:, gi * 4:gi * 4 + 1]
                p.op("vector", lambda e, o=nmax, i=sc:
                     e.reduce_max(o.ap, i.ap, AX.X, negate=True),
                     reads=[sc], writes=[nmax], tag="sc_max")
                bias = acol[:, gi * 4 + 1:gi * 4 + 2]
                p.op("vector", lambda e, o=bias, i=nmax:
                     e.tensor_scalar(o.ap, i.ap, 0.125, None, ALU.mult),
                     reads=[nmax], writes=[bias], tag="sc_bias")
                pr = prob[gi]
                p.op("scalar", lambda e, o=pr, i=sc, b=bias:
                     e.activation(o.ap, i.ap, AF.Exp, bias=b.ap, scale=0.125),
                     reads=[sc, bias], writes=[pr], tag="sc_exp")
                mk = masks[:, 256:512] if (tg == 0 and cl == 0) \
                    else masks[:, 0:256]
                p.op("gpsimd", lambda e, o=pr, m=mk:
                     e.tensor_tensor(o.ap, o.ap, m.ap, ALU.mult),
                     reads=[pr, mk], writes=[pr], tag="sc_mask")
                s_ = acol[:, gi * 4 + 2:gi * 4 + 3]
                p.op("vector", lambda e, o=s_, i=pr:
                     e.reduce_sum(o.ap, i.ap, AX.X),
                     reads=[pr], writes=[s_], tag="sc_sum")
                r_ = acol[:, gi * 4 + 3:gi * 4 + 4]
                p.op("vector", lambda e, o=r_, i=s_: e.reciprocal(o.ap, i.ap),
                     reads=[s_], writes=[r_], tag="sc_rec")
                aT = attT[gi]
                skip_prev = (tg == 0 and cl == 0)
                pb = ctx.bankbf()
                halves = [1] if skip_prev else [0, 1]
                for hf in halves:
                    dst_ps = pb.tv[:, hf * 128:(hf + 1) * 128]
                    srcp = pr[:, hf * 128:(hf + 1) * 128]
                    transpose128(ctx, dst_ps, srcp)
                    dtv = aT[:, hf * 128:(hf + 1) * 128]
                    p.op("vector", lambda e, o=dtv, i=dst_ps:
                         e.tensor_copy(o.ap, i.ap),
                         reads=[dst_ps], writes=[dtv], tag="ev_aT")
                po = ctx.banks_f32[5]
                ov = po.tv[:, 0:64]
                mms = []
                for hf in halves:
                    lt = aT[:, hf * 128:(hf + 1) * 128]
                    rt = vb[:, cl * 128 + hf * 128:cl * 128 + hf * 128 + 128][:, h0:h0 + 64]
                    mms.append((lt, rt))
                matmul_acc(ctx, ov, mms, tag="mm_pv")
                o = ao_tm[:, cl * 1024 + hp * 128 + h0:cl * 1024 + hp * 128 + h0 + 64]
                p.op("vector", lambda e, o=o, i=ov, r=r_:
                     e.tensor_scalar(o.ap, i.ap, r.ap, None, ALU.mult),
                     reads=[ov, r_], writes=[o], tag="ev_o")

            for ci in range(DEPTH):
                stage_A(ci)
            for ci in range(8):
                stage_BC(ci)
                if ci + DEPTH < 8:
                    stage_A(ci + DEPTH)

        for cl in range(4):
            for dt_ in range(DT):
                pb = ctx.bankbf()
                dst_ps = pb.tv[:, 0:128]
                src = ao_tm[:, cl * 1024 + dt_ * 128:cl * 1024 + (dt_ + 1) * 128]
                transpose128(ctx, dst_ps, src)
                dtv = ao_fm[:, dt_ * 512 + cl * 128:dt_ * 512 + (cl + 1) * 128]
                p.op("vector", lambda e, o=dtv, i=dst_ps: e.tensor_copy(o.ap, i.ap),
                     reads=[dst_ps], writes=[dtv], tag="ev_aofm")
        for mt in range(DT):
            wop = wos.load(ctx.drams["l0_wo"][mt])
            ps = ctx.bank()
            pv = ps.tv[:, 0:512]
            pairs = [(wop[:, kt * 128:(kt + 1) * 128],
                      ao_fm[:, kt * 512:kt * 512 + 512]) for kt in range(DT)]
            matmul_acc(ctx, pv, pairs, tag="mm_wo")
            xi = xinner[mt][:, t0:t0 + 512]
            p.op("vector", lambda e, o=xi, i=pv:
                 e.tensor_tensor(o.ap, o.ap, i.ap, ALU.add),
                 reads=[pv, xi], writes=[xi], tag="res_attn")
    wqkv.release()
    wos.release()
    for nm in ["at_q0", "at_q1", "at_k0", "at_k1", "at_v0", "at_v1",
               "at_pr0", "at_pr1", "at_pr2", "at_pr3",
               "at_aT0", "at_aT1", "at_aT2", "at_aT3", "at_ao", "at_af"]:
        ctx.ab.free(nm)
    ctx.af.free("at_acol")
    for i in range(DT):
        ctx.ab.free(f"xnb{i}")


def emit_l0_glu(ctx, st):
    p = ctx.p
    xres, xinner = st["xres"], st["xinner"]
    xnb = [ctx.ab.alloc(f"xnb{i}", T) for i in range(DT)]
    emit_norm(ctx, xinner, xnb, "rms", gamma="w_rms_ffn", eps=1e-6)

    gbuf = ctx.ab.alloc("glu_g", 512)
    act = [ctx.ab.alloc(f"glu_a{i}", T) for i in range(8)]
    ws = WStream(ctx, "glu_w", None, 32, 2 * D, nslots=2)
    wd_s = WStream(ctx, "glu_wd", None, 8, D, nslots=2)

    for g in range(4):
        for sub in range(8):
            mt = g * 8 + sub
            wgu = ws.load(ctx.drams["l0_gu"][mt])
            wgp, wup = wgu[:, 0:D], wgu[:, D:2 * D]
            for n in range(NT):
                ps = ctx.bank()
                pv = ps.tv[:, 0:512]
                pairs = [(wgp[:, kt * 128:(kt + 1) * 128],
                          xnb[kt][:, n * 512:(n + 1) * 512]) for kt in range(DT)]
                matmul_acc(ctx, pv, pairs, tag="mm_gate")
                o = gbuf[:, 0:512]
                p.op("scalar", lambda e, o=o, i=pv:
                     e.activation(o.ap, i.ap, AF.Silu),
                     reads=[pv], writes=[o], tag="ev_silu")
                ps2 = ctx.bank()
                pv2 = ps2.tv[:, 0:512]
                pairs = [(wup[:, kt * 128:(kt + 1) * 128],
                          xnb[kt][:, n * 512:(n + 1) * 512]) for kt in range(DT)]
                matmul_acc(ctx, pv2, pairs, tag="mm_up")
                oa = act[sub][:, n * 512:(n + 1) * 512]
                p.op("vector", lambda e, o=oa, i=pv2, gg=o:
                     e.tensor_tensor(o.ap, i.ap, gg.ap, ALU.mult),
                     reads=[pv2, o], writes=[oa], tag="ev_act")
        for mt in range(DT):
            wop = wd_s.load(ctx.drams[f"l0_wout{g}"][mt])
            for n in range(NT):
                ps = ctx.bank()
                pv = ps.tv[:, 0:512]
                pairs = [(wop[:, kt * 128:(kt + 1) * 128],
                          act[kt][:, n * 512:(n + 1) * 512]) for kt in range(DT)]
                matmul_acc(ctx, pv, pairs, tag="mm_down")
                xi = xinner[mt][:, n * 512:(n + 1) * 512]
                p.op("vector", lambda e, o=xi, i=pv:
                     e.tensor_tensor(o.ap, o.ap, i.ap, ALU.add),
                     reads=[pv, xi], writes=[xi], tag="res_glu")
    ws.release()
    wd_s.release()
    ctx.ab.free("glu_g")
    for i in range(8):
        ctx.ab.free(f"glu_a{i}")
    for i in range(DT):
        ctx.ab.free(f"xnb{i}")
    for i in range(DT):
        p.op("vector", lambda e, o=xres[i], b=xinner[i]:
             e.tensor_tensor(o.ap, o.ap, b.ap, ALU.add),
             reads=[xres[i], xinner[i]], writes=[xres[i]], tag="res_mod")
    for i in range(DT):
        ctx.ab.free(f"xi{i}")
    del st["xinner"]


def emit_l0_ffn(ctx, st):
    p = ctx.p
    xres = st["xres"]
    xnb = [ctx.ab.alloc(f"xnb{i}", T) for i in range(DT)]
    emit_norm(ctx, xres, xnb, "ln", eps=1e-5)

    fslot = [ctx.af.alloc("fw0", D)]
    accc = ctx.af.alloc("q_acc", 8)
    onef = ctx.of32
    p.op("vector", lambda e, o=accc[:, 0:1]: e.memset(o.ap, 0.0),
         writes=[accc[:, 0:1]], tag="q_z")
    for mt in range(DT):
        sl = fslot[0]
        src = ctx.drams["l0f_w"][mt]
        p.op("sync", lambda e, d=sl, s=src: e.dma_start(out=d.ap, in_=s),
             writes=[sl], dma="l0fw0", tag="qw_load")
        c = accc[:, 3:4]
        p.op("vector", lambda e, o=c, i=sl:
             e.tensor_reduce(o.ap, i.ap, AX.X, ALU.add, apply_absolute_value=True),
             reads=[sl], writes=[c], tag="q_abs")
        p.op("vector", lambda e, o=accc[:, 0:1], i=c:
             e.tensor_tensor(o.ap, o.ap, i.ap, ALU.add),
             reads=[accc[:, 0:1], c], writes=[accc[:, 0:1]], tag="q_accadd")
    ps = ctx.bank()
    tot = ps.tv[0:1, 0:1]
    p.op("tensor", lambda e, o=tot, a=onef[:, 0:1], b=accc[:, 0:1]:
         e.matmul(o.ap, a.ap, b.ap, start=True, stop=True),
         reads=[onef[:, 0:1], accc[:, 0:1]], writes=[tot], tag="q_tot")
    sc1 = ctx.rowf.tv[0:1, 0:1]
    p.op("vector", lambda e, o=sc1, i=tot:
         e.tensor_scalar(o.ap, i.ap, 1.0 / (D * D), 1e-5, ALU.mult, ALU.max),
         reads=[tot], writes=[sc1], tag="q_scale")
    ps2 = ctx.bank()
    p.op("tensor", lambda e, o=ps2.tv[:, 0:1], a=onef[0:1, 0:128], b=sc1:
         e.matmul(o.ap, a.ap, b.ap, start=True, stop=True),
         reads=[onef[0:1, 0:128], sc1], writes=[ps2.tv[:, 0:1]], tag="q_bs")
    p.op("vector", lambda e, o=accc[:, 2:3], i=ps2.tv[:, 0:1]:
         e.tensor_copy(o.ap, i.ap),
         reads=[ps2.tv[:, 0:1]], writes=[accc[:, 2:3]], tag="q_scol")
    p.op("vector", lambda e, o=accc[:, 1:2], i=accc[:, 2:3]:
         e.tensor_scalar(o.ap, i.ap, 0.5, None, ALU.mult),
         reads=[accc[:, 2:3]], writes=[accc[:, 1:2]], tag="q_thr")
    wq = ctx.ab.alloc("q_wq", DT * D)
    t_abs = ctx.af.alloc("q_tabs", D)
    t_sgn = ctx.ab.alloc("q_tsgn", D)
    for mt in range(DT):
        sl = fslot[0]
        src = ctx.drams["l0f_w"][mt]
        p.op("sync", lambda e, d=sl, s=src: e.dma_start(out=d.ap, in_=s),
             writes=[sl], dma="l0fw0", tag="qw_load2")
        p.op("scalar", lambda e, o=t_abs, i=sl: e.activation(o.ap, i.ap, AF.Abs),
             reads=[sl], writes=[t_abs], tag="q_abs2")
        p.op("scalar", lambda e, o=t_sgn, i=sl: e.activation(o.ap, i.ap, AF.Sign),
             reads=[sl], writes=[t_sgn], tag="q_sgn")
        p.op("vector", lambda e, o=t_abs, i=t_abs, th=accc[:, 1:2]:
             e.tensor_scalar(o.ap, i.ap, th.ap, None, ALU.is_ge),
             reads=[t_abs, accc[:, 1:2]], writes=[t_abs], tag="q_ge")
        o = wq[:, mt * D:(mt + 1) * D]
        p.op("vector", lambda e, o=o, a=t_abs, b=t_sgn:
             e.tensor_tensor(o.ap, a.ap, b.ap, ALU.mult),
             reads=[t_abs, t_sgn], writes=[o], tag="q_mul")
    for mt in range(DT):
        for n in range(NT):
            ps = ctx.bank()
            pv = ps.tv[:, 0:512]
            pairs = [(wq[:, mt * D + kt * 128:mt * D + (kt + 1) * 128],
                      xnb[kt][:, n * 512:(n + 1) * 512]) for kt in range(DT)]
            matmul_acc(ctx, pv, pairs, tag="mm_tern")
            xr = xres[mt][:, n * 512:(n + 1) * 512]
            p.op("vector", lambda e, o=xr, i=pv, s=accc[:, 2:3]:
                 e.scalar_tensor_tensor(o.ap, i.ap, s.ap, o.ap, ALU.mult, ALU.add),
                 reads=[pv, accc[:, 2:3], xr], writes=[xr], tag="res_tern")
    ctx.ab.free("q_wq")
    ctx.af.free("q_tabs")
    ctx.ab.free("q_tsgn")
    ctx.af.free("q_acc")
    ctx.af.free("fw0")
    for i in range(DT):
        ctx.ab.free(f"xnb{i}")


MAMBA_DEBUG = False


def dbg_dump(ctx, name, tv, parts, free):
    if not MAMBA_DEBUG:
        return
    t = ctx.p.nc.dram_tensor("dbg_" + name, [parts, free], F32,
                             kind="ExternalOutput").ap()
    ctx.p.op("gpsimd", lambda e, d=t, s=tv[0:parts, 0:free]:
             e.dma_start(out=d, in_=s.ap),
             reads=[tv[0:parts, 0:free]], dma="dbg", tag="dbg")


def emit_l1_mamba(ctx, st):
    p = ctx.p
    xres = st["xres"]
    xnb = [ctx.ab.alloc(f"xnb{i}", T) for i in range(DT)]
    emit_norm(ctx, xres, xnb, "ln", eps=1e-5)
    for i in range(DT):
        dst = ctx.drams["xspill"][i * 128:(i + 1) * 128, :]
        p.op("sync", lambda e, d=dst, s=xres[i]: e.dma_start(out=d, in_=s.ap),
             reads=[xres[i]], dma="xsp_st", tag="x_spill")
    for i in range(DT):
        ctx.af.free(f"xr{i}")
    del st["xres"]

    h1 = [ctx.ab.alloc(f"h1_{i}", T) for i in range(DT)]
    ws = WStream(ctx, "mam_w", None, 8, D, nslots=2)
    for mt in range(DT):
        wp = ws.load(ctx.drams["l1_fc1"][mt])
        for n in range(NT):
            ps = ctx.bank()
            pv = ps.tv[:, 0:512]
            pairs = [(wp[:, kt * 128:(kt + 1) * 128],
                      xnb[kt][:, n * 512:(n + 1) * 512]) for kt in range(DT)]
            matmul_acc(ctx, pv, pairs, tag="mm_fc1")
            o = h1[mt][:, n * 512:(n + 1) * 512]
            b = ctx.vcol("fc1_b", mt)
            p.op("vector", lambda e, o=o, i=pv, b=b:
                 e.tensor_scalar(o.ap, i.ap, b.ap, None, ALU.add),
                 reads=[pv, b], writes=[o], tag="ev_h1")
    for i in range(DT):
        ctx.ab.free(f"xnb{i}")

    xpd = ctx.ab.alloc("xpd", 2048)
    xpb = ctx.ab.alloc("xpb", 16 * 16)
    xpc = ctx.ab.alloc("xpc", 16 * 16)
    dtps = ctx.ab.alloc("dtps", 256)
    for kt in range(16):
        for nm, dst, w in (("xpd", xpd[:, kt * 64:(kt + 1) * 64], 64),
                           ("xpb", xpb[0:128, kt * 16:(kt + 1) * 16], 16),
                           ("xpc", xpc[0:128, kt * 16:(kt + 1) * 16], 16)):
            src = ctx.drams["l1_" + nm][kt]
            p.op("gpsimd", lambda e, d=dst, s=src: e.dma_start(out=d.ap, in_=s),
                 writes=[dst], dma=f"ld_{nm}", tag="xp_load")

    Abuf = ctx.af.alloc("Abuf", 256)
    for dtile in range(16):
        src = ctx.vcols("Alog", dtile * 16, (dtile + 1) * 16)
        dst = Abuf[:, dtile * 16:(dtile + 1) * 16]
        p.op("scalar", lambda e, o=dst, i=src: e.activation(o.ap, i.ap, AF.Exp),
             reads=[src], writes=[dst], tag="A_exp")
        p.op("vector", lambda e, o=dst, i=dst:
             e.tensor_scalar(o.ap, i.ap, -1.0, None, ALU.mult),
             reads=[dst], writes=[dst], tag="A_neg")

    accd = ctx.af.alloc("accd", T)
    accb = ctx.af.alloc("accb", T)
    accc = ctx.af.alloc("accc", T)
    for nm, tv, pp in (("accd", accd, 64), ("accb", accb, 16), ("accc", accc, 16)):
        v = tv[0:pp, :]
        p.op("vector", lambda e, o=v: e.memset(o.ap, 0.0), writes=[v], tag="xdbl_z")
    xcpad = ctx.af.alloc("xcpad", T + 16)
    ctmp = ctx.af.alloc("ctmp", T)
    ug = [ctx.ab.alloc(f"ug{i}", T) for i in range(4)]

    def mm_stream(pv, wp, h1_src, n, tag):
        for kt in range(DT):
            lt = wp[:, kt * 128:(kt + 1) * 128]
            rt = h1_src(kt, n)
            p.op("tensor",
                 lambda e, o=pv, a=lt, b=rt, s=(kt == 0), t=(kt == DT - 1):
                     e.matmul(o.ap, a.ap, b.ap, start=s, stop=t),
                 reads=[lt, rt], writes=[pv], tag=tag)

    def make_u(dtile, udst, h1_src):
        wp = ws.load(ctx.drams["l1_inproj"][dtile])
        z3 = xcpad[:, 0:3]
        p.op("vector", lambda e, o=z3: e.memset(o.ap, 0.0), writes=[z3], tag="xc_z")
        for n in range(NT):
            ps = ctx.bank()
            pv = ps.tv[:, 0:512]
            mm_stream(pv, wp, h1_src, n, "mm_xc")
            o = xcpad[:, 3 + n * 512:3 + (n + 1) * 512]
            p.op("vector", lambda e, o=o, i=pv: e.tensor_copy(o.ap, i.ap),
                 reads=[pv], writes=[o], tag="ev_xc")
        c0 = ctx.vcol("cw0", dtile)
        p.op("vector", lambda e, o=ctmp, i=xcpad[:, 0:T], s=c0:
             e.tensor_scalar(o.ap, i.ap, s.ap, None, ALU.mult),
             reads=[xcpad[:, 0:T], c0], writes=[ctmp], tag="conv0")
        for w in range(1, 4):
            cw = ctx.vcol(f"cw{w}", dtile)
            sl = xcpad[:, w:w + T]
            p.op("vector", lambda e, o=ctmp, i=sl, s=cw, a=ctmp:
                 e.scalar_tensor_tensor(o.ap, i.ap, s.ap, a.ap, ALU.mult, ALU.add),
                 reads=[sl, cw, ctmp], writes=[ctmp], tag=f"conv{w}")
        cb = ctx.vcol("conv_b", dtile)
        p.op("scalar", lambda e, o=udst, i=ctmp, b=cb:
             e.activation(o.ap, i.ap, AF.Silu, bias=b.ap),
             reads=[ctmp, cb], writes=[udst], tag="u_silu")

    h1_res = lambda kt, n: h1[kt][:, n * 512:(n + 1) * 512]
    for g4 in range(4):
        for j in range(4):
            make_u(g4 * 4 + j, ug[j], h1_res)
        for (acc, pp, wsl, msz) in ((accd, 64, xpd, 64), (accb, 16, xpb, 16),
                                    (accc, 16, xpc, 16)):
            for n in range(NT):
                ps = ctx.bank()
                pv = ps.tv[0:pp, 0:512]
                pairs = [(wsl[0:128, (g4 * 4 + j) * msz:(g4 * 4 + j + 1) * msz],
                          ug[j][:, n * 512:(n + 1) * 512]) for j in range(4)]
                matmul_acc(ctx, pv, pairs, tag="mm_xdbl")
                o = acc[0:pp, n * 512:(n + 1) * 512]
                p.op("vector", lambda e, o=o, i=pv:
                     e.tensor_tensor(o.ap, o.ap, i.ap, ALU.add),
                     reads=[pv, o], writes=[o], tag="xdbl_add")
    for i in range(4):
        ctx.ab.free(f"ug{i}")
    ctx.ab.free("xpd")
    ctx.ab.free("xpb")
    ctx.ab.free("xpc")

    dbg_dump(ctx, "accd", accd, 64, T)
    dbg_dump(ctx, "accb", accb, 16, T)
    dbg_dump(ctx, "accc", accc, 16, T)
    dtlow = ctx.ab.alloc("dtlow", T)
    Bbf = ctx.ab.alloc("Bbf", T)[0:16, :]
    Cbf_full = ctx.ab.alloc("Cbf", T)[0:16, :]
    for (src, d, pp) in ((accd[0:64, :], dtlow[0:64, :], 64),
                         (accb[0:16, :], Bbf, 16),
                         (accc[0:16, :], Cbf_full, 16)):
        s = src
        p.op("vector", lambda e, o=d, i=s: e.tensor_copy(o.ap, i.ap),
             reads=[s], writes=[d], tag="xdbl_cast")
    ctx.af.free("accd")
    ctx.af.free("accb")
    ctx.af.free("accc")

    u2 = [ctx.ab.alloc(f"u{i}", T) for i in range(2)]
    zs2 = [ctx.ab.alloc(f"zs{i}", T) for i in range(2)]
    hb_a = ctx.ab.alloc("hb0", T)
    X2 = [ctx.ab.alloc("X0", T)] * 2
    hb2 = [hb_a, ctx.ab.alloc("hb1", T)]
    Xc = ctx.ab.alloc("Xc", T)
    Bbs2 = [ctx.ab.alloc(f"Bbs{i}", T) for i in range(2)]
    Cbs2 = [ctx.ab.alloc(f"Cbs{i}", T) for i in range(2)]
    dtu2 = [ctx.ab.alloc(f"dtu{i}", T) for i in range(2)]
    dtt2 = [ctx.af.alloc(f"dtt{i}", T) for i in range(2)]
    ysa2 = [ctx.af.alloc(f"ysa{i}", T) for i in range(2)]
    dA2 = [ctx.af.alloc(f"dA{i}", T) for i in range(2)]
    sp = [ctx.af.alloc(f"sp{i}", 512) for i in range(3)]

    def pre_tile(dtile, sub):
        u_t, zs, dtt, dtu, ysa = (u2[sub], zs2[sub], dtt2[sub], dtu2[sub],
                                  ysa2[sub])
        make_u(dtile, u_t, h1_res)
        wp = ws.load(ctx.drams["l1_inproj"][16 + dtile])
        for n in range(NT):
            ps = ctx.bank()
            pv = ps.tv[:, 0:512]
            mm_stream(pv, wp, h1_res, n, "mm_z")
            o = zs[:, n * 512:(n + 1) * 512]
            p.op("scalar", lambda e, o=o, i=pv: e.activation(o.ap, i.ap, AF.Silu),
                 reads=[pv], writes=[o], tag="ev_zs")
        bdt = ctx.vcol("dt_b", dtile)
        dtw = dtps[0:64, (dtile % 2) * 128:(dtile % 2) * 128 + 128]
        src = ctx.drams["l1_dtp"][dtile]
        p.op("gpsimd", lambda e, d=dtw, s=src: e.dma_start(out=d.ap, in_=s),
             writes=[dtw], dma=f"dtp{dtile % 2}", tag="dtp_load")
        for n in range(NT):
            ps = ctx.bank()
            pv = ps.tv[:, 0:512]
            lt = dtw
            rt = dtlow[0:64, n * 512:(n + 1) * 512]
            matmul_acc(ctx, pv, [(lt, rt)], tag="mm_dt")
            s0, s1, s2 = sp
            p.op("scalar", lambda e, o=s0, i=pv, b=bdt:
                 e.activation(o.ap, i.ap, AF.Abs, bias=b.ap),
                 reads=[pv, bdt], writes=[s0], tag="sp_abs")
            p.op("scalar", lambda e, o=s1, i=s0:
                 e.activation(o.ap, i.ap, AF.Exp, scale=-1.0),
                 reads=[s0], writes=[s1], tag="sp_exp")
            p.op("scalar", lambda e, o=s1, i=s1, b=ctx.vcol("one"):
                 e.activation(o.ap, i.ap, AF.Ln, bias=b.ap),
                 reads=[s1, ctx.vcol("one")], writes=[s1], tag="sp_ln")
            p.op("scalar", lambda e, o=s2, i=pv, b=bdt:
                 e.activation(o.ap, i.ap, AF.Relu, bias=b.ap),
                 reads=[pv, bdt], writes=[s2], tag="sp_relu")
            o = dtt[:, n * 512:(n + 1) * 512]
            p.op("vector", lambda e, o=o, a=s1, b=s2:
                 e.tensor_tensor(o.ap, a.ap, b.ap, ALU.add),
                 reads=[s1, s2], writes=[o], tag="sp_add")
        p.op("vector", lambda e, o=dtu, a=dtt, b=u_t:
             e.tensor_tensor(o.ap, a.ap, b.ap, ALU.mult),
             reads=[dtt, u_t], writes=[dtu], tag="dtu")
        p.op("vector", lambda e, o=ysa: e.memset(o.ap, 0.0), writes=[ysa],
             tag="ys_z")

    def post_tile(dtile, sub):
        u_t, zs, ysa = u2[sub], zs2[sub], ysa2[sub]
        Dc = ctx.vcol("Dvec", dtile)
        p.op("vector", lambda e, o=ysa, i=u_t, s=Dc, a=ysa:
             e.scalar_tensor_tensor(o.ap, i.ap, s.ap, a.ap, ALU.mult, ALU.add),
             reads=[u_t, Dc, ysa], writes=[ysa], tag="y_uD")
        yt = X2[0] if sub == 0 else hb2[1]
        p.op("vector", lambda e, o=yt, a=ysa, b=zs:
             e.tensor_tensor(o.ap, a.ap, b.ap, ALU.mult),
             reads=[ysa, zs], writes=[yt], tag="y_z")
        dst = ctx.drams["ysp"][dtile * 128:(dtile + 1) * 128, :]
        p.op("sync", lambda e, d=dst, s=yt: e.dma_start(out=d, in_=s.ap),
             reads=[yt], dma=f"ysp{sub}", tag="y_spill")

    for dp in range(8):
        pre_tile(2 * dp, 0)
        pre_tile(2 * dp + 1, 1)
        for sidx in range(16):
            Bbs, Cbs = Bbs2[sidx % 2], Cbs2[sidx % 2]
            sel = ctx.sel16[0:16, sidx * 128:(sidx + 1) * 128]
            for n in range(NT):
                ps = ctx.bank()
                pv = ps.tv[:, 0:512]
                rt = Bbf[:, n * 512:(n + 1) * 512]
                matmul_acc(ctx, pv, [(sel, rt)], tag="mm_bb")
                o = Bbs[:, n * 512:(n + 1) * 512]
                p.op("scalar", lambda e, o=o, i=pv:
                     e.activation(o.ap, i.ap, AF.Identity),
                     reads=[pv], writes=[o], tag="ev_bb")
                ps2 = ctx.bank()
                pv2 = ps2.tv[:, 0:512]
                rt2 = Cbf_full[:, n * 512:(n + 1) * 512]
                matmul_acc(ctx, pv2, [(sel, rt2)], tag="mm_cb")
                o2 = Cbs[:, n * 512:(n + 1) * 512]
                p.op("scalar", lambda e, o=o2, i=pv2:
                     e.activation(o.ap, i.ap, AF.Identity),
                     reads=[pv2], writes=[o2], tag="ev_cb")
            for sub in range(2):
                dtile = 2 * dp + sub
                dtt, dtu, ysa = dtt2[sub], dtu2[sub], ysa2[sub]
                dAb, Xb, hb = dA2[sub], X2[sub], hb2[sub]
                acol = Abuf[:, dtile * 16 + sidx:dtile * 16 + sidx + 1]
                p.op("scalar", lambda e, o=dAb, i=dtt, s=acol:
                     e.activation(o.ap, i.ap, AF.Exp, scale=s.ap),
                     reads=[dtt, acol], writes=[dAb], tag="dA")
                p.op("vector", lambda e, o=Xb, a=dtu, b=Bbs:
                     e.tensor_tensor(o.ap, a.ap, b.ap, ALU.mult),
                     reads=[dtu, Bbs], writes=[Xb], tag="X_mul")
                p.op("vector", lambda e, o=hb, a=dAb, x=Xb:
                     e.tensor_tensor_scan(o.ap, a.ap, x.ap, 0.0, ALU.mult,
                                          ALU.add),
                     reads=[dAb, Xb], writes=[hb], tag="scan")
                eng_hc = "vector" if sub == 0 else "gpsimd"
                p.op(eng_hc, lambda e, o=Xc, a=hb, b=Cbs:
                     e.tensor_tensor(o.ap, a.ap, b.ap, ALU.mult),
                     reads=[hb, Cbs], writes=[Xc], tag="hC_mul")
                p.op("gpsimd", lambda e, o=ysa, a=ysa, b=Xc:
                     e.tensor_tensor(o.ap, a.ap, b.ap, ALU.add),
                     reads=[ysa, Xc], writes=[ysa], tag="ys_add")
        post_tile(2 * dp, 0)
        post_tile(2 * dp + 1, 1)

    for nm in ["zs0", "zs1", "u0", "u1", "X0", "hb0", "hb1", "Xc", "dtlow",
               "Bbf", "Cbf", "dtps", "Bbs0", "Bbs1",
               "Cbs0", "Cbs1", "dtu0", "dtu1"]:
        ctx.ab.free(nm)
    for nm in ["dtt0", "dtt1", "ysa0", "ysa1", "dA0", "dA1", "ctmp", "Abuf",
               "xcpad"]:
        ctx.af.free(nm)
    for i in range(3):
        ctx.af.free(f"sp{i}")
    for i in range(DT):
        ctx.ab.free(f"h1_{i}")

    opw = ctx.ab.alloc("opw", DT * 2048)
    for mt in range(DT):
        dst = opw[:, mt * 2048:(mt + 1) * 2048]
        srcw = ctx.drams["l1_outproj"][mt]
        p.op("gpsimd", lambda e, d=dst, s=srcw: e.dma_start(out=d.ap, in_=s),
             writes=[dst], dma=f"opw{mt}", tag="opw_load")
    ych = ctx.ab.alloc("ych", 16 * 512)
    oacc = [ctx.ab.alloc(f"oac{i}", T) for i in range(DT)]
    for n in range(NT):
        for kt in range(16):
            dst = ych[:, kt * 512:(kt + 1) * 512]
            srcy = ctx.drams["ysp"][kt * 128:(kt + 1) * 128,
                                    n * 512:(n + 1) * 512]
            p.op("sync", lambda e, d=dst, s=srcy: e.dma_start(out=d.ap, in_=s),
                 writes=[dst], dma=f"ych{kt}", tag="ych_load")
        for mt in range(DT):
            ps = ctx.bank()
            pv = ps.tv[:, 0:512]
            pairs = [(opw[:, mt * 2048 + kt * 128:mt * 2048 + (kt + 1) * 128],
                      ych[:, kt * 512:(kt + 1) * 512]) for kt in range(16)]
            matmul_acc(ctx, pv, pairs, tag="mm_op")
            o = oacc[mt][:, n * 512:(n + 1) * 512]
            p.op("vector", lambda e, o=o, i=pv: e.tensor_copy(o.ap, i.ap),
                 reads=[pv], writes=[o], tag="ev_op")
    ctx.ab.free("opw")
    ctx.ab.free("ych")

    dbg_dump(ctx, "oac0", oacc[0], 128, T)
    xres = [ctx.af.alloc(f"xr{i}", T) for i in range(DT)]
    st["xres"] = xres
    xsrc = ctx.drams["xspill"]
    for i in range(DT):
        src = xsrc[i * 128:(i + 1) * 128, :]
        p.op("sync", lambda e, d=xres[i], s=src: e.dma_start(out=d.ap, in_=s),
             writes=[xres[i]], dma=f"xrl{i}", tag="x_reload")
    for mt in range(DT):
        wp = ws.load(ctx.drams["l1_fc2"][mt])
        for n in range(NT):
            ps = ctx.bank()
            pv = ps.tv[:, 0:512]
            pairs = [(wp[:, kt * 128:(kt + 1) * 128],
                      oacc[kt][:, n * 512:(n + 1) * 512]) for kt in range(DT)]
            matmul_acc(ctx, pv, pairs, tag="mm_fc2")
            xr = xres[mt][:, n * 512:(n + 1) * 512]
            b2 = ctx.vcol("fc2_b", mt)
            p.op("vector", lambda e, o=xr, i=pv, s=b2:
                 e.scalar_tensor_tensor(o.ap, i.ap, s.ap, o.ap, ALU.add, ALU.add),
                 reads=[pv, b2, xr], writes=[xr], tag="res_fc2")
    for i in range(DT):
        ctx.ab.free(f"oac{i}")
    ws.release()


def emit_l1_moe(ctx, st):
    p = ctx.p
    xres = st["xres"]
    xnb = [ctx.ab.alloc(f"xnb{i}", T) for i in range(DT)]
    smalls = ctx.af.alloc("moe_sm", 64)
    mcols = smalls[:, 16:32]
    rcols = smalls[:, 32:48]
    emit_norm(ctx, xres, xnb, "ln", eps=1e-5, exact=True,
              keep_cols=(mcols, rcols))

    gwf = ctx.af.alloc("moe_gw", 8 * DT)
    for kt in range(DT):
        dst = gwf[:, kt * 8:(kt + 1) * 8]
        src = ctx.drams["moe_gate"][kt]
        p.op("sync", lambda e, d=dst, s=src: e.dma_start(out=d.ap, in_=s),
             writes=[dst], dma="moe_gw", tag="gw_load")
    ps = ctx.bank()
    cs_ps = ps.tv[0:1, 0:8]
    pairs = [(ctx.of32[:, 0:1], gwf[:, kt * 8:(kt + 1) * 8]) for kt in range(DT)]
    matmul_acc(ctx, cs_ps, pairs, tag="mm_cs")
    csrow = smalls[0:1, 56:64]
    p.op("vector", lambda e, o=csrow, i=cs_ps: e.tensor_copy(o.ap, i.ap),
         reads=[cs_ps], writes=[csrow], tag="ev_cs")
    csb_ps = ctx.bank()
    p.op("tensor", lambda e, o=csb_ps.tv[:, 0:8], a=ctx.of32[0:1, 0:128], b=csrow:
         e.matmul(o.ap, a.ap, b.ap, start=True, stop=True),
         reads=[ctx.of32[0:1, 0:128], csrow], writes=[csb_ps.tv[:, 0:8]],
         tag="mm_csb")
    csb = smalls[:, 0:8]
    p.op("vector", lambda e, o=csb, i=csb_ps.tv[:, 0:8]: e.tensor_copy(o.ap, i.ap),
         reads=[csb_ps.tv[:, 0:8]], writes=[csb], tag="ev_csb")
    mask_tm = ctx.af.alloc("moe_msk", 16)
    logit_t = ctx.af.alloc("moe_lg", 16)
    srow_b = ctx.rowb.tv[0:1, 1024:3072]
    mkb = ctx.ab.alloc("moe_mkb", 16)
    mask8 = ctx.ab.alloc("moe_mask8", T)
    for ti in range(16):
        psx = ctx.bank()
        raw = psx.tv[:, 0:8]
        pairs = [(xres[kt][:, ti * 128:(ti + 1) * 128],
                  gwf[:, kt * 8:(kt + 1) * 8]) for kt in range(DT)]
        matmul_acc(ctx, raw, pairs, tag="mm_lg")
        lg = logit_t[:, 0:8]
        mc = mcols[:, ti:ti + 1]
        rc = rcols[:, ti:ti + 1]
        t1 = mask_tm[:, 8:16]
        p.op("vector", lambda e, o=t1, i=csb, s=mc:
             e.tensor_scalar(o.ap, i.ap, s.ap, None, ALU.mult),
             reads=[csb, mc], writes=[t1], tag="lg_mcs")
        p.op("vector", lambda e, o=lg, a=raw, b=t1:
             e.tensor_tensor(o.ap, a.ap, b.ap, ALU.subtract),
             reads=[raw, t1], writes=[lg], tag="lg_sub")
        p.op("vector", lambda e, o=lg, i=lg, s=rc:
             e.tensor_scalar(o.ap, i.ap, s.ap, None, ALU.mult),
             reads=[lg, rc], writes=[lg], tag="lg_rs")
        nmax = smalls[:, 48:49]
        p.op("vector", lambda e, o=nmax, i=lg:
             e.reduce_max(o.ap, i.ap, AX.X, negate=True),
             reads=[lg], writes=[nmax], tag="lg_max")
        top = smalls[:, 49:50]
        p.op("vector", lambda e, o=top, i=nmax:
             e.tensor_scalar(o.ap, i.ap, -1.0, None, ALU.mult),
             reads=[nmax], writes=[top], tag="lg_top")
        mk = mask_tm[:, 0:8]
        p.op("vector", lambda e, o=mk, i=lg, s=top:
             e.tensor_scalar(o.ap, i.ap, s.ap, None, ALU.is_equal),
             reads=[lg, top], writes=[mk], tag="lg_eq")
        sc = smalls[:, 50:51]
        p.op("scalar", lambda e, o=sc, i=nmax:
             e.activation(o.ap, i.ap, AF.Sigmoid, scale=-1.0),
             reads=[nmax], writes=[sc], tag="lg_sig")
        psy = ctx.bank()
        pv = psy.tv[0:1, 0:128]
        p.op("tensor", lambda e, o=pv, a=sc, b=ctx.if32:
             e.matmul(o.ap, a.ap, b.ap, start=True, stop=True),
             reads=[sc, ctx.if32], writes=[pv], tag="mm_scT")
        o = srow_b[0:1, ti * 128:(ti + 1) * 128]
        p.op("vector", lambda e, o=o, i=pv: e.tensor_copy(o.ap, i.ap),
             reads=[pv], writes=[o], tag="ev_scT")
        mcb = mkb[:, 0:8]
        mk8 = mask_tm[:, 0:8]
        p.op("vector", lambda e, o=mcb, i=mk8: e.tensor_copy(o.ap, i.ap),
             reads=[mk8], writes=[mcb], tag="mk_cast")
        psz = ctx.bankbf()
        pv = psz.tv[0:8, 0:128]
        transpose128(ctx, pv, mcb)
        o = mask8[0:8, ti * 128:(ti + 1) * 128]
        p.op("vector", lambda e, o=o, i=pv: e.tensor_copy(o.ap, i.ap),
             reads=[pv], writes=[o], tag="ev_mkT")
    ctx.af.free("moe_msk")
    ctx.af.free("moe_lg")
    ctx.af.free("moe_gw")

    gbuf = ctx.ab.alloc("moe_gb", T)
    maskb = ctx.ab.alloc("moe_mb", T)
    act = [ctx.ab.alloc(f"moe_a{i}", T) for i in range(8)]
    ws = WStream(ctx, "moe_w", None, 8, 2 * D, nslots=2)

    def group4(wtile_fn, src_tiles, KT, evac_fn, tag):
        for n in range(NT):
            pv = ctx.bank().tv[:, 0:512]
            pairs = [(wtile_fn(kt), src_tiles[kt][:, n * 512:(n + 1) * 512])
                     for kt in range(KT)]
            matmul_acc(ctx, pv, pairs, tag=tag)
            evac_fn(n, pv)

    def expert(gu_dram, down_dram, src_tiles, masked):
        for g in range(2):
            for sub in range(8):
                mt = g * 8 + sub
                wgu = ws.load(gu_dram[mt])
                wp, wu = wgu[:, 0:D], wgu[:, D:2 * D]

                def ev_g(n, pv):
                    o = gbuf[:, n * 512:(n + 1) * 512]
                    p.op("scalar", lambda e, o=o, i=pv:
                         e.activation(o.ap, i.ap, AF.Silu),
                         reads=[pv], writes=[o], tag="ev_esilu")
                    if masked:
                        mb = maskb[:, n * 512:(n + 1) * 512]
                        p.op("vector", lambda e, o=o, a=o, b=mb:
                             e.tensor_tensor(o.ap, a.ap, b.ap, ALU.mult),
                             reads=[o, mb], writes=[o], tag="gm_mask")
                group4(lambda kt: wp[:, kt * 128:(kt + 1) * 128], src_tiles,
                       DT, ev_g, "mm_eg")

                def ev_u(n, pv):
                    oa = act[sub][:, n * 512:(n + 1) * 512]
                    o = gbuf[:, n * 512:(n + 1) * 512]
                    p.op("vector", lambda e, o=oa, a=pv, b=o:
                         e.tensor_tensor(o.ap, a.ap, b.ap, ALU.mult),
                         reads=[pv, o], writes=[oa], tag="ev_eact")
                group4(lambda kt: wu[:, kt * 128:(kt + 1) * 128], src_tiles,
                       DT, ev_u, "mm_eu")
            for mt in range(DT):
                wp = ws.load(down_dram[g][mt])

                def ev_d(n, pv, mt=mt):
                    xr = xres[mt][:, n * 512:(n + 1) * 512]
                    p.op("vector", lambda e, o=xr, i=pv:
                         e.tensor_tensor(o.ap, o.ap, i.ap, ALU.add),
                         reads=[pv, xr], writes=[xr], tag="res_moe")
                group4(lambda kt: wp[:, kt * 128:(kt + 1) * 128], act,
                       DT, ev_d, "mm_ed")

    expert(ctx.drams["moe_shgu"],
           [ctx.drams["moe_shd"][g] for g in range(2)], xnb, masked=False)
    xs = xnb
    for n in range(NT):
        psb_ = ctx.bank()
        pv = psb_.tv[:, 0:512]
        row = srow_b[0:1, n * 512:(n + 1) * 512]
        bcast_row(ctx, row, pv)
        for dt_ in range(DT):
            o = xs[dt_][:, n * 512:(n + 1) * 512]
            p.op("vector", lambda e, o=o, b=pv:
                 e.tensor_tensor(o.ap, o.ap, b.ap, ALU.mult),
                 reads=[o, pv], writes=[o], tag="xs_mul")
    for ei in range(8):
        sel = ctx.sel16[0:8, ei * 128:(ei + 1) * 128]
        for n in range(NT):
            psx = ctx.bank()
            pv = psx.tv[:, 0:512]
            rt = mask8[0:8, n * 512:(n + 1) * 512]
            matmul_acc(ctx, pv, [(sel, rt)], tag="mm_mb")
            o = maskb[:, n * 512:(n + 1) * 512]
            p.op("vector", lambda e, o=o, i=pv: e.tensor_copy(o.ap, i.ap),
                 reads=[pv], writes=[o], tag="ev_mb")
        expert(ctx.drams["moe_rtgu"][ei],
               [ctx.drams["moe_rtd"][ei][g] for g in range(2)], xs, masked=True)
    ws.release()
    for i in range(DT):
        ctx.ab.free(f"xnb{i}")
    for i in range(8):
        ctx.ab.free(f"moe_a{i}")
    for nm in ["moe_gb", "moe_mb", "moe_mkb", "moe_mask8"]:
        ctx.ab.free(nm)
    ctx.af.free("moe_sm")


def build_core_program(phases, host):
    nc = bass.Bass("TRN2", target_bir_lowering=False)
    p = Prog(nc)
    ctx = Ctx(p)

    def din(name, arr):
        t = nc.dram_tensor(name, list(arr.shape), F32, kind="ExternalInput").ap()
        ctx.drams[name] = t
        return t

    out_names = []

    def dout(name, shape):
        t = nc.dram_tensor(name, list(shape), F32, kind="ExternalOutput").ap()
        ctx.drams[name] = t
        out_names.append(name)
        return t

    wnames = set()
    for ph in phases:
        wnames.update(PHASE_WEIGHTS[ph])
    for name in sorted(wnames):
        din(name, host["dram_inputs"][name])
    din("xfm", np.zeros((D, T), np.float32))
    din("vp", host["vecpack"])

    need_xi_in = ("l0_attn" in phases or "l0_glu" in phases) \
        and "l0_mix" not in phases
    need_xi_out = ("l0_mix" in phases or "l0_attn" in phases) \
        and "l0_glu" not in phases
    need_x_out = "l0_glu" in phases or "l0_ffn" in phases \
        or "l1_mamba" in phases or "l1_moe" in phases
    if need_xi_in:
        din("xi_in", np.zeros((D, T), np.float32))
    if need_xi_out:
        dout("xi_out", (D, T))
    if need_x_out:
        dout("x_out", (D, T))

    AF_EL = max(PHASE_ARENA[ph][0] for ph in phases)
    AB_EL = max(PHASE_ARENA[ph][1] for ph in phases)
    af_buf = p.sbuf("arena_f32", 128, AF_EL, F32)
    ab_buf = p.sbuf("arena_bf16", 128, AB_EL, BF16)
    ctx.af = Arena(af_buf)
    ctx.ab = Arena(ab_buf)
    ctx.banks_f32 = [p.psum(f"pf{i}", 128, 512, F32) for i in range(6)]
    ctx.banks_bf = [p.psum(f"pb{i}", 128, 1024, BF16) for i in range(2)]
    vp_arr = host["vecpack"]
    ctx.vp = p.sbuf("vecpack", 128, vp_arr.shape[1], F32)
    ctx.vp_index = host["vp_index"]
    ctx.rowb = p.sbuf("rowb", 1, 3072, BF16)
    ctx.rowf = p.sbuf("rowf", 1, 2048, F32)
    if "cmask" in wnames:
        ctx.masks = p.sbuf("c_masks", 128, 512, BF16).tv
    if "csel" in wnames:
        ctx.sel16 = p.sbuf("c_sel16", 16, 2048, BF16).tv
    if "l1_fc1" in wnames:
        ctx.drams["xspill"] = nc.dram_tensor("xspill", [D, T], F32,
                                             kind="Internal").ap()
        ctx.drams["ysp"] = nc.dram_tensor("ysp", [2048, T], BF16,
                                          kind="Internal").ap()

    with nc.Block() as block:
        p.op("sync", lambda e: e.dma_start(out=ctx.vp.tv.ap, in_=ctx.drams["vp"]),
             writes=[ctx.vp.tv], dma="vp")
        emit_consts(ctx)
        if ctx.masks is not None:
            p.op("gpsimd", lambda e: e.dma_start(out=ctx.masks.ap,
                                                 in_=ctx.drams["cmask"]),
                 writes=[ctx.masks], dma="cmask")
        if ctx.sel16 is not None:
            p.op("gpsimd", lambda e: e.dma_start(out=ctx.sel16.ap,
                                                 in_=ctx.drams["csel"]),
                 writes=[ctx.sel16], dma="csel")

        st = {}
        xres = [ctx.af.alloc(f"xr{i}", T) for i in range(DT)]
        st["xres"] = xres
        for i in range(DT):
            src = ctx.drams["xfm"][i * 128:(i + 1) * 128, :]
            p.op("sync", lambda e, d=xres[i], s=src: e.dma_start(out=d.ap, in_=s),
                 writes=[xres[i]], dma=f"xload{i}")
        if need_xi_in:
            xinner = [ctx.ab.alloc(f"xi{i}", T) for i in range(DT)]
            st["xinner"] = xinner
            for i in range(DT):
                src = ctx.drams["xi_in"][i * 128:(i + 1) * 128, :]
                p.op("gpsimd", lambda e, d=xinner[i], s=src: e.dma_start(out=d.ap, in_=s),
                     writes=[xinner[i]], dma=f"xiload{i}")

        for ph in phases:
            PHASE_EMITTERS[ph](ctx, st)

        final_tvs = []
        if need_xi_out:
            for i in range(DT):
                dst = ctx.drams["xi_out"][i * 128:(i + 1) * 128, :]
                tv = st["xinner"][i]
                p.op("gpsimd", lambda e, d=dst, s=tv: e.dma_start(out=d, in_=s.ap),
                     reads=[tv], dma="st_out")
                final_tvs.append(tv)
        if need_x_out:
            for i in range(DT):
                dst = ctx.drams["x_out"][i * 128:(i + 1) * 128, :]
                tv = st["xres"][i]
                p.op("sync", lambda e, d=dst, s=tv: e.dma_start(out=d, in_=s.ap),
                     reads=[tv], dma="st_out")
                final_tvs.append(tv)
        n_st = sum(1 for o in p.ops if o.dma_sem == "st_out")
        p.op("sync", lambda e, n=n_st: e.wait_ge(SEMS["st_out"], 16 * n),
             tag="final_wait")

        global SEMS
        SEMS = {}
        sem_cms = []
        names = list(ENGINES) + p.dma_sems_used()
        for nm in names:
            cm = nc.semaphore(f"s_{nm}")
            SEMS[nm] = cm.__enter__()
            sem_cms.append(cm)
        p.emit(block, SEMS)
        for cm in reversed(sem_cms):
            cm.__exit__(None, None, None)
    p.close_bufs()
    return nc, out_names


SEMS = {}
PHASE_EMITTERS = {"l0_mix": emit_l0_mix, "l0_attn": emit_l0_attn,
                  "l0_glu": emit_l0_glu, "l0_ffn": emit_l0_ffn,
                  "l1_mamba": emit_l1_mamba, "l1_moe": emit_l1_moe}
PHASE_ARENA = {
    "l0_mix": (19000, 54000),
    "l0_attn": (17500, 53000),
    "l0_glu": (17500, 56000),
    "l0_ffn": (19000, 50000),
    "l1_mamba": (18500, 56000),
    "l1_moe": (19000, 47000),
}
PHASE_WEIGHTS = {
    "l0_mix": ["l0_gvd"],
    "l0_attn": ["l0_qkv3", "l0_wo", "cmask"],
    "l0_glu": ["l0_gu", "l0_wout0", "l0_wout1", "l0_wout2", "l0_wout3"],
    "l0_ffn": ["l0f_w"],
    "l1_mamba": ["l1_fc1", "l1_inproj", "l1_xpd", "l1_xpb", "l1_xpc",
                 "l1_dtp", "l1_outproj", "l1_fc2", "csel"],
    "l1_moe": ["moe_gate", "moe_shgu", "moe_shd", "moe_rtgu", "moe_rtd",
               "csel"],
}


def prep_host(inp):
    vp = VecPack()
    vp.add("w_rms_mix", inp["L0_attn_w_rms_mix"])
    vp.add("bg", inp["L0_attn_bg"])
    vp.add("bv", inp["L0_attn_bv"])
    vp.add("bd", inp["L0_attn_bd"])
    vp.add("w_rms_attn", inp["L0_attn_w_rms_attn"])
    vp.add("w_rms_ffn", inp["L0_attn_w_rms_ffn"])
    vp.add("eps_1e5", np.full(128, 1e-5, np.float32))
    vp.add("eps_1e6", np.full(128, 1e-6, np.float32))
    vp.add("one", np.full(128, 1.0, np.float32))
    vp.add("fc1_b", inp["L1_attn_fc1_b"])
    for w in range(4):
        vp.add(f"cw{w}", inp["L1_attn_conv_w"][:, w])
    vp.add("conv_b", inp["L1_attn_conv_b"])
    vp.add("dt_b", inp["L1_attn_dt_proj_b"])
    vp.add("Dvec", inp["L1_attn_D"])
    vp.add("fc2_b", inp["L1_attn_fc2_b"])
    al = np.asarray(inp["L1_attn_A_log"], np.float32)
    vp.add_image("Alog", al.reshape(16, 128, 16).transpose(1, 0, 2).reshape(128, 256))

    j = np.arange(128)[:, None]
    kk = np.arange(256)[None, :]
    maskN = np.where(kk < 128, (kk > j), (kk - 128) <= j).astype(np.float32)
    mask0 = np.where(kk < 128, False, (kk - 128) <= j).astype(np.float32)
    cmask = np.concatenate([maskN, mask0], axis=1)

    wout = inp["L0_attn_Wout"]
    _g = lhsT_panels(inp["L0_attn_Wg"])
    _v = lhsT_panels(inp["L0_attn_Wv"])
    _d = lhsT_panels(inp["L0_attn_Wd"])
    _qkv = lhsT_panels(inp["L0_attn_Wqkv"])
    _ga = lhsT_panels(inp["L0_attn_Wgate"])
    _up = lhsT_panels(inp["L0_attn_Wup"])
    dram = {
        "l0_gvd": np.ascontiguousarray(np.concatenate([_g, _v, _d], axis=2)),
        "l0_qkv3": np.ascontiguousarray(np.concatenate(
            [_qkv[0:8], _qkv[8:16], _qkv[16:24]], axis=2)),
        "l0_wo": lhsT_panels(inp["L0_attn_Wo"]),
        "l0_gu": np.ascontiguousarray(np.concatenate([_ga, _up], axis=2)),
        "l0_wout0": lhsT_panels(wout[:, 0:1024]),
        "l0_wout1": lhsT_panels(wout[:, 1024:2048]),
        "l0_wout2": lhsT_panels(wout[:, 2048:3072]),
        "l0_wout3": lhsT_panels(wout[:, 3072:4096]),
        "l0f_w": lhsT_panels(inp["L0_ffn_weight"]),
        "cmask": np.ascontiguousarray(cmask),
    }
    xpw = np.asarray(inp["L1_attn_x_proj_w"], np.float32)
    dtpw = np.asarray(inp["L1_attn_dt_proj_w"], np.float32)
    sel = np.zeros((16, 16, 128), np.float32)
    for i in range(16):
        sel[i, i, :] = 1.0
    dram.update({
        "l1_fc1": lhsT_panels(inp["L1_attn_fc1_w"]),
        "l1_inproj": lhsT_panels(inp["L1_attn_in_proj_w"]),
        "l1_xpd": np.ascontiguousarray(
            xpw[0:64].reshape(64, 16, 128).transpose(1, 2, 0)),
        "l1_xpb": np.ascontiguousarray(
            xpw[64:80].reshape(16, 16, 128).transpose(1, 2, 0)),
        "l1_xpc": np.ascontiguousarray(
            xpw[80:96].reshape(16, 16, 128).transpose(1, 2, 0)),
        "l1_dtp": np.ascontiguousarray(
            dtpw.reshape(16, 128, 64).transpose(0, 2, 1)),
        "l1_outproj": lhsT_panels(inp["L1_attn_out_proj_w"]),
        "l1_fc2": lhsT_panels(inp["L1_attn_fc2_w"]),
        "csel": np.ascontiguousarray(sel.reshape(16, 2048)),
    })
    shd = np.asarray(inp["L1_ffn_sh_down"], np.float32).T
    dram.update({
        "moe_gate": np.ascontiguousarray(
            np.asarray(inp["L1_ffn_gate_w"], np.float32).reshape(8, 128, 8)),
        "moe_shgu": np.concatenate(
            [lhsT_panels(np.asarray(inp["L1_ffn_sh_gate"], np.float32).T),
             lhsT_panels(np.asarray(inp["L1_ffn_sh_up"], np.float32).T)], axis=2),
        "moe_shd": np.stack([lhsT_panels(shd[:, g * 1024:(g + 1) * 1024])
                             for g in range(2)]),
        "moe_rtgu": np.stack([np.concatenate(
            [lhsT_panels(np.asarray(inp["L1_ffn_rt_gate"][e], np.float32).T),
             lhsT_panels(np.asarray(inp["L1_ffn_rt_up"][e], np.float32).T)],
            axis=2) for e in range(8)]),
        "moe_rtd": np.stack([np.stack([
            lhsT_panels(np.asarray(inp["L1_ffn_rt_down"][e], np.float32)
                        .T[:, g * 1024:(g + 1) * 1024]) for g in range(2)])
            for e in range(8)]),
    })
    host = {
        "dram_inputs": dram,
        "vecpack": vp.array(),
        "vp_index": vp.index,
    }
    return host


def run_launch(phases, host, per_core_io):
    nc, out_names = build_core_program(phases, host)
    wnames = set()
    for ph in phases:
        wnames.update(PHASE_WEIGHTS[ph])
    in_maps = []
    for c in range(NCORES):
        m = {name: host["dram_inputs"][name] for name in wnames}
        m["vp"] = host["vecpack"]
        m.update(per_core_io[c])
        in_maps.append(m)
    res = run_bass_kernel_spmd(nc, in_maps, core_ids=list(range(NCORES)))
    return res.results, out_names


NCORES = 8

LAUNCHES = [ALL_PHASES]


def kernel(**inputs):
    host = prep_host(inputs)
    x = np.asarray(inputs["L0_attn_x"], np.float32)
    xfm = np.ascontiguousarray(x.transpose(0, 2, 1))
    cur_x = [xfm[c] for c in range(NCORES)]
    cur_xi = [None] * NCORES
    for phases in LAUNCHES:
        io = []
        for c in range(NCORES):
            m = {"xfm": cur_x[c]}
            if ("l0_attn" in phases or "l0_glu" in phases)                     and "l0_mix" not in phases:
                m["xi_in"] = cur_xi[c]
            io.append(m)
        results, outs = run_launch(phases, host, io)
        for c in range(NCORES):
            if "xi_out" in outs:
                cur_xi[c] = results[c]["xi_out"]
            if "x_out" in outs:
                cur_x[c] = results[c]["x_out"]
    out = np.stack(cur_x)
    return np.ascontiguousarray(out.transpose(0, 2, 1))


# version 29
# speedup vs baseline: 1.0324x; 1.0324x over previous
import numpy as np
import ml_dtypes
import concourse.bass as bass
import concourse.mybir as mybir
from concourse.bass_utils import run_bass_kernel_spmd

F32 = mybir.dt.float32
BF16 = mybir.dt.bfloat16
AF = mybir.ActivationFunctionType
ALU = mybir.AluOpType
AX = mybir.AxisListType

B, T, D = 8, 2048, 1024
DT = D // 128
NT = T // 512
ENGINES = ("sync", "vector", "scalar", "tensor", "gpsimd")

ALL_PHASES = ("l0_mix", "l0_attn", "l0_glu", "l0_ffn", "l1_mamba", "l1_moe")


class Buf:
    __slots__ = ("name", "handle", "parts", "free", "dtype", "space")

    def __init__(self, name, handle, parts, free, dtype, space):
        self.name, self.handle = name, handle
        self.parts, self.free, self.dtype, self.space = parts, free, dtype, space

    @property
    def tv(self):
        return TV(self, 0, self.parts, 0, self.free)

    def __getitem__(self, sl):
        return self.tv[sl]


class TV:
    __slots__ = ("buf", "p0", "p1", "f0", "f1")

    def __init__(self, buf, p0, p1, f0, f1):
        self.buf, self.p0, self.p1, self.f0, self.f1 = buf, p0, p1, f0, f1
        assert 0 <= p0 < p1 <= buf.parts and 0 <= f0 < f1 <= buf.free, \
            f"bad TV {buf.name}[{p0}:{p1},{f0}:{f1}] of [{buf.parts},{buf.free}]"

    @property
    def ap(self):
        return self.buf.handle[self.p0:self.p1, self.f0:self.f1]

    @property
    def P(self):
        return self.p1 - self.p0

    @property
    def F(self):
        return self.f1 - self.f0

    def __getitem__(self, sl):
        psl, fsl = sl if isinstance(sl, tuple) else (sl, slice(None))
        p0, p1, st = psl.indices(self.P)
        assert st == 1
        f0, f1, st = fsl.indices(self.F)
        assert st == 1
        return TV(self.buf, self.p0 + p0, self.p0 + p1, self.f0 + f0, self.f0 + f1)

    def overlaps(self, o):
        return (self.buf is o.buf and self.p0 < o.p1 and o.p0 < self.p1
                and self.f0 < o.f1 and o.f0 < self.f1)

    def covers(self, o):
        return (self.buf is o.buf and self.p0 <= o.p0 and self.p1 >= o.p1
                and self.f0 <= o.f0 and self.f1 >= o.f1)


class Op:
    __slots__ = ("i", "eng", "fn", "reads", "writes", "dma_sem", "need_inc", "tag")

    def __init__(self, i, eng, fn, reads, writes, dma_sem, tag):
        self.i, self.eng, self.fn = i, eng, fn
        self.reads, self.writes, self.dma_sem = reads, writes, dma_sem
        self.need_inc = False
        self.tag = tag


class Arena:

    def __init__(self, buf, align=16):
        self.buf = buf
        self.align = align
        self.free_list = [(0, buf.free)]
        self.live = {}

    def alloc(self, name, elems):
        a = self.align
        elems = -(-elems // a) * a
        for idx, (lo, hi) in enumerate(self.free_list):
            lo_a = -(-lo // a) * a
            if hi - lo_a >= elems:
                if lo_a > lo:
                    self.free_list[idx] = (lo, lo_a)
                    self.free_list.insert(idx + 1, (lo_a + elems, hi))
                else:
                    self.free_list[idx] = (lo_a + elems, hi)
                    if self.free_list[idx][0] >= self.free_list[idx][1]:
                        del self.free_list[idx]
                self.live[name] = (lo_a, lo_a + elems)
                return TV(self.buf, 0, self.buf.parts, lo_a, lo_a + elems)
        raise MemoryError(
            f"arena {self.buf.name}: cannot alloc {elems} for {name}; "
            f"free={self.free_list} live={self.live}")

    def free(self, name):
        lo, hi = self.live.pop(name)
        self.free_list.append((lo, hi))
        self.free_list.sort()
        merged = []
        for seg in self.free_list:
            if merged and merged[-1][1] >= seg[0]:
                merged[-1] = (merged[-1][0], max(merged[-1][1], seg[1]))
            else:
                merged.append(list(seg))
        self.free_list = [tuple(s) for s in merged]


class Prog:
    def __init__(self, nc):
        self.nc = nc
        self.ops = []
        self.deps = []
        self.records = {}
        self._ctxs = []

    def sbuf(self, name, parts, free, dtype):
        cm = self.nc.sbuf_tensor(name, [parts, free], dtype)
        h = cm.__enter__()
        self._ctxs.append(cm)
        return Buf(name, h, parts, free, dtype, "sbuf")

    def psum(self, name, parts, free, dtype):
        cm = self.nc.psum_tensor(name, [parts, free], dtype)
        h = cm.__enter__()
        self._ctxs.append(cm)
        return Buf(name, h, parts, free, dtype, "psum")

    def close_bufs(self):
        for cm in reversed(self._ctxs):
            cm.__exit__(None, None, None)
        self._ctxs = []

    def op(self, eng, fn, reads=(), writes=(), dma=None, tag=""):
        i = len(self.ops)
        reads = tuple(tv.buf.tv if tv.buf.space == "psum" else tv for tv in reads)
        writes = tuple(tv.buf.tv if tv.buf.space == "psum" else tv for tv in writes)
        o = Op(i, eng, fn, tuple(reads), tuple(writes), dma, tag)
        dep = set()
        for tv in o.reads:
            for (rtv, w, _rd) in self.records.get(tv.buf.name, ()):
                if w is not None and tv.overlaps(rtv):
                    dep.add(w)
        for tv in o.writes:
            for (rtv, w, rd) in self.records.get(tv.buf.name, ()):
                if tv.overlaps(rtv):
                    if w is not None:
                        dep.add(w)
                    dep.update(rd)
        for tv in o.reads:
            recs = self.records.setdefault(tv.buf.name, [])
            hit = False
            for r in recs:
                if r[0].covers(tv) and r[1] is not None:
                    r[2].append(i)
                    hit = True
            if not hit:
                recs.append([tv, None, [i]])
        for tv in o.writes:
            recs = self.records.setdefault(tv.buf.name, [])
            recs[:] = [r for r in recs if not tv.covers(r[0])]
            recs.append([tv, i, []])
        dep.discard(i)
        self.ops.append(o)
        self.deps.append(dep)
        return i

    def dep(self, a, b):
        self.deps[b].add(a)

    def barrier_read(self, tvs, eng="sync"):
        return self.op(eng, lambda e: e.nop(), reads=tvs, tag="barrier")

    def emit(self, block, sems):
        ops, deps = self.ops, self.deps
        for o in ops:
            for d in deps[o.i]:
                p = ops[d]
                if p.dma_sem is None and p.eng == "tensor" and o.eng == "tensor":
                    continue
                if p.dma_sem is None:
                    p.need_inc = True
        inc_count = {e: 0 for e in ENGINES}
        dma_count = {}
        prod_val = {}
        for o in ops:
            if o.dma_sem is not None:
                c = dma_count.get(o.dma_sem, 0) + 1
                dma_count[o.dma_sem] = c
                prod_val[o.i] = (o.dma_sem, 16 * c)
            elif o.need_inc:
                inc_count[o.eng] += 1
                prod_val[o.i] = (o.eng, inc_count[o.eng])
            else:
                prod_val[o.i] = None
        waits = []
        for o in ops:
            w = {}
            for d in deps[o.i]:
                p = ops[d]
                if p.dma_sem is None and p.eng == "tensor" and o.eng == "tensor":
                    continue
                s, v = prod_val[d]
                if w.get(s, -1) < v:
                    w[s] = v
            waits.append(w)
        self.n_incs = dict(inc_count)
        self.n_dma = dict(dma_count)

        def body_for(engine):
            def body(eng):
                hw = {}
                for o in ops:
                    if o.eng != engine:
                        continue
                    for s, v in sorted(waits[o.i].items()):
                        if hw.get(s, 0) >= v:
                            continue
                        eng.wait_ge(sems[s], v)
                        hw[s] = v
                    inst = o.fn(eng)
                    if o.dma_sem is not None:
                        inst.then_inc(sems[o.dma_sem], 16)
                    elif o.need_inc:
                        inst.then_inc(sems[o.eng], 1)
            return body

        for e in ENGINES:
            getattr(block, e)(body_for(e))

    def dma_sems_used(self):
        return sorted({o.dma_sem for o in self.ops if o.dma_sem is not None})


def lhsT_panels(w_out_in):
    O, I = w_out_in.shape
    assert O % 128 == 0 and I % 128 == 0, (O, I)
    MT, KT = O // 128, I // 128
    a = w_out_in.reshape(MT, 128, KT, 128).transpose(0, 3, 2, 1)
    return np.ascontiguousarray(a.reshape(MT, 128, KT * 128), dtype=np.float32)


def lhsT_panels_rows(w_out_in, row_lo, row_hi):
    w = w_out_in[row_lo:row_hi]
    M, I = w.shape
    KT = I // 128
    a = w.reshape(M, KT, 128).transpose(2, 1, 0)
    return np.ascontiguousarray(a.reshape(1, 128, KT * M), dtype=np.float32)


def rhs_panels(w_out_in):
    O, I = w_out_in.shape
    assert I % 128 == 0
    KT = I // 128
    return np.ascontiguousarray(w_out_in.T.reshape(KT, 128, O), dtype=np.float32)


def vec_image(v):
    v = np.asarray(v, dtype=np.float32).ravel()
    n = v.size // 128
    assert v.size == n * 128, v.size
    return np.ascontiguousarray(v.reshape(n, 128).T)


class VecPack:
    def __init__(self):
        self.cols = []
        self.index = {}
        self.n = 0

    def add(self, name, v):
        img = vec_image(v)
        self.index[name] = (self.n, img.shape[1])
        self.cols.append(img)
        self.n += img.shape[1]

    def add_image(self, name, img):
        img = np.ascontiguousarray(np.asarray(img, np.float32))
        assert img.shape[0] == 128
        self.index[name] = (self.n, img.shape[1])
        self.cols.append(img)
        self.n += img.shape[1]

    def add_scalar_slot(self, name):
        self.index[name] = (self.n, 1)
        self.cols.append(np.zeros((128, 1), np.float32))
        self.n += 1

    def array(self):
        return np.ascontiguousarray(np.concatenate(self.cols, axis=1))


class Ctx:
    def __init__(self, prog):
        self.p = prog
        self.af = None
        self.ab = None
        self.banks_f32 = []
        self.banks_bf = []
        self._rrf = 0
        self._rrb = 0
        self.vp = None
        self.of32 = None
        self.if32 = None
        self.sel16 = None
        self.masks = None
        self.vp_index = {}
        self.ones = None
        self.ident = None
        self.rowb = None
        self.rowf = None
        self.drams = {}

    def bank(self):
        b = self.banks_f32[self._rrf % len(self.banks_f32)]
        self._rrf += 1
        return b

    def bankbf(self):
        b = self.banks_bf[self._rrb % len(self.banks_bf)]
        self._rrb += 1
        return b

    def vcol(self, name, j=0):
        lo, n = self.vp_index[name]
        assert j < n, (name, j, n)
        return self.vp[:, lo + j:lo + j + 1]

    def vcols(self, name, j0, j1):
        lo, n = self.vp_index[name]
        assert j1 <= n
        return self.vp[:, lo + j0:lo + j1]


def emit_consts(ctx):
    p = ctx.p
    ones = p.sbuf("c_ones", 128, 128, BF16)
    ident = p.sbuf("c_ident", 128, 128, BF16)
    of32 = p.sbuf("c_of32", 128, 128, F32)
    ctx.ones, ctx.ident, ctx.of32 = ones, ident, of32.tv
    p.op("vector", lambda e: e.memset(ones.tv.ap, 1.0), writes=[ones.tv])
    p.op("vector", lambda e: e.memset(of32.tv.ap, 1.0), writes=[of32.tv])
    p.op("vector", lambda e: e.memset(ident.tv.ap, 1.0), writes=[ident.tv])
    p.op("gpsimd",
         lambda e: e.affine_select(ident.tv.ap, ident.tv.ap, [[1, 128]],
                                   ALU.is_equal, 0.0, base=0,
                                   channel_multiplier=-1),
         reads=[ident.tv], writes=[ident.tv])
    if32 = p.sbuf("c_if32", 128, 128, F32)
    ctx.if32 = if32.tv
    p.op("vector", lambda e: e.memset(if32.tv.ap, 1.0), writes=[if32.tv])
    p.op("gpsimd",
         lambda e: e.affine_select(if32.tv.ap, if32.tv.ap, [[1, 128]],
                                   ALU.is_equal, 0.0, base=0,
                                   channel_multiplier=-1),
         reads=[if32.tv], writes=[if32.tv])


class WStream:

    def __init__(self, ctx, name, dram_ap, n_panels, panel_free, nslots=4):
        self.ctx, self.name = ctx, name
        self.dram = dram_ap
        self.n_panels, self.pf = n_panels, panel_free
        self.slots = [ctx.ab.alloc(f"{name}_s{i}", panel_free) for i in range(nslots)]
        self.nslots = nslots
        self.count = 0

    def panel(self, i):
        return self.load(self.dram[i])

    def load(self, src_ap):
        ctx = self.ctx
        slot = self.slots[self.count % self.nslots]
        sem = f"w_{self.name}_{self.count % self.nslots}"
        self.count += 1
        shp = list(src_ap.shape)
        fw = int(np.prod(shp[1:])) if len(shp) > 1 else int(shp[0])
        assert fw <= slot.F, (self.name, fw, slot.F)
        dst = slot[:, 0:fw] if fw < slot.F else slot
        ctx.p.op("gpsimd", lambda e, d=dst, s=src_ap: e.dma_start(out=d.ap, in_=s),
                 writes=[dst], dma=sem, tag="wload")
        return dst

    def release(self):
        for i in range(self.nslots):
            self.ctx.ab.free(f"{self.name}_s{i}")


def matmul_acc(ctx, psum_tv, pairs, tag="mm"):
    p = ctx.p
    n = len(pairs)
    for k, (lt, rt) in enumerate(pairs):
        p.op("tensor",
             lambda e, o=psum_tv, a=lt, b=rt, s=(k == 0), t=(k == n - 1):
                 e.matmul(o.ap, a.ap, b.ap, start=s, stop=t),
             reads=[lt, rt], writes=[psum_tv], tag=tag)


def transpose128(ctx, psum_bf_tv, src_tv):
    ident = ctx.ident
    ctx.p.op("tensor",
             lambda e, o=psum_bf_tv, i=src_tv, idt=ident.tv:
                 e.transpose(o.ap, i.ap, idt.ap),
             reads=[src_tv, ident.tv], writes=[psum_bf_tv], tag="transp")


def bcast_row(ctx, row_tv, out_psum_tv, tag="bcast"):
    lhsT = ctx.ones[0:1, 0:out_psum_tv.P]
    ctx.p.op("tensor",
             lambda e, o=out_psum_tv, l=lhsT, r=row_tv:
                 e.matmul(o.ap, l.ap, r.ap, start=True, stop=True),
             reads=[row_tv, lhsT], writes=[out_psum_tv], tag=tag)


def emit_norm(ctx, src_tiles, out_tiles, kind, gamma=None, eps=1e-5,
              out2_tiles=None, eng_apply="vector", exact=False, keep_rows=None,
              keep_cols=None):
    p = ctx.p
    DTl = len(src_tiles)
    Dl = DTl * 128
    src_bf = src_tiles[0].buf.dtype == BF16
    tb = [ctx.ab.alloc(f"nrm_tb{i}", 512) for i in range(2)]
    if exact:
        sq = [ctx.af.alloc(f"nrm_sq{i}", 512) for i in range(2)]
        onecol = ctx.of32[:, 0:1]
    else:
        sq = [ctx.ab.alloc(f"nrm_sq{i}", 512) for i in range(2)]
        onecol = ctx.ones[:, 0:1]
    d1 = [ctx.af.alloc(f"nrm_d1{i}", 512) for i in range(2)]
    rowb, rowf = ctx.rowb, ctx.rowf
    ebias = ctx.vcol("eps_1e5" if eps == 1e-5 else "eps_1e6")[0:1, :]
    Tl = src_tiles[0].F
    NTl = Tl // 512
    for n in range(NTl):
        t0, t1 = n * 512, (n + 1) * 512
        ps_sum = ctx.bank()
        ps_sumsq = ctx.bank()
        sum_tv = ps_sum.tv[0:1, 0:512]
        sumsq_tv = ps_sumsq.tv[0:1, 0:512]
        for dt_ in range(DTl):
            s = src_tiles[dt_][:, t0:t1]
            if kind == "ln":
                if exact:
                    xb = s
                elif src_bf:
                    xb = s
                else:
                    xb = tb[dt_ % 2]
                    p.op("vector", lambda e, o=xb, i=s: e.tensor_copy(o.ap, i.ap),
                         reads=[s], writes=[xb], tag="nrm_cast")
                p.op("tensor",
                     lambda e, o=sum_tv, a=onecol, b=xb, s_=(dt_ == 0),
                            t_=(dt_ == DTl - 1):
                         e.matmul(o.ap, a.ap, b.ap, start=s_, stop=t_),
                     reads=[onecol, xb], writes=[sum_tv], tag="nrm_mmx")
            sqt = sq[dt_ % 2]
            p.op("scalar", lambda e, o=sqt, i=s: e.activation(o.ap, i.ap, AF.Square),
                 reads=[s], writes=[sqt], tag="nrm_sq")
            p.op("tensor",
                 lambda e, o=sumsq_tv, a=onecol, b=sqt, s_=(dt_ == 0),
                        t_=(dt_ == DTl - 1):
                     e.matmul(o.ap, a.ap, b.ap, start=s_, stop=t_),
                 reads=[onecol, sqt], writes=[sumsq_tv], tag="nrm_mmsq")
        r0 = rowf.tv[0:1, 0:512]
        r1 = rowf.tv[0:1, 512:1024]
        r2 = rowf.tv[0:1, 1024:1536]
        r3 = rowf.tv[0:1, 1536:2048]
        if kind == "ln":
            p.op("vector", lambda e, o=r2, i=sum_tv:
                 e.tensor_scalar(o.ap, i.ap, 1.0 / Dl, None, ALU.mult),
                 reads=[sum_tv], writes=[r2], tag="nrm_m")
            p.op("scalar", lambda e, o=r3, i=r2: e.activation(o.ap, i.ap, AF.Square),
                 reads=[r2], writes=[r3], tag="nrm_msq")
            p.op("vector", lambda e, o=r0, i=sumsq_tv, s2=r3:
                 e.scalar_tensor_tensor(o.ap, i.ap, 1.0 / Dl, s2.ap,
                                        ALU.mult, ALU.subtract),
                 reads=[sumsq_tv, r3], writes=[r0], tag="nrm_var")
        else:
            p.op("vector", lambda e, o=r0, i=sumsq_tv:
                 e.tensor_scalar(o.ap, i.ap, 1.0 / Dl, None, ALU.mult),
                 reads=[sumsq_tv], writes=[r0], tag="nrm_var")
        p.op("scalar", lambda e, o=r1, i=r0, b=ebias:
             e.activation(o.ap, i.ap, AF.Sqrt, bias=b.ap),
             reads=[r0, ebias], writes=[r1], tag="nrm_sd")
        p.op("vector", lambda e, o=r3, i=r1: e.reciprocal(o.ap, i.ap),
             reads=[r1], writes=[r3], tag="nrm_rstd")
        rs_b = rowb.tv[0:1, 0:512]
        p.op("vector", lambda e, o=rs_b, i=r3: e.tensor_copy(o.ap, i.ap),
             reads=[r3], writes=[rs_b], tag="nrm_rb")
        if keep_rows is not None:
            km, kr = keep_rows
            o_ = kr[0:1, t0:t1]
            p.op("vector", lambda e, o=o_, i=r3: e.tensor_copy(o.ap, i.ap),
                 reads=[r3], writes=[o_], tag="nrm_keepr")
            o_ = km[0:1, t0:t1]
            p.op("vector", lambda e, o=o_, i=r2: e.tensor_copy(o.ap, i.ap),
                 reads=[r2], writes=[o_], tag="nrm_keepm")
        if keep_cols is not None:
            kmc, krc = keep_cols
            for ti in range(4):
                for (row, cols) in ((r2, kmc), (r3, krc)):
                    psy = ctx.bank()
                    pc = psy.tv[:, 0:1]
                    lt = row[0:1, ti * 128:(ti + 1) * 128]
                    p.op("tensor", lambda e, o=pc, a=lt, b=ctx.of32[0:1, 0:1]:
                         e.matmul(o.ap, a.ap, b.ap, start=True, stop=True),
                         reads=[lt, ctx.of32[0:1, 0:1]], writes=[pc], tag="mm_kc")
                    o_ = cols[:, n * 4 + ti:n * 4 + ti + 1]
                    p.op("vector", lambda e, o=o_, i=pc: e.tensor_copy(o.ap, i.ap),
                         reads=[pc], writes=[o_], tag="ev_kc")
        ps_r = ctx.bank()
        rB = ps_r.tv[:, 0:512]
        bcast_row(ctx, rs_b, rB)
        if kind == "ln":
            m_b = rowb.tv[0:1, 512:1024]
            p.op("vector", lambda e, o=m_b, i=r2: e.tensor_copy(o.ap, i.ap),
                 reads=[r2], writes=[m_b], tag="nrm_mb")
            ps_m = ctx.bank()
            mB = ps_m.tv[:, 0:512]
            bcast_row(ctx, m_b, mB)
        for dt_ in range(DTl):
            s = src_tiles[dt_][:, t0:t1]
            o = out_tiles[dt_][:, t0:t1]
            if kind == "ln":
                dd = d1[dt_ % 2]
                p.op(eng_apply, lambda e, a=s, b=mB, oo=dd:
                     e.tensor_tensor(oo.ap, a.ap, b.ap, ALU.subtract),
                     reads=[s, mB], writes=[dd], tag="nrm_sub")
                p.op(eng_apply, lambda e, a=dd, b=rB, oo=o:
                     e.tensor_tensor(oo.ap, a.ap, b.ap, ALU.mult),
                     reads=[dd, rB], writes=[o], tag="nrm_mul")
                if out2_tiles is not None:
                    o2 = out2_tiles[dt_][:, t0:t1]
                    p.op(eng_apply, lambda e, a=dd, b=rB, oo=o2:
                         e.tensor_tensor(oo.ap, a.ap, b.ap, ALU.mult),
                         reads=[dd, rB], writes=[o2], tag="nrm_mul2")
            else:
                g = ctx.vcol(gamma, dt_)
                p.op(eng_apply, lambda e, a=s, sc=g, b=rB, oo=o:
                     e.scalar_tensor_tensor(oo.ap, a.ap, sc.ap, b.ap,
                                            ALU.mult, ALU.mult),
                     reads=[s, rB, g], writes=[o], tag="nrm_app")
                if out2_tiles is not None:
                    o2 = out2_tiles[dt_][:, t0:t1]
                    p.op(eng_apply, lambda e, a=s, sc=g, b=rB, oo=o2:
                         e.scalar_tensor_tensor(oo.ap, a.ap, sc.ap, b.ap,
                                                ALU.mult, ALU.mult),
                         reads=[s, rB, g], writes=[o2], tag="nrm_app2")
    for i in range(2):
        ctx.ab.free(f"nrm_tb{i}")
        (ctx.af if exact else ctx.ab).free(f"nrm_sq{i}")
        ctx.af.free(f"nrm_d1{i}")


def proj_fm(ctx, in_tiles, wstream, MT, evac, panel_order=None, tag="proj"):
    KT = len(in_tiles)
    order = panel_order if panel_order is not None else range(MT)
    for mt in order:
        w = wstream.panel(mt)
        for n in range(NT):
            ps = ctx.bank()
            pv = ps.tv[:, 0:512]
            pairs = [(w[:, kt * 128:(kt + 1) * 128],
                      in_tiles[kt][:, n * 512:(n + 1) * 512]) for kt in range(KT)]
            matmul_acc(ctx, pv, pairs, tag=tag)
            evac(mt, n, pv)


def emit_l0_mix(ctx, st):
    p = ctx.p
    xres = st["xres"]
    xinner = [ctx.ab.alloc(f"xi{i}", T) for i in range(DT)]
    st["xinner"] = xinner
    emit_norm(ctx, xres, xinner, "ln", eps=1e-5)

    xnb = [ctx.ab.alloc(f"xnb{i}", T) for i in range(DT)]
    emit_norm(ctx, xinner, xnb, "rms", gamma="w_rms_mix", eps=1e-6)

    sig = ctx.ab.alloc("sig_t", T)
    tnh = ctx.ab.alloc("tanh_t", T)
    xs = [ctx.ab.alloc(f"xs{i}", T) for i in range(2)]
    a_f = [ctx.af.alloc("a0", T)]
    h = [ctx.ab.alloc(f"h{i}", T) for i in range(2)]
    ws = WStream(ctx, "ws", None, DT, 3 * D, nslots=2)

    def mm_evac(wpanel, src_tiles, n, act, bname, mt, out_tile):
        ps = ctx.bank()
        pv = ps.tv[:, 0:512]
        pairs = [(wpanel[:, kt * 128:(kt + 1) * 128],
                  src_tiles[kt][:, n * 512:(n + 1) * 512]) for kt in range(DT)]
        matmul_acc(ctx, pv, pairs)
        o = out_tile[:, n * 512:(n + 1) * 512]
        bcol = ctx.vcol(bname, mt)
        p.op("scalar", lambda e, oo=o, i=pv, b=bcol:
             e.activation(oo.ap, i.ap, act, bias=b.ap),
             reads=[pv, bcol], writes=[o], tag="evac")

    for mt in range(DT):
        wall = ws.load(ctx.drams["l0_gvd"][mt])
        wgp, wvp, wdp = wall[:, 0:D], wall[:, D:2 * D], wall[:, 2 * D:3 * D]
        for n in range(NT):
            mm_evac(wgp, xnb, n, AF.Sigmoid, "bg", mt, sig)
        for n in range(NT):
            mm_evac(wvp, xnb, n, AF.Tanh, "bv", mt, tnh)
        xst = xs[mt % 2]
        p.op("vector", lambda e, oo=xst, x1=sig, x2=tnh:
             e.tensor_tensor(oo.ap, x1.ap, x2.ap, ALU.mult),
             reads=[sig, tnh], writes=[xst], tag="xs_mul")
        for n in range(NT):
            mm_evac(wdp, xnb, n, AF.Sigmoid, "bd", mt, sig)
        at = a_f[0]
        p.op("vector", lambda e, oo=at, i=sig:
             e.tensor_scalar(oo.ap, i.ap, 0.998, 0.001, ALU.mult, ALU.add),
             reads=[sig], writes=[at], tag="a_ts")
        ht = h[mt % 2]
        p.op("vector", lambda e, o=ht, a=at, x=xst:
             e.tensor_tensor_scan(o.ap, a.ap, x.ap, 0.0, ALU.mult, ALU.add),
             reads=[at, xst], writes=[ht], tag="scan")
        xi = xinner[mt]
        p.op("vector", lambda e, o=xi, a=xi, b=ht:
             e.tensor_tensor(o.ap, a.ap, b.ap, ALU.add),
             reads=[xi, ht], writes=[xi], tag="res_add")
    ws.release()
    ctx.ab.free("sig_t")
    ctx.ab.free("tanh_t")
    ctx.af.free("a0")
    for i in range(2):
        ctx.ab.free(f"xs{i}")
        ctx.ab.free(f"h{i}")
    for i in range(DT):
        ctx.ab.free(f"xnb{i}")


def emit_l0_attn(ctx, st):
    p = ctx.p
    xinner = st["xinner"]
    xnb = [ctx.ab.alloc(f"xnb{i}", T) for i in range(DT)]
    emit_norm(ctx, xinner, xnb, "rms", gamma="w_rms_attn", eps=1e-6)

    qb2 = [ctx.ab.alloc(f"at_q{i}", 512) for i in range(2)]
    kb2 = [ctx.ab.alloc(f"at_k{i}", 640) for i in range(2)]
    vb2 = [ctx.ab.alloc(f"at_v{i}", 640) for i in range(2)]
    prob = [ctx.ab.alloc(f"at_pr{i}", 256) for i in range(4)]
    attT = [ctx.ab.alloc(f"at_aT{i}", 256) for i in range(4)]
    ao_tm = ctx.ab.alloc("at_ao", 4096)
    ao_fm = ctx.ab.alloc("at_af", 4096)
    acol = ctx.af.alloc("at_acol", 64)
    wqkv = WStream(ctx, "wqkv", None, 8, 3 * D, nslots=2)
    wos = WStream(ctx, "wos", None, 8, D, nslots=2)
    masks = ctx.masks

    for tg in range(NT):
        t0 = tg * 512

        def emit_qkv(hp):
            qb, kb, vb = qb2[hp % 2], kb2[hp % 2], vb2[hp % 2]
            w3 = wqkv.load(ctx.drams["l0_qkv3"][hp])
            wq = w3[:, 0:D]
            ps = ctx.bank()
            pv = ps.tv[:, 0:512]
            pairs = [(wq[:, kt * 128:(kt + 1) * 128],
                      xnb[kt][:, t0:t0 + 512]) for kt in range(DT)]
            matmul_acc(ctx, pv, pairs, tag="mm_q")
            p.op("vector", lambda e, o=qb, i=pv: e.tensor_copy(o.ap, i.ap),
                 reads=[pv], writes=[qb], tag="ev_q")
            wk = w3[:, D:2 * D]
            if tg == 0:
                p.op("vector", lambda e, o=kb[:, 0:128]: e.memset(o.ap, 0.0),
                     writes=[kb[:, 0:128]], tag="kpad0")
                kranges = [(128, t0, 512)]
            else:
                kranges = [(0, t0 - 128, 512), (512, t0 + 384, 128)]
            for (df, src_t, ln) in kranges:
                ps = ctx.bank()
                pv = ps.tv[:, 0:ln]
                pairs = [(wk[:, kt * 128:(kt + 1) * 128],
                          xnb[kt][:, src_t:src_t + ln]) for kt in range(DT)]
                matmul_acc(ctx, pv, pairs, tag="mm_k")
                o = kb[:, df:df + ln]
                p.op("vector", lambda e, o=o, i=pv: e.tensor_copy(o.ap, i.ap),
                     reads=[pv], writes=[o], tag="ev_k")
            wv = w3[:, 2 * D:3 * D]
            if tg == 0:
                p.op("vector", lambda e, o=vb[:, 0:128]: e.memset(o.ap, 0.0),
                     writes=[vb[:, 0:128]], tag="vpad0")
                vtis = range(1, 5)
            else:
                vtis = range(0, 5)
            for vi in vtis:
                tt = t0 - 128 + vi * 128
                ps = ctx.bank()
                pv = ps.tv[:, 0:128]
                pairs = [(xnb[kt][:, tt:tt + 128],
                          wv[:, kt * 128:(kt + 1) * 128]) for kt in range(DT)]
                matmul_acc(ctx, pv, pairs, tag="mm_v")
                o = vb[:, vi * 128:(vi + 1) * 128]
                p.op("vector", lambda e, o=o, i=pv: e.tensor_copy(o.ap, i.ap),
                     reads=[pv], writes=[o], tag="ev_v")
            return qb, kb, vb

        def emit_chains(hp, qb, kb, vb):
            chains = [(cl, hh) for cl in range(4) for hh in range(2)]
            DEPTH = 5
            scs = {}

            def stage_A(ci):
                cl, hh = chains[ci]
                h0 = hh * 64
                q_tv = qb[h0:h0 + 64, cl * 128:(cl + 1) * 128]
                k_tv = kb[h0:h0 + 64, cl * 128:cl * 128 + 256]
                ps = ctx.banks_f32[ci % DEPTH]
                sc = ps.tv[:, 0:256]
                scs[ci] = sc
                p.op("tensor", lambda e, o=sc, a=q_tv, b=k_tv:
                     e.matmul(o.ap, a.ap, b.ap, start=True, stop=True),
                     reads=[q_tv, k_tv], writes=[sc], tag="mm_sc")

            def stage_BC(ci):
                cl, hh = chains[ci]
                h0 = hh * 64
                gi = ci % 4
                sc = scs[ci]
                nmax = acol[:, gi * 4:gi * 4 + 1]
                p.op("vector", lambda e, o=nmax, i=sc:
                     e.reduce_max(o.ap, i.ap, AX.X, negate=True),
                     reads=[sc], writes=[nmax], tag="sc_max")
                bias = acol[:, gi * 4 + 1:gi * 4 + 2]
                p.op("vector", lambda e, o=bias, i=nmax:
                     e.tensor_scalar(o.ap, i.ap, 0.125, None, ALU.mult),
                     reads=[nmax], writes=[bias], tag="sc_bias")
                pr = prob[gi]
                p.op("scalar", lambda e, o=pr, i=sc, b=bias:
                     e.activation(o.ap, i.ap, AF.Exp, bias=b.ap, scale=0.125),
                     reads=[sc, bias], writes=[pr], tag="sc_exp")
                mk = masks[:, 256:512] if (tg == 0 and cl == 0) \
                    else masks[:, 0:256]
                p.op("gpsimd", lambda e, o=pr, m=mk:
                     e.tensor_tensor(o.ap, o.ap, m.ap, ALU.mult),
                     reads=[pr, mk], writes=[pr], tag="sc_mask")
                s_ = acol[:, gi * 4 + 2:gi * 4 + 3]
                p.op("vector", lambda e, o=s_, i=pr:
                     e.reduce_sum(o.ap, i.ap, AX.X),
                     reads=[pr], writes=[s_], tag="sc_sum")
                r_ = acol[:, gi * 4 + 3:gi * 4 + 4]
                p.op("vector", lambda e, o=r_, i=s_: e.reciprocal(o.ap, i.ap),
                     reads=[s_], writes=[r_], tag="sc_rec")
                aT = attT[gi]
                skip_prev = (tg == 0 and cl == 0)
                pb = ctx.bankbf()
                halves = [1] if skip_prev else [0, 1]
                for hf in halves:
                    dst_ps = pb.tv[:, hf * 128:(hf + 1) * 128]
                    srcp = pr[:, hf * 128:(hf + 1) * 128]
                    transpose128(ctx, dst_ps, srcp)
                    dtv = aT[:, hf * 128:(hf + 1) * 128]
                    p.op("vector", lambda e, o=dtv, i=dst_ps:
                         e.tensor_copy(o.ap, i.ap),
                         reads=[dst_ps], writes=[dtv], tag="ev_aT")
                po = ctx.banks_f32[5]
                ov = po.tv[:, 0:64]
                mms = []
                for hf in halves:
                    lt = aT[:, hf * 128:(hf + 1) * 128]
                    rt = vb[:, cl * 128 + hf * 128:cl * 128 + hf * 128 + 128][:, h0:h0 + 64]
                    mms.append((lt, rt))
                matmul_acc(ctx, ov, mms, tag="mm_pv")
                o = ao_tm[:, cl * 1024 + hp * 128 + h0:cl * 1024 + hp * 128 + h0 + 64]
                p.op("vector", lambda e, o=o, i=ov, r=r_:
                     e.tensor_scalar(o.ap, i.ap, r.ap, None, ALU.mult),
                     reads=[ov, r_], writes=[o], tag="ev_o")

            for ci in range(DEPTH):
                stage_A(ci)
            for ci in range(8):
                stage_BC(ci)
                if ci + DEPTH < 8:
                    stage_A(ci + DEPTH)

        prev = None
        for hp in range(8):
            cur = emit_qkv(hp)
            if prev is not None:
                emit_chains(hp - 1, *prev)
            prev = cur
        emit_chains(7, *prev)

        for cl in range(4):
            for dt_ in range(DT):
                pb = ctx.bankbf()
                dst_ps = pb.tv[:, 0:128]
                src = ao_tm[:, cl * 1024 + dt_ * 128:cl * 1024 + (dt_ + 1) * 128]
                transpose128(ctx, dst_ps, src)
                dtv = ao_fm[:, dt_ * 512 + cl * 128:dt_ * 512 + (cl + 1) * 128]
                p.op("vector", lambda e, o=dtv, i=dst_ps: e.tensor_copy(o.ap, i.ap),
                     reads=[dst_ps], writes=[dtv], tag="ev_aofm")
        for mt in range(DT):
            wop = wos.load(ctx.drams["l0_wo"][mt])
            ps = ctx.bank()
            pv = ps.tv[:, 0:512]
            pairs = [(wop[:, kt * 128:(kt + 1) * 128],
                      ao_fm[:, kt * 512:kt * 512 + 512]) for kt in range(DT)]
            matmul_acc(ctx, pv, pairs, tag="mm_wo")
            xi = xinner[mt][:, t0:t0 + 512]
            p.op("vector", lambda e, o=xi, i=pv:
                 e.tensor_tensor(o.ap, o.ap, i.ap, ALU.add),
                 reads=[pv, xi], writes=[xi], tag="res_attn")
    wqkv.release()
    wos.release()
    for nm in ["at_q0", "at_q1", "at_k0", "at_k1", "at_v0", "at_v1",
               "at_pr0", "at_pr1", "at_pr2", "at_pr3",
               "at_aT0", "at_aT1", "at_aT2", "at_aT3", "at_ao", "at_af"]:
        ctx.ab.free(nm)
    ctx.af.free("at_acol")
    for i in range(DT):
        ctx.ab.free(f"xnb{i}")


def emit_l0_glu(ctx, st):
    p = ctx.p
    xres, xinner = st["xres"], st["xinner"]
    xnb = [ctx.ab.alloc(f"xnb{i}", T) for i in range(DT)]
    emit_norm(ctx, xinner, xnb, "rms", gamma="w_rms_ffn", eps=1e-6)

    gbuf = ctx.ab.alloc("glu_g", 512)
    act = [ctx.ab.alloc(f"glu_a{i}", T) for i in range(8)]
    ws = WStream(ctx, "glu_w", None, 32, 2 * D, nslots=2)
    wd_s = WStream(ctx, "glu_wd", None, 8, D, nslots=2)

    for g in range(4):
        for sub in range(8):
            mt = g * 8 + sub
            wgu = ws.load(ctx.drams["l0_gu"][mt])
            wgp, wup = wgu[:, 0:D], wgu[:, D:2 * D]
            for n in range(NT):
                ps = ctx.bank()
                pv = ps.tv[:, 0:512]
                pairs = [(wgp[:, kt * 128:(kt + 1) * 128],
                          xnb[kt][:, n * 512:(n + 1) * 512]) for kt in range(DT)]
                matmul_acc(ctx, pv, pairs, tag="mm_gate")
                o = gbuf[:, 0:512]
                p.op("scalar", lambda e, o=o, i=pv:
                     e.activation(o.ap, i.ap, AF.Silu),
                     reads=[pv], writes=[o], tag="ev_silu")
                ps2 = ctx.bank()
                pv2 = ps2.tv[:, 0:512]
                pairs = [(wup[:, kt * 128:(kt + 1) * 128],
                          xnb[kt][:, n * 512:(n + 1) * 512]) for kt in range(DT)]
                matmul_acc(ctx, pv2, pairs, tag="mm_up")
                oa = act[sub][:, n * 512:(n + 1) * 512]
                p.op("vector", lambda e, o=oa, i=pv2, gg=o:
                     e.tensor_tensor(o.ap, i.ap, gg.ap, ALU.mult),
                     reads=[pv2, o], writes=[oa], tag="ev_act")
        for mt in range(DT):
            wop = wd_s.load(ctx.drams[f"l0_wout{g}"][mt])
            for n in range(NT):
                ps = ctx.bank()
                pv = ps.tv[:, 0:512]
                pairs = [(wop[:, kt * 128:(kt + 1) * 128],
                          act[kt][:, n * 512:(n + 1) * 512]) for kt in range(DT)]
                matmul_acc(ctx, pv, pairs, tag="mm_down")
                xi = xinner[mt][:, n * 512:(n + 1) * 512]
                p.op("vector", lambda e, o=xi, i=pv:
                     e.tensor_tensor(o.ap, o.ap, i.ap, ALU.add),
                     reads=[pv, xi], writes=[xi], tag="res_glu")
    ws.release()
    wd_s.release()
    ctx.ab.free("glu_g")
    for i in range(8):
        ctx.ab.free(f"glu_a{i}")
    for i in range(DT):
        ctx.ab.free(f"xnb{i}")
    for i in range(DT):
        p.op("vector", lambda e, o=xres[i], b=xinner[i]:
             e.tensor_tensor(o.ap, o.ap, b.ap, ALU.add),
             reads=[xres[i], xinner[i]], writes=[xres[i]], tag="res_mod")
    for i in range(DT):
        ctx.ab.free(f"xi{i}")
    del st["xinner"]


def emit_l0_ffn(ctx, st):
    p = ctx.p
    xres = st["xres"]
    xnb = [ctx.ab.alloc(f"xnb{i}", T) for i in range(DT)]
    emit_norm(ctx, xres, xnb, "ln", eps=1e-5)

    fslot = [ctx.af.alloc("fw0", D)]
    accc = ctx.af.alloc("q_acc", 8)
    onef = ctx.of32
    p.op("vector", lambda e, o=accc[:, 0:1]: e.memset(o.ap, 0.0),
         writes=[accc[:, 0:1]], tag="q_z")
    for mt in range(DT):
        sl = fslot[0]
        src = ctx.drams["l0f_w"][mt]
        p.op("sync", lambda e, d=sl[:, 0:512], s=src:
             e.dma_start(out=d.ap, in_=s[:, 0:512]),
             writes=[sl[:, 0:512]], dma="l0fw0", tag="qw_load")
        p.op("scalar", lambda e, d=sl[:, 512:1024], s=src:
             e.dma_start(out=d.ap, in_=s[:, 512:1024]),
             writes=[sl[:, 512:1024]], dma="l0fw1", tag="qw_load")
        c = accc[:, 3:4]
        p.op("vector", lambda e, o=c, i=sl:
             e.tensor_reduce(o.ap, i.ap, AX.X, ALU.add, apply_absolute_value=True),
             reads=[sl], writes=[c], tag="q_abs")
        p.op("vector", lambda e, o=accc[:, 0:1], i=c:
             e.tensor_tensor(o.ap, o.ap, i.ap, ALU.add),
             reads=[accc[:, 0:1], c], writes=[accc[:, 0:1]], tag="q_accadd")
    ps = ctx.bank()
    tot = ps.tv[0:1, 0:1]
    p.op("tensor", lambda e, o=tot, a=onef[:, 0:1], b=accc[:, 0:1]:
         e.matmul(o.ap, a.ap, b.ap, start=True, stop=True),
         reads=[onef[:, 0:1], accc[:, 0:1]], writes=[tot], tag="q_tot")
    sc1 = ctx.rowf.tv[0:1, 0:1]
    p.op("vector", lambda e, o=sc1, i=tot:
         e.tensor_scalar(o.ap, i.ap, 1.0 / (D * D), 1e-5, ALU.mult, ALU.max),
         reads=[tot], writes=[sc1], tag="q_scale")
    ps2 = ctx.bank()
    p.op("tensor", lambda e, o=ps2.tv[:, 0:1], a=onef[0:1, 0:128], b=sc1:
         e.matmul(o.ap, a.ap, b.ap, start=True, stop=True),
         reads=[onef[0:1, 0:128], sc1], writes=[ps2.tv[:, 0:1]], tag="q_bs")
    p.op("vector", lambda e, o=accc[:, 2:3], i=ps2.tv[:, 0:1]:
         e.tensor_copy(o.ap, i.ap),
         reads=[ps2.tv[:, 0:1]], writes=[accc[:, 2:3]], tag="q_scol")
    p.op("vector", lambda e, o=accc[:, 1:2], i=accc[:, 2:3]:
         e.tensor_scalar(o.ap, i.ap, 0.5, None, ALU.mult),
         reads=[accc[:, 2:3]], writes=[accc[:, 1:2]], tag="q_thr")
    wq = ctx.ab.alloc("q_wq", DT * D)
    t_abs = ctx.af.alloc("q_tabs", D)
    t_sgn = ctx.ab.alloc("q_tsgn", D)
    for mt in range(DT):
        sl = fslot[0]
        src = ctx.drams["l0f_w"][mt]
        p.op("sync", lambda e, d=sl[:, 0:512], s=src:
             e.dma_start(out=d.ap, in_=s[:, 0:512]),
             writes=[sl[:, 0:512]], dma="l0fw0", tag="qw_load2")
        p.op("scalar", lambda e, d=sl[:, 512:1024], s=src:
             e.dma_start(out=d.ap, in_=s[:, 512:1024]),
             writes=[sl[:, 512:1024]], dma="l0fw1", tag="qw_load2")
        p.op("scalar", lambda e, o=t_abs, i=sl: e.activation(o.ap, i.ap, AF.Abs),
             reads=[sl], writes=[t_abs], tag="q_abs2")
        p.op("scalar", lambda e, o=t_sgn, i=sl: e.activation(o.ap, i.ap, AF.Sign),
             reads=[sl], writes=[t_sgn], tag="q_sgn")
        p.op("vector", lambda e, o=t_abs, i=t_abs, th=accc[:, 1:2]:
             e.tensor_scalar(o.ap, i.ap, th.ap, None, ALU.is_ge),
             reads=[t_abs, accc[:, 1:2]], writes=[t_abs], tag="q_ge")
        o = wq[:, mt * D:(mt + 1) * D]
        p.op("vector", lambda e, o=o, a=t_abs, b=t_sgn:
             e.tensor_tensor(o.ap, a.ap, b.ap, ALU.mult),
             reads=[t_abs, t_sgn], writes=[o], tag="q_mul")
    for mt in range(DT):
        for n in range(NT):
            ps = ctx.bank()
            pv = ps.tv[:, 0:512]
            pairs = [(wq[:, mt * D + kt * 128:mt * D + (kt + 1) * 128],
                      xnb[kt][:, n * 512:(n + 1) * 512]) for kt in range(DT)]
            matmul_acc(ctx, pv, pairs, tag="mm_tern")
            xr = xres[mt][:, n * 512:(n + 1) * 512]
            p.op("vector", lambda e, o=xr, i=pv, s=accc[:, 2:3]:
                 e.scalar_tensor_tensor(o.ap, i.ap, s.ap, o.ap, ALU.mult, ALU.add),
                 reads=[pv, accc[:, 2:3], xr], writes=[xr], tag="res_tern")
    ctx.ab.free("q_wq")
    ctx.af.free("q_tabs")
    ctx.ab.free("q_tsgn")
    ctx.af.free("q_acc")
    ctx.af.free("fw0")
    for i in range(DT):
        ctx.ab.free(f"xnb{i}")


MAMBA_DEBUG = False


def dbg_dump(ctx, name, tv, parts, free):
    if not MAMBA_DEBUG:
        return
    t = ctx.p.nc.dram_tensor("dbg_" + name, [parts, free], F32,
                             kind="ExternalOutput").ap()
    ctx.p.op("gpsimd", lambda e, d=t, s=tv[0:parts, 0:free]:
             e.dma_start(out=d, in_=s.ap),
             reads=[tv[0:parts, 0:free]], dma="dbg", tag="dbg")


def emit_l1_mamba(ctx, st):
    p = ctx.p
    xres = st["xres"]
    xnb = [ctx.ab.alloc(f"xnb{i}", T) for i in range(DT)]
    emit_norm(ctx, xres, xnb, "ln", eps=1e-5)
    for i in range(DT):
        dst = ctx.drams["xspill"][i * 128:(i + 1) * 128, :]
        p.op("sync", lambda e, d=dst, s=xres[i]: e.dma_start(out=d, in_=s.ap),
             reads=[xres[i]], dma="xsp_st", tag="x_spill")
    for i in range(DT):
        ctx.af.free(f"xr{i}")
    del st["xres"]

    h1 = [ctx.ab.alloc(f"h1_{i}", T) for i in range(DT)]
    ws = WStream(ctx, "mam_w", None, 8, D, nslots=2)
    for mt in range(DT):
        wp = ws.load(ctx.drams["l1_fc1"][mt])
        for n in range(NT):
            ps = ctx.bank()
            pv = ps.tv[:, 0:512]
            pairs = [(wp[:, kt * 128:(kt + 1) * 128],
                      xnb[kt][:, n * 512:(n + 1) * 512]) for kt in range(DT)]
            matmul_acc(ctx, pv, pairs, tag="mm_fc1")
            o = h1[mt][:, n * 512:(n + 1) * 512]
            b = ctx.vcol("fc1_b", mt)
            p.op("vector", lambda e, o=o, i=pv, b=b:
                 e.tensor_scalar(o.ap, i.ap, b.ap, None, ALU.add),
                 reads=[pv, b], writes=[o], tag="ev_h1")
    for i in range(DT):
        ctx.ab.free(f"xnb{i}")

    xpd = ctx.ab.alloc("xpd", 16 * 64)
    xpb = ctx.ab.alloc("xpb", 16 * 16)
    xpc = ctx.ab.alloc("xpc", 16 * 16)
    dtpb = ctx.ab.alloc("dtpb", 16 * 128)
    for kt in range(16):
        for nm, dst, w in (("xpd", xpd[:, kt * 64:(kt + 1) * 64], 64),
                           ("xpb", xpb[0:128, kt * 16:(kt + 1) * 16], 16),
                           ("xpc", xpc[0:128, kt * 16:(kt + 1) * 16], 16)):
            src = ctx.drams["l1_" + nm][kt]
            p.op("gpsimd", lambda e, d=dst, s=src: e.dma_start(out=d.ap, in_=s),
                 writes=[dst], dma=f"ld_{nm}", tag="xp_load")
        dst = dtpb[0:64, kt * 128:(kt + 1) * 128]
        src = ctx.drams["l1_dtp"][kt]
        p.op("gpsimd", lambda e, d=dst, s=src: e.dma_start(out=d.ap, in_=s),
             writes=[dst], dma="ld_dtp", tag="dtp_load")

    Abuf = ctx.af.alloc("Abuf", 256)
    for dtile in range(16):
        src = ctx.vcols("Alog", dtile * 16, (dtile + 1) * 16)
        dst = Abuf[:, dtile * 16:(dtile + 1) * 16]
        p.op("scalar", lambda e, o=dst, i=src: e.activation(o.ap, i.ap, AF.Exp),
             reads=[src], writes=[dst], tag="A_exp")
        p.op("vector", lambda e, o=dst, i=dst:
             e.tensor_scalar(o.ap, i.ap, -1.0, None, ALU.mult),
             reads=[dst], writes=[dst], tag="A_neg")

    accd = ctx.af.alloc("accd", T)
    accb = ctx.af.alloc("accb", T)
    accc = ctx.af.alloc("accc", T)
    for nm, tv, pp in (("accd", accd, 64), ("accb", accb, 16), ("accc", accc, 16)):
        v = tv[0:pp, :]
        p.op("vector", lambda e, o=v: e.memset(o.ap, 0.0), writes=[v], tag="xdbl_z")
    xcpad = ctx.af.alloc("xcpad", T + 16)
    ctmp = ctx.af.alloc("ctmp", T)
    ug = [ctx.ab.alloc(f"ug{i}", T) for i in range(4)]

    def mm_stream(pv, wp, h1_src, n, tag):
        for kt in range(DT):
            lt = wp[:, kt * 128:(kt + 1) * 128]
            rt = h1_src(kt, n)
            p.op("tensor",
                 lambda e, o=pv, a=lt, b=rt, s=(kt == 0), t=(kt == DT - 1):
                     e.matmul(o.ap, a.ap, b.ap, start=s, stop=t),
                 reads=[lt, rt], writes=[pv], tag=tag)

    def make_u(dtile, udst, h1_src):
        wp = ws.load(ctx.drams["l1_inproj"][dtile])
        z3 = xcpad[:, 0:3]
        p.op("vector", lambda e, o=z3: e.memset(o.ap, 0.0), writes=[z3], tag="xc_z")
        for n in range(NT):
            ps = ctx.bank()
            pv = ps.tv[:, 0:512]
            mm_stream(pv, wp, h1_src, n, "mm_xc")
            o = xcpad[:, 3 + n * 512:3 + (n + 1) * 512]
            p.op("vector", lambda e, o=o, i=pv: e.tensor_copy(o.ap, i.ap),
                 reads=[pv], writes=[o], tag="ev_xc")
        c0 = ctx.vcol("cw0", dtile)
        p.op("vector", lambda e, o=ctmp, i=xcpad[:, 0:T], s=c0:
             e.tensor_scalar(o.ap, i.ap, s.ap, None, ALU.mult),
             reads=[xcpad[:, 0:T], c0], writes=[ctmp], tag="conv0")
        for w in range(1, 4):
            cw = ctx.vcol(f"cw{w}", dtile)
            sl = xcpad[:, w:w + T]
            p.op("vector", lambda e, o=ctmp, i=sl, s=cw, a=ctmp:
                 e.scalar_tensor_tensor(o.ap, i.ap, s.ap, a.ap, ALU.mult, ALU.add),
                 reads=[sl, cw, ctmp], writes=[ctmp], tag=f"conv{w}")
        cb = ctx.vcol("conv_b", dtile)
        p.op("scalar", lambda e, o=udst, i=ctmp, b=cb:
             e.activation(o.ap, i.ap, AF.Silu, bias=b.ap),
             reads=[ctmp, cb], writes=[udst], tag="u_silu")

    h1_res = lambda kt, n: h1[kt][:, n * 512:(n + 1) * 512]
    for g4 in range(4):
        for j in range(4):
            make_u(g4 * 4 + j, ug[j], h1_res)
        for (acc, pp, wsl, msz) in ((accd, 64, xpd, 64), (accb, 16, xpb, 16),
                                    (accc, 16, xpc, 16)):
            for n in range(NT):
                ps = ctx.bank()
                pv = ps.tv[0:pp, 0:512]
                pairs = [(wsl[0:128, (g4 * 4 + j) * msz:(g4 * 4 + j + 1) * msz],
                          ug[j][:, n * 512:(n + 1) * 512]) for j in range(4)]
                matmul_acc(ctx, pv, pairs, tag="mm_xdbl")
                o = acc[0:pp, n * 512:(n + 1) * 512]
                p.op("vector", lambda e, o=o, i=pv:
                     e.tensor_tensor(o.ap, o.ap, i.ap, ALU.add),
                     reads=[pv, o], writes=[o], tag="xdbl_add")
    for i in range(4):
        ctx.ab.free(f"ug{i}")

    dbg_dump(ctx, "accd", accd, 64, T)
    dbg_dump(ctx, "accb", accb, 16, T)
    dbg_dump(ctx, "accc", accc, 16, T)
    dtlow = ctx.ab.alloc("dtlow", T)
    Bbf = ctx.ab.alloc("Bbf", T)[0:16, :]
    Cbf_full = ctx.ab.alloc("Cbf", T)[0:16, :]
    for (src, d, pp) in ((accd[0:64, :], dtlow[0:64, :], 64),
                         (accb[0:16, :], Bbf, 16),
                         (accc[0:16, :], Cbf_full, 16)):
        s = src
        p.op("vector", lambda e, o=d, i=s: e.tensor_copy(o.ap, i.ap),
             reads=[s], writes=[d], tag="xdbl_cast")
    ctx.af.free("accd")
    ctx.af.free("accb")
    ctx.af.free("accc")

    u2 = [ctx.ab.alloc(f"u{i}", T) for i in range(2)]
    zs2 = [ctx.ab.alloc(f"zs{i}", T) for i in range(2)]
    X2 = [ctx.ab.alloc(f"X{i}", T) for i in range(2)]
    hb2 = [ctx.ab.alloc("hb0", T)] * 2
    Bbs2 = [ctx.ab.alloc(f"Bbs{i}", T) for i in range(2)]
    Cbs2 = [ctx.ab.alloc(f"Cbs{i}", T) for i in range(2)]
    dtu2 = [ctx.ab.alloc(f"dtu{i}", T) for i in range(2)]
    dtt2 = [ctx.af.alloc(f"dtt{i}", T) for i in range(2)]
    ysa2 = [ctx.af.alloc(f"ysa{i}", T) for i in range(2)]
    dA2 = [ctx.af.alloc(f"dA{i}", T) for i in range(2)]
    sp = [ctx.af.alloc(f"sp{i}", 512) for i in range(3)]

    def pre_tile(dtile, sub):
        u_t, zs, dtt, dtu, ysa = (u2[sub], zs2[sub], dtt2[sub], dtu2[sub],
                                  ysa2[sub])
        make_u(dtile, u_t, h1_res)
        wp = ws.load(ctx.drams["l1_inproj"][16 + dtile])
        for n in range(NT):
            ps = ctx.bank()
            pv = ps.tv[:, 0:512]
            mm_stream(pv, wp, h1_res, n, "mm_z")
            o = zs[:, n * 512:(n + 1) * 512]
            p.op("scalar", lambda e, o=o, i=pv: e.activation(o.ap, i.ap, AF.Silu),
                 reads=[pv], writes=[o], tag="ev_zs")
        bdt = ctx.vcol("dt_b", dtile)
        for n in range(NT):
            ps = ctx.bank()
            pv = ps.tv[:, 0:512]
            lt = dtpb[0:64, dtile * 128:(dtile + 1) * 128]
            rt = dtlow[0:64, n * 512:(n + 1) * 512]
            matmul_acc(ctx, pv, [(lt, rt)], tag="mm_dt")
            s0, s1, s2 = sp
            p.op("scalar", lambda e, o=s0, i=pv, b=bdt:
                 e.activation(o.ap, i.ap, AF.Abs, bias=b.ap),
                 reads=[pv, bdt], writes=[s0], tag="sp_abs")
            p.op("scalar", lambda e, o=s1, i=s0:
                 e.activation(o.ap, i.ap, AF.Exp, scale=-1.0),
                 reads=[s0], writes=[s1], tag="sp_exp")
            p.op("scalar", lambda e, o=s1, i=s1, b=ctx.vcol("one"):
                 e.activation(o.ap, i.ap, AF.Ln, bias=b.ap),
                 reads=[s1, ctx.vcol("one")], writes=[s1], tag="sp_ln")
            p.op("scalar", lambda e, o=s2, i=pv, b=bdt:
                 e.activation(o.ap, i.ap, AF.Relu, bias=b.ap),
                 reads=[pv, bdt], writes=[s2], tag="sp_relu")
            o = dtt[:, n * 512:(n + 1) * 512]
            p.op("vector", lambda e, o=o, a=s1, b=s2:
                 e.tensor_tensor(o.ap, a.ap, b.ap, ALU.add),
                 reads=[s1, s2], writes=[o], tag="sp_add")
        p.op("vector", lambda e, o=dtu, a=dtt, b=u_t:
             e.tensor_tensor(o.ap, a.ap, b.ap, ALU.mult),
             reads=[dtt, u_t], writes=[dtu], tag="dtu")
        p.op("vector", lambda e, o=ysa: e.memset(o.ap, 0.0), writes=[ysa],
             tag="ys_z")

    def post_tile(dtile, sub):
        u_t, zs, ysa = u2[sub], zs2[sub], ysa2[sub]
        Dc = ctx.vcol("Dvec", dtile)
        p.op("vector", lambda e, o=ysa, i=u_t, s=Dc, a=ysa:
             e.scalar_tensor_tensor(o.ap, i.ap, s.ap, a.ap, ALU.mult, ALU.add),
             reads=[u_t, Dc, ysa], writes=[ysa], tag="y_uD")
        yt = X2[sub]
        p.op("vector", lambda e, o=yt, a=ysa, b=zs:
             e.tensor_tensor(o.ap, a.ap, b.ap, ALU.mult),
             reads=[ysa, zs], writes=[yt], tag="y_z")
        dst = ctx.drams["ysp"][dtile * 128:(dtile + 1) * 128, :]
        p.op("sync", lambda e, d=dst, s=yt: e.dma_start(out=d, in_=s.ap),
             reads=[yt], dma=f"ysp{sub}", tag="y_spill")

    for dp in range(8):
        pre_tile(2 * dp, 0)
        pre_tile(2 * dp + 1, 1)
        for sidx in range(16):
            Bbs, Cbs = Bbs2[sidx % 2], Cbs2[sidx % 2]
            sel = ctx.sel16[0:16, sidx * 128:(sidx + 1) * 128]
            for n in range(NT):
                ps = ctx.bank()
                pv = ps.tv[:, 0:512]
                rt = Bbf[:, n * 512:(n + 1) * 512]
                matmul_acc(ctx, pv, [(sel, rt)], tag="mm_bb")
                o = Bbs[:, n * 512:(n + 1) * 512]
                p.op("scalar", lambda e, o=o, i=pv:
                     e.activation(o.ap, i.ap, AF.Identity),
                     reads=[pv], writes=[o], tag="ev_bb")
                ps2 = ctx.bank()
                pv2 = ps2.tv[:, 0:512]
                rt2 = Cbf_full[:, n * 512:(n + 1) * 512]
                matmul_acc(ctx, pv2, [(sel, rt2)], tag="mm_cb")
                o2 = Cbs[:, n * 512:(n + 1) * 512]
                p.op("scalar", lambda e, o=o2, i=pv2:
                     e.activation(o.ap, i.ap, AF.Identity),
                     reads=[pv2], writes=[o2], tag="ev_cb")
            for sub in range(2):
                dtile = 2 * dp + sub
                dtt, dtu, ysa = dtt2[sub], dtu2[sub], ysa2[sub]
                dAb, Xb, hb = dA2[sub], X2[sub], hb2[sub]
                acol = Abuf[:, dtile * 16 + sidx:dtile * 16 + sidx + 1]
                p.op("scalar", lambda e, o=dAb, i=dtt, s=acol:
                     e.activation(o.ap, i.ap, AF.Exp, scale=s.ap),
                     reads=[dtt, acol], writes=[dAb], tag="dA")
                p.op("vector", lambda e, o=Xb, a=dtu, b=Bbs:
                     e.tensor_tensor(o.ap, a.ap, b.ap, ALU.mult),
                     reads=[dtu, Bbs], writes=[Xb], tag="X_mul")
                p.op("vector", lambda e, o=hb, a=dAb, x=Xb:
                     e.tensor_tensor_scan(o.ap, a.ap, x.ap, 0.0, ALU.mult,
                                          ALU.add),
                     reads=[dAb, Xb], writes=[hb], tag="scan")
                p.op("vector", lambda e, o=Xb, a=hb, b=Cbs:
                     e.tensor_tensor(o.ap, a.ap, b.ap, ALU.mult),
                     reads=[hb, Cbs], writes=[Xb], tag="hC_mul")
                p.op("gpsimd", lambda e, o=ysa, a=ysa, b=Xb:
                     e.tensor_tensor(o.ap, a.ap, b.ap, ALU.add),
                     reads=[ysa, Xb], writes=[ysa], tag="ys_add")
        post_tile(2 * dp, 0)
        post_tile(2 * dp + 1, 1)

    for nm in ["zs0", "zs1", "u0", "u1", "X0", "X1", "hb0", "dtlow",
               "Bbf", "Cbf", "dtpb", "xpd", "xpb", "xpc", "Bbs0", "Bbs1",
               "Cbs0", "Cbs1", "dtu0", "dtu1"]:
        ctx.ab.free(nm)
    for nm in ["dtt0", "dtt1", "ysa0", "ysa1", "dA0", "dA1", "ctmp", "Abuf",
               "xcpad"]:
        ctx.af.free(nm)
    for i in range(3):
        ctx.af.free(f"sp{i}")
    for i in range(DT):
        ctx.ab.free(f"h1_{i}")

    opw = ctx.ab.alloc("opw", DT * 2048)
    for mt in range(DT):
        dst = opw[:, mt * 2048:(mt + 1) * 2048]
        srcw = ctx.drams["l1_outproj"][mt]
        p.op("gpsimd", lambda e, d=dst, s=srcw: e.dma_start(out=d.ap, in_=s),
             writes=[dst], dma=f"opw{mt}", tag="opw_load")
    ych = ctx.ab.alloc("ych", 16 * 512)
    oacc = [ctx.ab.alloc(f"oac{i}", T) for i in range(DT)]
    for n in range(NT):
        for kt in range(16):
            dst = ych[:, kt * 512:(kt + 1) * 512]
            srcy = ctx.drams["ysp"][kt * 128:(kt + 1) * 128,
                                    n * 512:(n + 1) * 512]
            p.op("sync", lambda e, d=dst, s=srcy: e.dma_start(out=d.ap, in_=s),
                 writes=[dst], dma=f"ych{kt}", tag="ych_load")
        for mt in range(DT):
            ps = ctx.bank()
            pv = ps.tv[:, 0:512]
            pairs = [(opw[:, mt * 2048 + kt * 128:mt * 2048 + (kt + 1) * 128],
                      ych[:, kt * 512:(kt + 1) * 512]) for kt in range(16)]
            matmul_acc(ctx, pv, pairs, tag="mm_op")
            o = oacc[mt][:, n * 512:(n + 1) * 512]
            p.op("vector", lambda e, o=o, i=pv: e.tensor_copy(o.ap, i.ap),
                 reads=[pv], writes=[o], tag="ev_op")
    ctx.ab.free("opw")
    ctx.ab.free("ych")

    dbg_dump(ctx, "oac0", oacc[0], 128, T)
    xres = [ctx.af.alloc(f"xr{i}", T) for i in range(DT)]
    st["xres"] = xres
    xsrc = ctx.drams["xspill"]
    for i in range(DT):
        src = xsrc[i * 128:(i + 1) * 128, :]
        p.op("sync", lambda e, d=xres[i], s=src: e.dma_start(out=d.ap, in_=s),
             writes=[xres[i]], dma=f"xrl{i}", tag="x_reload")
    for mt in range(DT):
        wp = ws.load(ctx.drams["l1_fc2"][mt])
        for n in range(NT):
            ps = ctx.bank()
            pv = ps.tv[:, 0:512]
            pairs = [(wp[:, kt * 128:(kt + 1) * 128],
                      oacc[kt][:, n * 512:(n + 1) * 512]) for kt in range(DT)]
            matmul_acc(ctx, pv, pairs, tag="mm_fc2")
            xr = xres[mt][:, n * 512:(n + 1) * 512]
            b2 = ctx.vcol("fc2_b", mt)
            p.op("vector", lambda e, o=xr, i=pv, s=b2:
                 e.scalar_tensor_tensor(o.ap, i.ap, s.ap, o.ap, ALU.add, ALU.add),
                 reads=[pv, b2, xr], writes=[xr], tag="res_fc2")
    for i in range(DT):
        ctx.ab.free(f"oac{i}")
    ws.release()


def emit_l1_moe(ctx, st):
    p = ctx.p
    xres = st["xres"]
    xnb = [ctx.ab.alloc(f"xnb{i}", T) for i in range(DT)]
    smalls = ctx.af.alloc("moe_sm", 64)
    mcols = smalls[:, 16:32]
    rcols = smalls[:, 32:48]
    emit_norm(ctx, xres, xnb, "ln", eps=1e-5, exact=True,
              keep_cols=(mcols, rcols))

    gwf = ctx.af.alloc("moe_gw", 8 * DT)
    for kt in range(DT):
        dst = gwf[:, kt * 8:(kt + 1) * 8]
        src = ctx.drams["moe_gate"][kt]
        p.op("sync", lambda e, d=dst, s=src: e.dma_start(out=d.ap, in_=s),
             writes=[dst], dma="moe_gw", tag="gw_load")
    ps = ctx.bank()
    cs_ps = ps.tv[0:1, 0:8]
    pairs = [(ctx.of32[:, 0:1], gwf[:, kt * 8:(kt + 1) * 8]) for kt in range(DT)]
    matmul_acc(ctx, cs_ps, pairs, tag="mm_cs")
    csrow = smalls[0:1, 56:64]
    p.op("vector", lambda e, o=csrow, i=cs_ps: e.tensor_copy(o.ap, i.ap),
         reads=[cs_ps], writes=[csrow], tag="ev_cs")
    csb_ps = ctx.bank()
    p.op("tensor", lambda e, o=csb_ps.tv[:, 0:8], a=ctx.of32[0:1, 0:128], b=csrow:
         e.matmul(o.ap, a.ap, b.ap, start=True, stop=True),
         reads=[ctx.of32[0:1, 0:128], csrow], writes=[csb_ps.tv[:, 0:8]],
         tag="mm_csb")
    csb = smalls[:, 0:8]
    p.op("vector", lambda e, o=csb, i=csb_ps.tv[:, 0:8]: e.tensor_copy(o.ap, i.ap),
         reads=[csb_ps.tv[:, 0:8]], writes=[csb], tag="ev_csb")
    mask_tm = ctx.af.alloc("moe_msk", 16)
    logit_t = ctx.af.alloc("moe_lg", 16)
    srow_b = ctx.rowb.tv[0:1, 1024:3072]
    mkb = ctx.ab.alloc("moe_mkb", 16)
    mask8 = ctx.ab.alloc("moe_mask8", T)
    for ti in range(16):
        psx = ctx.bank()
        raw = psx.tv[:, 0:8]
        pairs = [(xres[kt][:, ti * 128:(ti + 1) * 128],
                  gwf[:, kt * 8:(kt + 1) * 8]) for kt in range(DT)]
        matmul_acc(ctx, raw, pairs, tag="mm_lg")
        lg = logit_t[:, 0:8]
        mc = mcols[:, ti:ti + 1]
        rc = rcols[:, ti:ti + 1]
        t1 = mask_tm[:, 8:16]
        p.op("vector", lambda e, o=t1, i=csb, s=mc:
             e.tensor_scalar(o.ap, i.ap, s.ap, None, ALU.mult),
             reads=[csb, mc], writes=[t1], tag="lg_mcs")
        p.op("vector", lambda e, o=lg, a=raw, b=t1:
             e.tensor_tensor(o.ap, a.ap, b.ap, ALU.subtract),
             reads=[raw, t1], writes=[lg], tag="lg_sub")
        p.op("vector", lambda e, o=lg, i=lg, s=rc:
             e.tensor_scalar(o.ap, i.ap, s.ap, None, ALU.mult),
             reads=[lg, rc], writes=[lg], tag="lg_rs")
        nmax = smalls[:, 48:49]
        p.op("vector", lambda e, o=nmax, i=lg:
             e.reduce_max(o.ap, i.ap, AX.X, negate=True),
             reads=[lg], writes=[nmax], tag="lg_max")
        top = smalls[:, 49:50]
        p.op("vector", lambda e, o=top, i=nmax:
             e.tensor_scalar(o.ap, i.ap, -1.0, None, ALU.mult),
             reads=[nmax], writes=[top], tag="lg_top")
        mk = mask_tm[:, 0:8]
        p.op("vector", lambda e, o=mk, i=lg, s=top:
             e.tensor_scalar(o.ap, i.ap, s.ap, None, ALU.is_equal),
             reads=[lg, top], writes=[mk], tag="lg_eq")
        sc = smalls[:, 50:51]
        p.op("scalar", lambda e, o=sc, i=nmax:
             e.activation(o.ap, i.ap, AF.Sigmoid, scale=-1.0),
             reads=[nmax], writes=[sc], tag="lg_sig")
        psy = ctx.bank()
        pv = psy.tv[0:1, 0:128]
        p.op("tensor", lambda e, o=pv, a=sc, b=ctx.if32:
             e.matmul(o.ap, a.ap, b.ap, start=True, stop=True),
             reads=[sc, ctx.if32], writes=[pv], tag="mm_scT")
        o = srow_b[0:1, ti * 128:(ti + 1) * 128]
        p.op("vector", lambda e, o=o, i=pv: e.tensor_copy(o.ap, i.ap),
             reads=[pv], writes=[o], tag="ev_scT")
        mcb = mkb[:, 0:8]
        mk8 = mask_tm[:, 0:8]
        p.op("vector", lambda e, o=mcb, i=mk8: e.tensor_copy(o.ap, i.ap),
             reads=[mk8], writes=[mcb], tag="mk_cast")
        psz = ctx.bankbf()
        pv = psz.tv[0:8, 0:128]
        transpose128(ctx, pv, mcb)
        o = mask8[0:8, ti * 128:(ti + 1) * 128]
        p.op("vector", lambda e, o=o, i=pv: e.tensor_copy(o.ap, i.ap),
             reads=[pv], writes=[o], tag="ev_mkT")
    ctx.af.free("moe_msk")
    ctx.af.free("moe_lg")
    ctx.af.free("moe_gw")

    gbuf = ctx.ab.alloc("moe_gb", T)
    maskb = ctx.ab.alloc("moe_mb", T)
    act = [ctx.ab.alloc(f"moe_a{i}", T) for i in range(8)]
    ws = WStream(ctx, "moe_w", None, 8, 2 * D, nslots=2)

    def group4(wtile_fn, src_tiles, KT, evac_fn, tag):
        for n in range(NT):
            pv = ctx.bank().tv[:, 0:512]
            pairs = [(wtile_fn(kt), src_tiles[kt][:, n * 512:(n + 1) * 512])
                     for kt in range(KT)]
            matmul_acc(ctx, pv, pairs, tag=tag)
            evac_fn(n, pv)

    def expert(gu_dram, down_dram, src_tiles, masked):
        for g in range(2):
            for sub in range(8):
                mt = g * 8 + sub
                wgu = ws.load(gu_dram[mt])
                wp, wu = wgu[:, 0:D], wgu[:, D:2 * D]

                def ev_g(n, pv):
                    o = gbuf[:, n * 512:(n + 1) * 512]
                    p.op("scalar", lambda e, o=o, i=pv:
                         e.activation(o.ap, i.ap, AF.Silu),
                         reads=[pv], writes=[o], tag="ev_esilu")
                    if masked:
                        mb = maskb[:, n * 512:(n + 1) * 512]
                        p.op("vector", lambda e, o=o, a=o, b=mb:
                             e.tensor_tensor(o.ap, a.ap, b.ap, ALU.mult),
                             reads=[o, mb], writes=[o], tag="gm_mask")
                group4(lambda kt: wp[:, kt * 128:(kt + 1) * 128], src_tiles,
                       DT, ev_g, "mm_eg")

                def ev_u(n, pv):
                    oa = act[sub][:, n * 512:(n + 1) * 512]
                    o = gbuf[:, n * 512:(n + 1) * 512]
                    p.op("vector", lambda e, o=oa, a=pv, b=o:
                         e.tensor_tensor(o.ap, a.ap, b.ap, ALU.mult),
                         reads=[pv, o], writes=[oa], tag="ev_eact")
                group4(lambda kt: wu[:, kt * 128:(kt + 1) * 128], src_tiles,
                       DT, ev_u, "mm_eu")
            for mt in range(DT):
                wp = ws.load(down_dram[g][mt])

                def ev_d(n, pv, mt=mt):
                    xr = xres[mt][:, n * 512:(n + 1) * 512]
                    p.op("vector", lambda e, o=xr, i=pv:
                         e.tensor_tensor(o.ap, o.ap, i.ap, ALU.add),
                         reads=[pv, xr], writes=[xr], tag="res_moe")
                group4(lambda kt: wp[:, kt * 128:(kt + 1) * 128], act,
                       DT, ev_d, "mm_ed")

    expert(ctx.drams["moe_shgu"],
           [ctx.drams["moe_shd"][g] for g in range(2)], xnb, masked=False)
    xs = xnb
    for n in range(NT):
        psb_ = ctx.bank()
        pv = psb_.tv[:, 0:512]
        row = srow_b[0:1, n * 512:(n + 1) * 512]
        bcast_row(ctx, row, pv)
        for dt_ in range(DT):
            o = xs[dt_][:, n * 512:(n + 1) * 512]
            p.op("vector", lambda e, o=o, b=pv:
                 e.tensor_tensor(o.ap, o.ap, b.ap, ALU.mult),
                 reads=[o, pv], writes=[o], tag="xs_mul")
    for ei in range(8):
        sel = ctx.sel16[0:8, ei * 128:(ei + 1) * 128]
        for n in range(NT):
            psx = ctx.bank()
            pv = psx.tv[:, 0:512]
            rt = mask8[0:8, n * 512:(n + 1) * 512]
            matmul_acc(ctx, pv, [(sel, rt)], tag="mm_mb")
            o = maskb[:, n * 512:(n + 1) * 512]
            p.op("vector", lambda e, o=o, i=pv: e.tensor_copy(o.ap, i.ap),
                 reads=[pv], writes=[o], tag="ev_mb")
        expert(ctx.drams["moe_rtgu"][ei],
               [ctx.drams["moe_rtd"][ei][g] for g in range(2)], xs, masked=True)
    ws.release()
    for i in range(DT):
        ctx.ab.free(f"xnb{i}")
    for i in range(8):
        ctx.ab.free(f"moe_a{i}")
    for nm in ["moe_gb", "moe_mb", "moe_mkb", "moe_mask8"]:
        ctx.ab.free(nm)
    ctx.af.free("moe_sm")


def build_core_program(phases, host):
    nc = bass.Bass("TRN2", target_bir_lowering=False)
    p = Prog(nc)
    ctx = Ctx(p)

    def din(name, arr):
        t = nc.dram_tensor(name, list(arr.shape), F32, kind="ExternalInput").ap()
        ctx.drams[name] = t
        return t

    out_names = []

    def dout(name, shape):
        t = nc.dram_tensor(name, list(shape), F32, kind="ExternalOutput").ap()
        ctx.drams[name] = t
        out_names.append(name)
        return t

    wnames = set()
    for ph in phases:
        wnames.update(PHASE_WEIGHTS[ph])
    for name in sorted(wnames):
        din(name, host["dram_inputs"][name])
    din("xfm", np.zeros((D, T), np.float32))
    din("vp", host["vecpack"])

    need_xi_in = ("l0_attn" in phases or "l0_glu" in phases) \
        and "l0_mix" not in phases
    need_xi_out = ("l0_mix" in phases or "l0_attn" in phases) \
        and "l0_glu" not in phases
    need_x_out = "l0_glu" in phases or "l0_ffn" in phases \
        or "l1_mamba" in phases or "l1_moe" in phases
    if need_xi_in:
        din("xi_in", np.zeros((D, T), np.float32))
    if need_xi_out:
        dout("xi_out", (D, T))
    if need_x_out:
        dout("x_out", (D, T))

    AF_EL = max(PHASE_ARENA[ph][0] for ph in phases)
    AB_EL = max(PHASE_ARENA[ph][1] for ph in phases)
    af_buf = p.sbuf("arena_f32", 128, AF_EL, F32)
    ab_buf = p.sbuf("arena_bf16", 128, AB_EL, BF16)
    ctx.af = Arena(af_buf)
    ctx.ab = Arena(ab_buf)
    ctx.banks_f32 = [p.psum(f"pf{i}", 128, 512, F32) for i in range(6)]
    ctx.banks_bf = [p.psum(f"pb{i}", 128, 1024, BF16) for i in range(2)]
    vp_arr = host["vecpack"]
    ctx.vp = p.sbuf("vecpack", 128, vp_arr.shape[1], F32)
    ctx.vp_index = host["vp_index"]
    ctx.rowb = p.sbuf("rowb", 1, 3072, BF16)
    ctx.rowf = p.sbuf("rowf", 1, 2048, F32)
    if "cmask" in wnames:
        ctx.masks = p.sbuf("c_masks", 128, 512, BF16).tv
    if "csel" in wnames:
        ctx.sel16 = p.sbuf("c_sel16", 16, 2048, BF16).tv
    if "l1_fc1" in wnames:
        ctx.drams["xspill"] = nc.dram_tensor("xspill", [D, T], F32,
                                             kind="Internal").ap()
        ctx.drams["ysp"] = nc.dram_tensor("ysp", [2048, T], BF16,
                                          kind="Internal").ap()

    with nc.Block() as block:
        p.op("sync", lambda e: e.dma_start(out=ctx.vp.tv.ap, in_=ctx.drams["vp"]),
             writes=[ctx.vp.tv], dma="vp")
        emit_consts(ctx)
        if ctx.masks is not None:
            p.op("gpsimd", lambda e: e.dma_start(out=ctx.masks.ap,
                                                 in_=ctx.drams["cmask"]),
                 writes=[ctx.masks], dma="cmask")
        if ctx.sel16 is not None:
            p.op("gpsimd", lambda e: e.dma_start(out=ctx.sel16.ap,
                                                 in_=ctx.drams["csel"]),
                 writes=[ctx.sel16], dma="csel")

        st = {}
        xres = [ctx.af.alloc(f"xr{i}", T) for i in range(DT)]
        st["xres"] = xres
        for i in range(DT):
            src = ctx.drams["xfm"][i * 128:(i + 1) * 128, :]
            p.op("sync", lambda e, d=xres[i], s=src: e.dma_start(out=d.ap, in_=s),
                 writes=[xres[i]], dma=f"xload{i}")
        if need_xi_in:
            xinner = [ctx.ab.alloc(f"xi{i}", T) for i in range(DT)]
            st["xinner"] = xinner
            for i in range(DT):
                src = ctx.drams["xi_in"][i * 128:(i + 1) * 128, :]
                p.op("gpsimd", lambda e, d=xinner[i], s=src: e.dma_start(out=d.ap, in_=s),
                     writes=[xinner[i]], dma=f"xiload{i}")

        for ph in phases:
            PHASE_EMITTERS[ph](ctx, st)

        final_tvs = []
        if need_xi_out:
            for i in range(DT):
                dst = ctx.drams["xi_out"][i * 128:(i + 1) * 128, :]
                tv = st["xinner"][i]
                p.op("gpsimd", lambda e, d=dst, s=tv: e.dma_start(out=d, in_=s.ap),
                     reads=[tv], dma="st_out")
                final_tvs.append(tv)
        if need_x_out:
            for i in range(DT):
                dst = ctx.drams["x_out"][i * 128:(i + 1) * 128, :]
                tv = st["xres"][i]
                p.op("sync", lambda e, d=dst, s=tv: e.dma_start(out=d, in_=s.ap),
                     reads=[tv], dma="st_out")
                final_tvs.append(tv)
        n_st = sum(1 for o in p.ops if o.dma_sem == "st_out")
        p.op("sync", lambda e, n=n_st: e.wait_ge(SEMS["st_out"], 16 * n),
             tag="final_wait")

        global SEMS
        SEMS = {}
        sem_cms = []
        names = list(ENGINES) + p.dma_sems_used()
        for nm in names:
            cm = nc.semaphore(f"s_{nm}")
            SEMS[nm] = cm.__enter__()
            sem_cms.append(cm)
        p.emit(block, SEMS)
        for cm in reversed(sem_cms):
            cm.__exit__(None, None, None)
    p.close_bufs()
    return nc, out_names


SEMS = {}
PHASE_EMITTERS = {"l0_mix": emit_l0_mix, "l0_attn": emit_l0_attn,
                  "l0_glu": emit_l0_glu, "l0_ffn": emit_l0_ffn,
                  "l1_mamba": emit_l1_mamba, "l1_moe": emit_l1_moe}
PHASE_ARENA = {
    "l0_mix": (19000, 54000),
    "l0_attn": (17500, 53000),
    "l0_glu": (17500, 56000),
    "l0_ffn": (19000, 50000),
    "l1_mamba": (18500, 55000),
    "l1_moe": (19000, 47000),
}
PHASE_WEIGHTS = {
    "l0_mix": ["l0_gvd"],
    "l0_attn": ["l0_qkv3", "l0_wo", "cmask"],
    "l0_glu": ["l0_gu", "l0_wout0", "l0_wout1", "l0_wout2", "l0_wout3"],
    "l0_ffn": ["l0f_w"],
    "l1_mamba": ["l1_fc1", "l1_inproj", "l1_xpd", "l1_xpb", "l1_xpc",
                 "l1_dtp", "l1_outproj", "l1_fc2", "csel"],
    "l1_moe": ["moe_gate", "moe_shgu", "moe_shd", "moe_rtgu", "moe_rtd",
               "csel"],
}


def prep_host(inp):
    vp = VecPack()
    vp.add("w_rms_mix", inp["L0_attn_w_rms_mix"])
    vp.add("bg", inp["L0_attn_bg"])
    vp.add("bv", inp["L0_attn_bv"])
    vp.add("bd", inp["L0_attn_bd"])
    vp.add("w_rms_attn", inp["L0_attn_w_rms_attn"])
    vp.add("w_rms_ffn", inp["L0_attn_w_rms_ffn"])
    vp.add("eps_1e5", np.full(128, 1e-5, np.float32))
    vp.add("eps_1e6", np.full(128, 1e-6, np.float32))
    vp.add("one", np.full(128, 1.0, np.float32))
    vp.add("fc1_b", inp["L1_attn_fc1_b"])
    for w in range(4):
        vp.add(f"cw{w}", inp["L1_attn_conv_w"][:, w])
    vp.add("conv_b", inp["L1_attn_conv_b"])
    vp.add("dt_b", inp["L1_attn_dt_proj_b"])
    vp.add("Dvec", inp["L1_attn_D"])
    vp.add("fc2_b", inp["L1_attn_fc2_b"])
    al = np.asarray(inp["L1_attn_A_log"], np.float32)
    vp.add_image("Alog", al.reshape(16, 128, 16).transpose(1, 0, 2).reshape(128, 256))

    j = np.arange(128)[:, None]
    kk = np.arange(256)[None, :]
    maskN = np.where(kk < 128, (kk > j), (kk - 128) <= j).astype(np.float32)
    mask0 = np.where(kk < 128, False, (kk - 128) <= j).astype(np.float32)
    cmask = np.concatenate([maskN, mask0], axis=1)

    wout = inp["L0_attn_Wout"]
    _g = lhsT_panels(inp["L0_attn_Wg"])
    _v = lhsT_panels(inp["L0_attn_Wv"])
    _d = lhsT_panels(inp["L0_attn_Wd"])
    _qkv = lhsT_panels(inp["L0_attn_Wqkv"])
    _ga = lhsT_panels(inp["L0_attn_Wgate"])
    _up = lhsT_panels(inp["L0_attn_Wup"])
    dram = {
        "l0_gvd": np.ascontiguousarray(np.concatenate([_g, _v, _d], axis=2)),
        "l0_qkv3": np.ascontiguousarray(np.concatenate(
            [_qkv[0:8], _qkv[8:16], _qkv[16:24]], axis=2)),
        "l0_wo": lhsT_panels(inp["L0_attn_Wo"]),
        "l0_gu": np.ascontiguousarray(np.concatenate([_ga, _up], axis=2)),
        "l0_wout0": lhsT_panels(wout[:, 0:1024]),
        "l0_wout1": lhsT_panels(wout[:, 1024:2048]),
        "l0_wout2": lhsT_panels(wout[:, 2048:3072]),
        "l0_wout3": lhsT_panels(wout[:, 3072:4096]),
        "l0f_w": lhsT_panels(inp["L0_ffn_weight"]),
        "cmask": np.ascontiguousarray(cmask),
    }
    xpw = np.asarray(inp["L1_attn_x_proj_w"], np.float32)
    dtpw = np.asarray(inp["L1_attn_dt_proj_w"], np.float32)
    sel = np.zeros((16, 16, 128), np.float32)
    for i in range(16):
        sel[i, i, :] = 1.0
    dram.update({
        "l1_fc1": lhsT_panels(inp["L1_attn_fc1_w"]),
        "l1_inproj": lhsT_panels(inp["L1_attn_in_proj_w"]),
        "l1_xpd": np.ascontiguousarray(
            xpw[0:64].reshape(64, 16, 128).transpose(1, 2, 0)),
        "l1_xpb": np.ascontiguousarray(
            xpw[64:80].reshape(16, 16, 128).transpose(1, 2, 0)),
        "l1_xpc": np.ascontiguousarray(
            xpw[80:96].reshape(16, 16, 128).transpose(1, 2, 0)),
        "l1_dtp": np.ascontiguousarray(
            dtpw.reshape(16, 128, 64).transpose(0, 2, 1)),
        "l1_outproj": lhsT_panels(inp["L1_attn_out_proj_w"]),
        "l1_fc2": lhsT_panels(inp["L1_attn_fc2_w"]),
        "csel": np.ascontiguousarray(sel.reshape(16, 2048)),
    })
    shd = np.asarray(inp["L1_ffn_sh_down"], np.float32).T
    dram.update({
        "moe_gate": np.ascontiguousarray(
            np.asarray(inp["L1_ffn_gate_w"], np.float32).reshape(8, 128, 8)),
        "moe_shgu": np.concatenate(
            [lhsT_panels(np.asarray(inp["L1_ffn_sh_gate"], np.float32).T),
             lhsT_panels(np.asarray(inp["L1_ffn_sh_up"], np.float32).T)], axis=2),
        "moe_shd": np.stack([lhsT_panels(shd[:, g * 1024:(g + 1) * 1024])
                             for g in range(2)]),
        "moe_rtgu": np.stack([np.concatenate(
            [lhsT_panels(np.asarray(inp["L1_ffn_rt_gate"][e], np.float32).T),
             lhsT_panels(np.asarray(inp["L1_ffn_rt_up"][e], np.float32).T)],
            axis=2) for e in range(8)]),
        "moe_rtd": np.stack([np.stack([
            lhsT_panels(np.asarray(inp["L1_ffn_rt_down"][e], np.float32)
                        .T[:, g * 1024:(g + 1) * 1024]) for g in range(2)])
            for e in range(8)]),
    })
    host = {
        "dram_inputs": dram,
        "vecpack": vp.array(),
        "vp_index": vp.index,
    }
    return host


def run_launch(phases, host, per_core_io):
    nc, out_names = build_core_program(phases, host)
    wnames = set()
    for ph in phases:
        wnames.update(PHASE_WEIGHTS[ph])
    in_maps = []
    for c in range(NCORES):
        m = {name: host["dram_inputs"][name] for name in wnames}
        m["vp"] = host["vecpack"]
        m.update(per_core_io[c])
        in_maps.append(m)
    res = run_bass_kernel_spmd(nc, in_maps, core_ids=list(range(NCORES)))
    return res.results, out_names


NCORES = 8

LAUNCHES = [ALL_PHASES]


def kernel(**inputs):
    host = prep_host(inputs)
    x = np.asarray(inputs["L0_attn_x"], np.float32)
    xfm = np.ascontiguousarray(x.transpose(0, 2, 1))
    cur_x = [xfm[c] for c in range(NCORES)]
    cur_xi = [None] * NCORES
    for phases in LAUNCHES:
        io = []
        for c in range(NCORES):
            m = {"xfm": cur_x[c]}
            if ("l0_attn" in phases or "l0_glu" in phases)                     and "l0_mix" not in phases:
                m["xi_in"] = cur_xi[c]
            io.append(m)
        results, outs = run_launch(phases, host, io)
        for c in range(NCORES):
            if "xi_out" in outs:
                cur_xi[c] = results[c]["xi_out"]
            if "x_out" in outs:
                cur_x[c] = results[c]["x_out"]
    out = np.stack(cur_x)
    return np.ascontiguousarray(out.transpose(0, 2, 1))


# version 30
# speedup vs baseline: 1.0663x; 1.0329x over previous
import numpy as np
import ml_dtypes
import concourse.bass as bass
import concourse.mybir as mybir
from concourse.bass_utils import run_bass_kernel_spmd

F32 = mybir.dt.float32
BF16 = mybir.dt.bfloat16
AF = mybir.ActivationFunctionType
ALU = mybir.AluOpType
AX = mybir.AxisListType

B, T, D = 8, 2048, 1024
DT = D // 128
NT = T // 512
ENGINES = ("sync", "vector", "scalar", "tensor", "gpsimd")

ALL_PHASES = ("l0_mix", "l0_attn", "l0_glu", "l0_ffn", "l1_mamba", "l1_moe")


class Buf:
    __slots__ = ("name", "handle", "parts", "free", "dtype", "space")

    def __init__(self, name, handle, parts, free, dtype, space):
        self.name, self.handle = name, handle
        self.parts, self.free, self.dtype, self.space = parts, free, dtype, space

    @property
    def tv(self):
        return TV(self, 0, self.parts, 0, self.free)

    def __getitem__(self, sl):
        return self.tv[sl]


class TV:
    __slots__ = ("buf", "p0", "p1", "f0", "f1")

    def __init__(self, buf, p0, p1, f0, f1):
        self.buf, self.p0, self.p1, self.f0, self.f1 = buf, p0, p1, f0, f1
        assert 0 <= p0 < p1 <= buf.parts and 0 <= f0 < f1 <= buf.free, \
            f"bad TV {buf.name}[{p0}:{p1},{f0}:{f1}] of [{buf.parts},{buf.free}]"

    @property
    def ap(self):
        return self.buf.handle[self.p0:self.p1, self.f0:self.f1]

    @property
    def P(self):
        return self.p1 - self.p0

    @property
    def F(self):
        return self.f1 - self.f0

    def __getitem__(self, sl):
        psl, fsl = sl if isinstance(sl, tuple) else (sl, slice(None))
        p0, p1, st = psl.indices(self.P)
        assert st == 1
        f0, f1, st = fsl.indices(self.F)
        assert st == 1
        return TV(self.buf, self.p0 + p0, self.p0 + p1, self.f0 + f0, self.f0 + f1)

    def overlaps(self, o):
        return (self.buf is o.buf and self.p0 < o.p1 and o.p0 < self.p1
                and self.f0 < o.f1 and o.f0 < self.f1)

    def covers(self, o):
        return (self.buf is o.buf and self.p0 <= o.p0 and self.p1 >= o.p1
                and self.f0 <= o.f0 and self.f1 >= o.f1)


class Op:
    __slots__ = ("i", "eng", "fn", "reads", "writes", "dma_sem", "need_inc", "tag")

    def __init__(self, i, eng, fn, reads, writes, dma_sem, tag):
        self.i, self.eng, self.fn = i, eng, fn
        self.reads, self.writes, self.dma_sem = reads, writes, dma_sem
        self.need_inc = False
        self.tag = tag


class Arena:

    def __init__(self, buf, align=16):
        self.buf = buf
        self.align = align
        self.free_list = [(0, buf.free)]
        self.live = {}

    def alloc(self, name, elems):
        a = self.align
        elems = -(-elems // a) * a
        for idx, (lo, hi) in enumerate(self.free_list):
            lo_a = -(-lo // a) * a
            if hi - lo_a >= elems:
                if lo_a > lo:
                    self.free_list[idx] = (lo, lo_a)
                    self.free_list.insert(idx + 1, (lo_a + elems, hi))
                else:
                    self.free_list[idx] = (lo_a + elems, hi)
                    if self.free_list[idx][0] >= self.free_list[idx][1]:
                        del self.free_list[idx]
                self.live[name] = (lo_a, lo_a + elems)
                return TV(self.buf, 0, self.buf.parts, lo_a, lo_a + elems)
        raise MemoryError(
            f"arena {self.buf.name}: cannot alloc {elems} for {name}; "
            f"free={self.free_list} live={self.live}")

    def free(self, name):
        lo, hi = self.live.pop(name)
        self.free_list.append((lo, hi))
        self.free_list.sort()
        merged = []
        for seg in self.free_list:
            if merged and merged[-1][1] >= seg[0]:
                merged[-1] = (merged[-1][0], max(merged[-1][1], seg[1]))
            else:
                merged.append(list(seg))
        self.free_list = [tuple(s) for s in merged]


class Prog:
    def __init__(self, nc):
        self.nc = nc
        self.ops = []
        self.deps = []
        self.records = {}
        self._ctxs = []

    def sbuf(self, name, parts, free, dtype):
        cm = self.nc.sbuf_tensor(name, [parts, free], dtype)
        h = cm.__enter__()
        self._ctxs.append(cm)
        return Buf(name, h, parts, free, dtype, "sbuf")

    def psum(self, name, parts, free, dtype):
        cm = self.nc.psum_tensor(name, [parts, free], dtype)
        h = cm.__enter__()
        self._ctxs.append(cm)
        return Buf(name, h, parts, free, dtype, "psum")

    def close_bufs(self):
        for cm in reversed(self._ctxs):
            cm.__exit__(None, None, None)
        self._ctxs = []

    def op(self, eng, fn, reads=(), writes=(), dma=None, tag=""):
        i = len(self.ops)
        reads = tuple(tv.buf.tv if tv.buf.space == "psum" else tv for tv in reads)
        writes = tuple(tv.buf.tv if tv.buf.space == "psum" else tv for tv in writes)
        o = Op(i, eng, fn, tuple(reads), tuple(writes), dma, tag)
        dep = set()
        for tv in o.reads:
            for (rtv, w, _rd) in self.records.get(tv.buf.name, ()):
                if w is not None and tv.overlaps(rtv):
                    dep.add(w)
        for tv in o.writes:
            for (rtv, w, rd) in self.records.get(tv.buf.name, ()):
                if tv.overlaps(rtv):
                    if w is not None:
                        dep.add(w)
                    dep.update(rd)
        for tv in o.reads:
            recs = self.records.setdefault(tv.buf.name, [])
            hit = False
            for r in recs:
                if r[0].covers(tv) and r[1] is not None:
                    r[2].append(i)
                    hit = True
            if not hit:
                recs.append([tv, None, [i]])
        for tv in o.writes:
            recs = self.records.setdefault(tv.buf.name, [])
            recs[:] = [r for r in recs if not tv.covers(r[0])]
            recs.append([tv, i, []])
        dep.discard(i)
        self.ops.append(o)
        self.deps.append(dep)
        return i

    def dep(self, a, b):
        self.deps[b].add(a)

    def barrier_read(self, tvs, eng="sync"):
        return self.op(eng, lambda e: e.nop(), reads=tvs, tag="barrier")

    def emit(self, block, sems):
        ops, deps = self.ops, self.deps
        for o in ops:
            for d in deps[o.i]:
                p = ops[d]
                if p.dma_sem is None and p.eng == "tensor" and o.eng == "tensor":
                    continue
                if p.dma_sem is None:
                    p.need_inc = True
        inc_count = {e: 0 for e in ENGINES}
        dma_count = {}
        prod_val = {}
        for o in ops:
            if o.dma_sem is not None:
                c = dma_count.get(o.dma_sem, 0) + 1
                dma_count[o.dma_sem] = c
                prod_val[o.i] = (o.dma_sem, 16 * c)
            elif o.need_inc:
                inc_count[o.eng] += 1
                prod_val[o.i] = (o.eng, inc_count[o.eng])
            else:
                prod_val[o.i] = None
        waits = []
        for o in ops:
            w = {}
            for d in deps[o.i]:
                p = ops[d]
                if p.dma_sem is None and p.eng == "tensor" and o.eng == "tensor":
                    continue
                s, v = prod_val[d]
                if w.get(s, -1) < v:
                    w[s] = v
            waits.append(w)
        self.n_incs = dict(inc_count)
        self.n_dma = dict(dma_count)

        def body_for(engine):
            def body(eng):
                hw = {}
                for o in ops:
                    if o.eng != engine:
                        continue
                    for s, v in sorted(waits[o.i].items()):
                        if hw.get(s, 0) >= v:
                            continue
                        eng.wait_ge(sems[s], v)
                        hw[s] = v
                    inst = o.fn(eng)
                    if o.dma_sem is not None:
                        inst.then_inc(sems[o.dma_sem], 16)
                    elif o.need_inc:
                        inst.then_inc(sems[o.eng], 1)
            return body

        for e in ENGINES:
            getattr(block, e)(body_for(e))

    def dma_sems_used(self):
        return sorted({o.dma_sem for o in self.ops if o.dma_sem is not None})


def lhsT_panels(w_out_in):
    O, I = w_out_in.shape
    assert O % 128 == 0 and I % 128 == 0, (O, I)
    MT, KT = O // 128, I // 128
    a = w_out_in.reshape(MT, 128, KT, 128).transpose(0, 3, 2, 1)
    return np.ascontiguousarray(a.reshape(MT, 128, KT * 128), dtype=np.float32)


def lhsT_panels_rows(w_out_in, row_lo, row_hi):
    w = w_out_in[row_lo:row_hi]
    M, I = w.shape
    KT = I // 128
    a = w.reshape(M, KT, 128).transpose(2, 1, 0)
    return np.ascontiguousarray(a.reshape(1, 128, KT * M), dtype=np.float32)


def rhs_panels(w_out_in):
    O, I = w_out_in.shape
    assert I % 128 == 0
    KT = I // 128
    return np.ascontiguousarray(w_out_in.T.reshape(KT, 128, O), dtype=np.float32)


def vec_image(v):
    v = np.asarray(v, dtype=np.float32).ravel()
    n = v.size // 128
    assert v.size == n * 128, v.size
    return np.ascontiguousarray(v.reshape(n, 128).T)


class VecPack:
    def __init__(self):
        self.cols = []
        self.index = {}
        self.n = 0

    def add(self, name, v):
        img = vec_image(v)
        self.index[name] = (self.n, img.shape[1])
        self.cols.append(img)
        self.n += img.shape[1]

    def add_image(self, name, img):
        img = np.ascontiguousarray(np.asarray(img, np.float32))
        assert img.shape[0] == 128
        self.index[name] = (self.n, img.shape[1])
        self.cols.append(img)
        self.n += img.shape[1]

    def add_scalar_slot(self, name):
        self.index[name] = (self.n, 1)
        self.cols.append(np.zeros((128, 1), np.float32))
        self.n += 1

    def array(self):
        return np.ascontiguousarray(np.concatenate(self.cols, axis=1))


class Ctx:
    def __init__(self, prog):
        self.p = prog
        self.af = None
        self.ab = None
        self.banks_f32 = []
        self.banks_bf = []
        self._rrf = 0
        self._rrb = 0
        self.vp = None
        self.of32 = None
        self.if32 = None
        self.sel16 = None
        self.masks = None
        self.vp_index = {}
        self.ones = None
        self.ident = None
        self.rowb = None
        self.rowf = None
        self.drams = {}

    def bank(self):
        b = self.banks_f32[self._rrf % len(self.banks_f32)]
        self._rrf += 1
        return b

    def bankbf(self):
        b = self.banks_bf[self._rrb % len(self.banks_bf)]
        self._rrb += 1
        return b

    def vcol(self, name, j=0):
        lo, n = self.vp_index[name]
        assert j < n, (name, j, n)
        return self.vp[:, lo + j:lo + j + 1]

    def vcols(self, name, j0, j1):
        lo, n = self.vp_index[name]
        assert j1 <= n
        return self.vp[:, lo + j0:lo + j1]


def emit_consts(ctx):
    p = ctx.p
    ones = p.sbuf("c_ones", 128, 128, BF16)
    ident = p.sbuf("c_ident", 128, 128, BF16)
    of32 = p.sbuf("c_of32", 128, 128, F32)
    ctx.ones, ctx.ident, ctx.of32 = ones, ident, of32.tv
    p.op("vector", lambda e: e.memset(ones.tv.ap, 1.0), writes=[ones.tv])
    p.op("vector", lambda e: e.memset(of32.tv.ap, 1.0), writes=[of32.tv])
    p.op("vector", lambda e: e.memset(ident.tv.ap, 1.0), writes=[ident.tv])
    p.op("gpsimd",
         lambda e: e.affine_select(ident.tv.ap, ident.tv.ap, [[1, 128]],
                                   ALU.is_equal, 0.0, base=0,
                                   channel_multiplier=-1),
         reads=[ident.tv], writes=[ident.tv])
    if32 = p.sbuf("c_if32", 128, 128, F32)
    ctx.if32 = if32.tv
    p.op("vector", lambda e: e.memset(if32.tv.ap, 1.0), writes=[if32.tv])
    p.op("gpsimd",
         lambda e: e.affine_select(if32.tv.ap, if32.tv.ap, [[1, 128]],
                                   ALU.is_equal, 0.0, base=0,
                                   channel_multiplier=-1),
         reads=[if32.tv], writes=[if32.tv])


class WStream:

    def __init__(self, ctx, name, dram_ap, n_panels, panel_free, nslots=4):
        self.ctx, self.name = ctx, name
        self.dram = dram_ap
        self.n_panels, self.pf = n_panels, panel_free
        self.slots = [ctx.ab.alloc(f"{name}_s{i}", panel_free) for i in range(nslots)]
        self.nslots = nslots
        self.count = 0

    def panel(self, i):
        return self.load(self.dram[i])

    def load(self, src_ap):
        ctx = self.ctx
        slot = self.slots[self.count % self.nslots]
        sem = f"w_{self.name}_{self.count % self.nslots}"
        self.count += 1
        shp = list(src_ap.shape)
        fw = int(np.prod(shp[1:])) if len(shp) > 1 else int(shp[0])
        assert fw <= slot.F, (self.name, fw, slot.F)
        dst = slot[:, 0:fw] if fw < slot.F else slot
        ctx.p.op("gpsimd", lambda e, d=dst, s=src_ap: e.dma_start(out=d.ap, in_=s),
                 writes=[dst], dma=sem, tag="wload")
        return dst

    def release(self):
        for i in range(self.nslots):
            self.ctx.ab.free(f"{self.name}_s{i}")


def matmul_acc(ctx, psum_tv, pairs, tag="mm"):
    p = ctx.p
    n = len(pairs)
    for k, (lt, rt) in enumerate(pairs):
        p.op("tensor",
             lambda e, o=psum_tv, a=lt, b=rt, s=(k == 0), t=(k == n - 1):
                 e.matmul(o.ap, a.ap, b.ap, start=s, stop=t),
             reads=[lt, rt], writes=[psum_tv], tag=tag)


def transpose128(ctx, psum_bf_tv, src_tv):
    ident = ctx.ident
    ctx.p.op("tensor",
             lambda e, o=psum_bf_tv, i=src_tv, idt=ident.tv:
                 e.transpose(o.ap, i.ap, idt.ap),
             reads=[src_tv, ident.tv], writes=[psum_bf_tv], tag="transp")


def bcast_row(ctx, row_tv, out_psum_tv, tag="bcast"):
    lhsT = ctx.ones[0:1, 0:out_psum_tv.P]
    ctx.p.op("tensor",
             lambda e, o=out_psum_tv, l=lhsT, r=row_tv:
                 e.matmul(o.ap, l.ap, r.ap, start=True, stop=True),
             reads=[row_tv, lhsT], writes=[out_psum_tv], tag=tag)


def emit_norm(ctx, src_tiles, out_tiles, kind, gamma=None, eps=1e-5,
              out2_tiles=None, eng_apply="vector", exact=False, keep_rows=None,
              keep_cols=None):
    p = ctx.p
    DTl = len(src_tiles)
    Dl = DTl * 128
    src_bf = src_tiles[0].buf.dtype == BF16
    tb = [ctx.ab.alloc(f"nrm_tb{i}", 512) for i in range(2)]
    if exact:
        sq = [ctx.af.alloc(f"nrm_sq{i}", 512) for i in range(2)]
        onecol = ctx.of32[:, 0:1]
    else:
        sq = [ctx.ab.alloc(f"nrm_sq{i}", 512) for i in range(2)]
        onecol = ctx.ones[:, 0:1]
    d1 = [ctx.af.alloc(f"nrm_d1{i}", 512) for i in range(2)]
    rowb, rowf = ctx.rowb, ctx.rowf
    ebias = ctx.vcol("eps_1e5" if eps == 1e-5 else "eps_1e6")[0:1, :]
    Tl = src_tiles[0].F
    NTl = Tl // 512
    for n in range(NTl):
        t0, t1 = n * 512, (n + 1) * 512
        ps_sum = ctx.bank()
        ps_sumsq = ctx.bank()
        sum_tv = ps_sum.tv[0:1, 0:512]
        sumsq_tv = ps_sumsq.tv[0:1, 0:512]
        for dt_ in range(DTl):
            s = src_tiles[dt_][:, t0:t1]
            if kind == "ln":
                if exact:
                    xb = s
                elif src_bf:
                    xb = s
                else:
                    xb = tb[dt_ % 2]
                    p.op("vector", lambda e, o=xb, i=s: e.tensor_copy(o.ap, i.ap),
                         reads=[s], writes=[xb], tag="nrm_cast")
                p.op("tensor",
                     lambda e, o=sum_tv, a=onecol, b=xb, s_=(dt_ == 0),
                            t_=(dt_ == DTl - 1):
                         e.matmul(o.ap, a.ap, b.ap, start=s_, stop=t_),
                     reads=[onecol, xb], writes=[sum_tv], tag="nrm_mmx")
            sqt = sq[dt_ % 2]
            p.op("scalar", lambda e, o=sqt, i=s: e.activation(o.ap, i.ap, AF.Square),
                 reads=[s], writes=[sqt], tag="nrm_sq")
            p.op("tensor",
                 lambda e, o=sumsq_tv, a=onecol, b=sqt, s_=(dt_ == 0),
                        t_=(dt_ == DTl - 1):
                     e.matmul(o.ap, a.ap, b.ap, start=s_, stop=t_),
                 reads=[onecol, sqt], writes=[sumsq_tv], tag="nrm_mmsq")
        r0 = rowf.tv[0:1, 0:512]
        r1 = rowf.tv[0:1, 512:1024]
        r2 = rowf.tv[0:1, 1024:1536]
        r3 = rowf.tv[0:1, 1536:2048]
        if kind == "ln":
            p.op("vector", lambda e, o=r2, i=sum_tv:
                 e.tensor_scalar(o.ap, i.ap, 1.0 / Dl, None, ALU.mult),
                 reads=[sum_tv], writes=[r2], tag="nrm_m")
            p.op("scalar", lambda e, o=r3, i=r2: e.activation(o.ap, i.ap, AF.Square),
                 reads=[r2], writes=[r3], tag="nrm_msq")
            p.op("vector", lambda e, o=r0, i=sumsq_tv, s2=r3:
                 e.scalar_tensor_tensor(o.ap, i.ap, 1.0 / Dl, s2.ap,
                                        ALU.mult, ALU.subtract),
                 reads=[sumsq_tv, r3], writes=[r0], tag="nrm_var")
        else:
            p.op("vector", lambda e, o=r0, i=sumsq_tv:
                 e.tensor_scalar(o.ap, i.ap, 1.0 / Dl, None, ALU.mult),
                 reads=[sumsq_tv], writes=[r0], tag="nrm_var")
        p.op("scalar", lambda e, o=r1, i=r0, b=ebias:
             e.activation(o.ap, i.ap, AF.Sqrt, bias=b.ap),
             reads=[r0, ebias], writes=[r1], tag="nrm_sd")
        p.op("vector", lambda e, o=r3, i=r1: e.reciprocal(o.ap, i.ap),
             reads=[r1], writes=[r3], tag="nrm_rstd")
        rs_b = rowb.tv[0:1, 0:512]
        p.op("vector", lambda e, o=rs_b, i=r3: e.tensor_copy(o.ap, i.ap),
             reads=[r3], writes=[rs_b], tag="nrm_rb")
        if keep_rows is not None:
            km, kr = keep_rows
            o_ = kr[0:1, t0:t1]
            p.op("vector", lambda e, o=o_, i=r3: e.tensor_copy(o.ap, i.ap),
                 reads=[r3], writes=[o_], tag="nrm_keepr")
            o_ = km[0:1, t0:t1]
            p.op("vector", lambda e, o=o_, i=r2: e.tensor_copy(o.ap, i.ap),
                 reads=[r2], writes=[o_], tag="nrm_keepm")
        if keep_cols is not None:
            kmc, krc = keep_cols
            for ti in range(4):
                for (row, cols) in ((r2, kmc), (r3, krc)):
                    psy = ctx.bank()
                    pc = psy.tv[:, 0:1]
                    lt = row[0:1, ti * 128:(ti + 1) * 128]
                    p.op("tensor", lambda e, o=pc, a=lt, b=ctx.of32[0:1, 0:1]:
                         e.matmul(o.ap, a.ap, b.ap, start=True, stop=True),
                         reads=[lt, ctx.of32[0:1, 0:1]], writes=[pc], tag="mm_kc")
                    o_ = cols[:, n * 4 + ti:n * 4 + ti + 1]
                    p.op("vector", lambda e, o=o_, i=pc: e.tensor_copy(o.ap, i.ap),
                         reads=[pc], writes=[o_], tag="ev_kc")
        ps_r = ctx.bank()
        rB = ps_r.tv[:, 0:512]
        bcast_row(ctx, rs_b, rB)
        if kind == "ln":
            m_b = rowb.tv[0:1, 512:1024]
            p.op("vector", lambda e, o=m_b, i=r2: e.tensor_copy(o.ap, i.ap),
                 reads=[r2], writes=[m_b], tag="nrm_mb")
            ps_m = ctx.bank()
            mB = ps_m.tv[:, 0:512]
            bcast_row(ctx, m_b, mB)
        for dt_ in range(DTl):
            s = src_tiles[dt_][:, t0:t1]
            o = out_tiles[dt_][:, t0:t1]
            if kind == "ln":
                dd = d1[dt_ % 2]
                p.op(eng_apply, lambda e, a=s, b=mB, oo=dd:
                     e.tensor_tensor(oo.ap, a.ap, b.ap, ALU.subtract),
                     reads=[s, mB], writes=[dd], tag="nrm_sub")
                p.op(eng_apply, lambda e, a=dd, b=rB, oo=o:
                     e.tensor_tensor(oo.ap, a.ap, b.ap, ALU.mult),
                     reads=[dd, rB], writes=[o], tag="nrm_mul")
                if out2_tiles is not None:
                    o2 = out2_tiles[dt_][:, t0:t1]
                    p.op(eng_apply, lambda e, a=dd, b=rB, oo=o2:
                         e.tensor_tensor(oo.ap, a.ap, b.ap, ALU.mult),
                         reads=[dd, rB], writes=[o2], tag="nrm_mul2")
            else:
                g = ctx.vcol(gamma, dt_)
                p.op(eng_apply, lambda e, a=s, sc=g, b=rB, oo=o:
                     e.scalar_tensor_tensor(oo.ap, a.ap, sc.ap, b.ap,
                                            ALU.mult, ALU.mult),
                     reads=[s, rB, g], writes=[o], tag="nrm_app")
                if out2_tiles is not None:
                    o2 = out2_tiles[dt_][:, t0:t1]
                    p.op(eng_apply, lambda e, a=s, sc=g, b=rB, oo=o2:
                         e.scalar_tensor_tensor(oo.ap, a.ap, sc.ap, b.ap,
                                                ALU.mult, ALU.mult),
                         reads=[s, rB, g], writes=[o2], tag="nrm_app2")
    for i in range(2):
        ctx.ab.free(f"nrm_tb{i}")
        (ctx.af if exact else ctx.ab).free(f"nrm_sq{i}")
        ctx.af.free(f"nrm_d1{i}")


def proj_fm(ctx, in_tiles, wstream, MT, evac, panel_order=None, tag="proj"):
    KT = len(in_tiles)
    order = panel_order if panel_order is not None else range(MT)
    for mt in order:
        w = wstream.panel(mt)
        for n in range(NT):
            ps = ctx.bank()
            pv = ps.tv[:, 0:512]
            pairs = [(w[:, kt * 128:(kt + 1) * 128],
                      in_tiles[kt][:, n * 512:(n + 1) * 512]) for kt in range(KT)]
            matmul_acc(ctx, pv, pairs, tag=tag)
            evac(mt, n, pv)


def emit_l0_mix(ctx, st):
    p = ctx.p
    xres = st["xres"]
    xinner = [ctx.ab.alloc(f"xi{i}", T) for i in range(DT)]
    st["xinner"] = xinner
    emit_norm(ctx, xres, xinner, "ln", eps=1e-5)

    xnb = [ctx.ab.alloc(f"xnb{i}", T) for i in range(DT)]
    emit_norm(ctx, xinner, xnb, "rms", gamma="w_rms_mix", eps=1e-6)

    sig = ctx.ab.alloc("sig_t", T)
    tnh = ctx.ab.alloc("tanh_t", T)
    xs = [ctx.ab.alloc(f"xs{i}", T) for i in range(2)]
    a_f = [ctx.af.alloc("a0", T)]
    h = [ctx.ab.alloc(f"h{i}", T) for i in range(2)]
    ws = WStream(ctx, "ws", None, DT, 3 * D, nslots=2)

    def mm_evac(wpanel, src_tiles, n, act, bname, mt, out_tile):
        ps = ctx.bank()
        pv = ps.tv[:, 0:512]
        pairs = [(wpanel[:, kt * 128:(kt + 1) * 128],
                  src_tiles[kt][:, n * 512:(n + 1) * 512]) for kt in range(DT)]
        matmul_acc(ctx, pv, pairs)
        o = out_tile[:, n * 512:(n + 1) * 512]
        bcol = ctx.vcol(bname, mt)
        p.op("scalar", lambda e, oo=o, i=pv, b=bcol:
             e.activation(oo.ap, i.ap, act, bias=b.ap),
             reads=[pv, bcol], writes=[o], tag="evac")

    for mt in range(DT):
        wall = ws.load(ctx.drams["l0_gvd"][mt])
        wgp, wvp, wdp = wall[:, 0:D], wall[:, D:2 * D], wall[:, 2 * D:3 * D]
        for n in range(NT):
            mm_evac(wgp, xnb, n, AF.Sigmoid, "bg", mt, sig)
        for n in range(NT):
            mm_evac(wvp, xnb, n, AF.Tanh, "bv", mt, tnh)
        xst = xs[mt % 2]
        p.op("vector", lambda e, oo=xst, x1=sig, x2=tnh:
             e.tensor_tensor(oo.ap, x1.ap, x2.ap, ALU.mult),
             reads=[sig, tnh], writes=[xst], tag="xs_mul")
        for n in range(NT):
            mm_evac(wdp, xnb, n, AF.Sigmoid, "bd", mt, sig)
        at = a_f[0]
        p.op("vector", lambda e, oo=at, i=sig:
             e.tensor_scalar(oo.ap, i.ap, 0.998, 0.001, ALU.mult, ALU.add),
             reads=[sig], writes=[at], tag="a_ts")
        ht = h[mt % 2]
        p.op("vector", lambda e, o=ht, a=at, x=xst:
             e.tensor_tensor_scan(o.ap, a.ap, x.ap, 0.0, ALU.mult, ALU.add),
             reads=[at, xst], writes=[ht], tag="scan")
        xi = xinner[mt]
        p.op("vector", lambda e, o=xi, a=xi, b=ht:
             e.tensor_tensor(o.ap, a.ap, b.ap, ALU.add),
             reads=[xi, ht], writes=[xi], tag="res_add")
    ws.release()
    ctx.ab.free("sig_t")
    ctx.ab.free("tanh_t")
    ctx.af.free("a0")
    for i in range(2):
        ctx.ab.free(f"xs{i}")
        ctx.ab.free(f"h{i}")
    for i in range(DT):
        ctx.ab.free(f"xnb{i}")


def emit_l0_attn(ctx, st):
    p = ctx.p
    xinner = st["xinner"]
    xnb = [ctx.ab.alloc(f"xnb{i}", T) for i in range(DT)]
    emit_norm(ctx, xinner, xnb, "rms", gamma="w_rms_attn", eps=1e-6)

    qb2 = [ctx.ab.alloc(f"at_q{i}", 512) for i in range(2)]
    kb2 = [ctx.ab.alloc(f"at_k{i}", 640) for i in range(2)]
    vb2 = [ctx.ab.alloc(f"at_v{i}", 640) for i in range(2)]
    prob = [ctx.ab.alloc(f"at_pr{i}", 256) for i in range(4)]
    attT = [ctx.ab.alloc(f"at_aT{i}", 256) for i in range(4)]
    ao_tm = ctx.ab.alloc("at_ao", 4096)
    ao_fm = ctx.ab.alloc("at_af", 4096)
    acol = ctx.af.alloc("at_acol", 64)
    wqkv = WStream(ctx, "wqkv", None, 8, 3 * D, nslots=2)
    wos = WStream(ctx, "wos", None, 8, D, nslots=2)
    masks = ctx.masks

    for tg in range(NT):
        t0 = tg * 512

        def emit_qkv(hp):
            qb, kb, vb = qb2[hp % 2], kb2[hp % 2], vb2[hp % 2]
            w3 = wqkv.load(ctx.drams["l0_qkv3"][hp])
            wq = w3[:, 0:D]
            ps = ctx.bank()
            pv = ps.tv[:, 0:512]
            pairs = [(wq[:, kt * 128:(kt + 1) * 128],
                      xnb[kt][:, t0:t0 + 512]) for kt in range(DT)]
            matmul_acc(ctx, pv, pairs, tag="mm_q")
            p.op("vector", lambda e, o=qb, i=pv: e.tensor_copy(o.ap, i.ap),
                 reads=[pv], writes=[qb], tag="ev_q")
            wk = w3[:, D:2 * D]
            if tg == 0:
                p.op("vector", lambda e, o=kb[:, 0:128]: e.memset(o.ap, 0.0),
                     writes=[kb[:, 0:128]], tag="kpad0")
                kranges = [(128, t0, 512)]
            else:
                kranges = [(0, t0 - 128, 512), (512, t0 + 384, 128)]
            for (df, src_t, ln) in kranges:
                ps = ctx.bank()
                pv = ps.tv[:, 0:ln]
                pairs = [(wk[:, kt * 128:(kt + 1) * 128],
                          xnb[kt][:, src_t:src_t + ln]) for kt in range(DT)]
                matmul_acc(ctx, pv, pairs, tag="mm_k")
                o = kb[:, df:df + ln]
                p.op("vector", lambda e, o=o, i=pv: e.tensor_copy(o.ap, i.ap),
                     reads=[pv], writes=[o], tag="ev_k")
            wv = w3[:, 2 * D:3 * D]
            if tg == 0:
                p.op("vector", lambda e, o=vb[:, 0:128]: e.memset(o.ap, 0.0),
                     writes=[vb[:, 0:128]], tag="vpad0")
                vtis = range(1, 5)
            else:
                vtis = range(0, 5)
            for vi in vtis:
                tt = t0 - 128 + vi * 128
                ps = ctx.bank()
                pv = ps.tv[:, 0:128]
                pairs = [(xnb[kt][:, tt:tt + 128],
                          wv[:, kt * 128:(kt + 1) * 128]) for kt in range(DT)]
                matmul_acc(ctx, pv, pairs, tag="mm_v")
                o = vb[:, vi * 128:(vi + 1) * 128]
                p.op("vector", lambda e, o=o, i=pv: e.tensor_copy(o.ap, i.ap),
                     reads=[pv], writes=[o], tag="ev_v")
            return qb, kb, vb

        def emit_chains(hp, qb, kb, vb):
            chains = [(cl, hh) for cl in range(4) for hh in range(2)]
            scs = {}

            def stage_A(ci):
                cl, hh = chains[ci]
                h0 = hh * 64
                q_tv = qb[h0:h0 + 64, cl * 128:(cl + 1) * 128]
                k_tv = kb[h0:h0 + 64, cl * 128:cl * 128 + 256]
                ps = ctx.banks_f32[ci % 4]
                sc = ps.tv[:, 0:256]
                scs[ci] = sc
                p.op("tensor", lambda e, o=sc, a=q_tv, b=k_tv:
                     e.matmul(o.ap, a.ap, b.ap, start=True, stop=True),
                     reads=[q_tv, k_tv], writes=[sc], tag="mm_sc")

            for ci in range(4):
                stage_A(ci)
            for grp in range(2):
                cis = [grp * 4 + j for j in range(4)]
                meta = {}
                for ci in cis:
                    cl, hh = chains[ci]
                    skip_prev = (tg == 0 and cl == 0)
                    meta[ci] = (cl, hh * 64, ci % 4,
                                [1] if skip_prev else [0, 1])
                for ci in cis:
                    _, _, gi, _ = meta[ci]
                    nmax = acol[:, gi * 4:gi * 4 + 1]
                    p.op("vector", lambda e, o=nmax, i=scs[ci]:
                         e.reduce_max(o.ap, i.ap, AX.X, negate=True),
                         reads=[scs[ci]], writes=[nmax], tag="sc_max")
                for ci in cis:
                    _, _, gi, _ = meta[ci]
                    nmax = acol[:, gi * 4:gi * 4 + 1]
                    bias = acol[:, gi * 4 + 1:gi * 4 + 2]
                    p.op("vector", lambda e, o=bias, i=nmax:
                         e.tensor_scalar(o.ap, i.ap, 0.125, None, ALU.mult),
                         reads=[nmax], writes=[bias], tag="sc_bias")
                for ci in cis:
                    _, _, gi, _ = meta[ci]
                    bias = acol[:, gi * 4 + 1:gi * 4 + 2]
                    p.op("scalar", lambda e, o=prob[gi], i=scs[ci], b=bias:
                         e.activation(o.ap, i.ap, AF.Exp, bias=b.ap,
                                      scale=0.125),
                         reads=[scs[ci], bias], writes=[prob[gi]],
                         tag="sc_exp")
                if grp == 0:
                    for ci in range(4, 8):
                        stage_A(ci)
                for ci in cis:
                    cl, _, gi, _ = meta[ci]
                    mk = masks[:, 256:512] if (tg == 0 and cl == 0) \
                        else masks[:, 0:256]
                    p.op("gpsimd", lambda e, o=prob[gi], m=mk:
                         e.tensor_tensor(o.ap, o.ap, m.ap, ALU.mult),
                         reads=[prob[gi], mk], writes=[prob[gi]],
                         tag="sc_mask")
                for ci in cis:
                    _, _, gi, _ = meta[ci]
                    s_ = acol[:, gi * 4 + 2:gi * 4 + 3]
                    p.op("vector", lambda e, o=s_, i=prob[gi]:
                         e.reduce_sum(o.ap, i.ap, AX.X),
                         reads=[prob[gi]], writes=[s_], tag="sc_sum")
                for ci in cis:
                    _, _, gi, _ = meta[ci]
                    s_ = acol[:, gi * 4 + 2:gi * 4 + 3]
                    r_ = acol[:, gi * 4 + 3:gi * 4 + 4]
                    p.op("vector", lambda e, o=r_, i=s_:
                         e.reciprocal(o.ap, i.ap),
                         reads=[s_], writes=[r_], tag="sc_rec")
                pb = ctx.bankbf()
                for ci in cis:
                    _, _, gi, halves = meta[ci]
                    for hf in halves:
                        dst_ps = pb.tv[:, gi * 256 + hf * 128:
                                       gi * 256 + (hf + 1) * 128]
                        transpose128(ctx, dst_ps, prob[gi][:, hf * 128:
                                                           (hf + 1) * 128])
                for ci in cis:
                    _, _, gi, halves = meta[ci]
                    for hf in halves:
                        dst_ps = pb.tv[:, gi * 256 + hf * 128:
                                       gi * 256 + (hf + 1) * 128]
                        dtv = attT[gi][:, hf * 128:(hf + 1) * 128]
                        p.op("vector", lambda e, o=dtv, i=dst_ps:
                             e.tensor_copy(o.ap, i.ap),
                             reads=[dst_ps], writes=[dtv], tag="ev_aT")
                po = ctx.banks_f32[5]
                for ci in cis:
                    cl, h0, gi, halves = meta[ci]
                    ov = po.tv[:, gi * 64:(gi + 1) * 64]
                    mms = []
                    for hf in halves:
                        lt = attT[gi][:, hf * 128:(hf + 1) * 128]
                        rt = vb[:, cl * 128 + hf * 128:
                                cl * 128 + hf * 128 + 128][:, h0:h0 + 64]
                        mms.append((lt, rt))
                    matmul_acc(ctx, ov, mms, tag="mm_pv")
                for ci in cis:
                    cl, h0, gi, _ = meta[ci]
                    ov = po.tv[:, gi * 64:(gi + 1) * 64]
                    r_ = acol[:, gi * 4 + 3:gi * 4 + 4]
                    o = ao_tm[:, cl * 1024 + hp * 128 + h0:
                              cl * 1024 + hp * 128 + h0 + 64]
                    p.op("vector", lambda e, o=o, i=ov, r=r_:
                         e.tensor_scalar(o.ap, i.ap, r.ap, None, ALU.mult),
                         reads=[ov, r_], writes=[o], tag="ev_o")

        prev = None
        for hp in range(8):
            cur = emit_qkv(hp)
            if prev is not None:
                emit_chains(hp - 1, *prev)
            prev = cur
        emit_chains(7, *prev)

        for cl in range(4):
            for dt_ in range(DT):
                pb = ctx.bankbf()
                dst_ps = pb.tv[:, 0:128]
                src = ao_tm[:, cl * 1024 + dt_ * 128:cl * 1024 + (dt_ + 1) * 128]
                transpose128(ctx, dst_ps, src)
                dtv = ao_fm[:, dt_ * 512 + cl * 128:dt_ * 512 + (cl + 1) * 128]
                p.op("vector", lambda e, o=dtv, i=dst_ps: e.tensor_copy(o.ap, i.ap),
                     reads=[dst_ps], writes=[dtv], tag="ev_aofm")
        for mt in range(DT):
            wop = wos.load(ctx.drams["l0_wo"][mt])
            ps = ctx.bank()
            pv = ps.tv[:, 0:512]
            pairs = [(wop[:, kt * 128:(kt + 1) * 128],
                      ao_fm[:, kt * 512:kt * 512 + 512]) for kt in range(DT)]
            matmul_acc(ctx, pv, pairs, tag="mm_wo")
            xi = xinner[mt][:, t0:t0 + 512]
            p.op("vector", lambda e, o=xi, i=pv:
                 e.tensor_tensor(o.ap, o.ap, i.ap, ALU.add),
                 reads=[pv, xi], writes=[xi], tag="res_attn")
    wqkv.release()
    wos.release()
    for nm in ["at_q0", "at_q1", "at_k0", "at_k1", "at_v0", "at_v1",
               "at_pr0", "at_pr1", "at_pr2", "at_pr3",
               "at_aT0", "at_aT1", "at_aT2", "at_aT3", "at_ao", "at_af"]:
        ctx.ab.free(nm)
    ctx.af.free("at_acol")
    for i in range(DT):
        ctx.ab.free(f"xnb{i}")


def emit_l0_glu(ctx, st):
    p = ctx.p
    xres, xinner = st["xres"], st["xinner"]
    xnb = [ctx.ab.alloc(f"xnb{i}", T) for i in range(DT)]
    emit_norm(ctx, xinner, xnb, "rms", gamma="w_rms_ffn", eps=1e-6)

    gbuf = ctx.ab.alloc("glu_g", 512)
    act = [ctx.ab.alloc(f"glu_a{i}", T) for i in range(8)]
    ws = WStream(ctx, "glu_w", None, 32, 2 * D, nslots=2)
    wd_s = WStream(ctx, "glu_wd", None, 8, D, nslots=2)

    for g in range(4):
        for sub in range(8):
            mt = g * 8 + sub
            wgu = ws.load(ctx.drams["l0_gu"][mt])
            wgp, wup = wgu[:, 0:D], wgu[:, D:2 * D]
            for n in range(NT):
                ps = ctx.bank()
                pv = ps.tv[:, 0:512]
                pairs = [(wgp[:, kt * 128:(kt + 1) * 128],
                          xnb[kt][:, n * 512:(n + 1) * 512]) for kt in range(DT)]
                matmul_acc(ctx, pv, pairs, tag="mm_gate")
                o = gbuf[:, 0:512]
                p.op("scalar", lambda e, o=o, i=pv:
                     e.activation(o.ap, i.ap, AF.Silu),
                     reads=[pv], writes=[o], tag="ev_silu")
                ps2 = ctx.bank()
                pv2 = ps2.tv[:, 0:512]
                pairs = [(wup[:, kt * 128:(kt + 1) * 128],
                          xnb[kt][:, n * 512:(n + 1) * 512]) for kt in range(DT)]
                matmul_acc(ctx, pv2, pairs, tag="mm_up")
                oa = act[sub][:, n * 512:(n + 1) * 512]
                p.op("vector", lambda e, o=oa, i=pv2, gg=o:
                     e.tensor_tensor(o.ap, i.ap, gg.ap, ALU.mult),
                     reads=[pv2, o], writes=[oa], tag="ev_act")
        for mt in range(DT):
            wop = wd_s.load(ctx.drams[f"l0_wout{g}"][mt])
            for n in range(NT):
                ps = ctx.bank()
                pv = ps.tv[:, 0:512]
                pairs = [(wop[:, kt * 128:(kt + 1) * 128],
                          act[kt][:, n * 512:(n + 1) * 512]) for kt in range(DT)]
                matmul_acc(ctx, pv, pairs, tag="mm_down")
                xi = xinner[mt][:, n * 512:(n + 1) * 512]
                p.op("vector", lambda e, o=xi, i=pv:
                     e.tensor_tensor(o.ap, o.ap, i.ap, ALU.add),
                     reads=[pv, xi], writes=[xi], tag="res_glu")
    ws.release()
    wd_s.release()
    ctx.ab.free("glu_g")
    for i in range(8):
        ctx.ab.free(f"glu_a{i}")
    for i in range(DT):
        ctx.ab.free(f"xnb{i}")
    for i in range(DT):
        p.op("vector", lambda e, o=xres[i], b=xinner[i]:
             e.tensor_tensor(o.ap, o.ap, b.ap, ALU.add),
             reads=[xres[i], xinner[i]], writes=[xres[i]], tag="res_mod")
    for i in range(DT):
        ctx.ab.free(f"xi{i}")
    del st["xinner"]


def emit_l0_ffn(ctx, st):
    p = ctx.p
    xres = st["xres"]
    xnb = [ctx.ab.alloc(f"xnb{i}", T) for i in range(DT)]
    emit_norm(ctx, xres, xnb, "ln", eps=1e-5)

    fslot = [ctx.af.alloc("fw0", D)]
    accc = ctx.af.alloc("q_acc", 8)
    onef = ctx.of32
    p.op("vector", lambda e, o=accc[:, 0:1]: e.memset(o.ap, 0.0),
         writes=[accc[:, 0:1]], tag="q_z")
    for mt in range(DT):
        sl = fslot[0]
        src = ctx.drams["l0f_w"][mt]
        p.op("sync", lambda e, d=sl[:, 0:512], s=src:
             e.dma_start(out=d.ap, in_=s[:, 0:512]),
             writes=[sl[:, 0:512]], dma="l0fw0", tag="qw_load")
        p.op("scalar", lambda e, d=sl[:, 512:1024], s=src:
             e.dma_start(out=d.ap, in_=s[:, 512:1024]),
             writes=[sl[:, 512:1024]], dma="l0fw1", tag="qw_load")
        c = accc[:, 3:4]
        p.op("vector", lambda e, o=c, i=sl:
             e.tensor_reduce(o.ap, i.ap, AX.X, ALU.add, apply_absolute_value=True),
             reads=[sl], writes=[c], tag="q_abs")
        p.op("vector", lambda e, o=accc[:, 0:1], i=c:
             e.tensor_tensor(o.ap, o.ap, i.ap, ALU.add),
             reads=[accc[:, 0:1], c], writes=[accc[:, 0:1]], tag="q_accadd")
    ps = ctx.bank()
    tot = ps.tv[0:1, 0:1]
    p.op("tensor", lambda e, o=tot, a=onef[:, 0:1], b=accc[:, 0:1]:
         e.matmul(o.ap, a.ap, b.ap, start=True, stop=True),
         reads=[onef[:, 0:1], accc[:, 0:1]], writes=[tot], tag="q_tot")
    sc1 = ctx.rowf.tv[0:1, 0:1]
    p.op("vector", lambda e, o=sc1, i=tot:
         e.tensor_scalar(o.ap, i.ap, 1.0 / (D * D), 1e-5, ALU.mult, ALU.max),
         reads=[tot], writes=[sc1], tag="q_scale")
    ps2 = ctx.bank()
    p.op("tensor", lambda e, o=ps2.tv[:, 0:1], a=onef[0:1, 0:128], b=sc1:
         e.matmul(o.ap, a.ap, b.ap, start=True, stop=True),
         reads=[onef[0:1, 0:128], sc1], writes=[ps2.tv[:, 0:1]], tag="q_bs")
    p.op("vector", lambda e, o=accc[:, 2:3], i=ps2.tv[:, 0:1]:
         e.tensor_copy(o.ap, i.ap),
         reads=[ps2.tv[:, 0:1]], writes=[accc[:, 2:3]], tag="q_scol")
    p.op("vector", lambda e, o=accc[:, 1:2], i=accc[:, 2:3]:
         e.tensor_scalar(o.ap, i.ap, 0.5, None, ALU.mult),
         reads=[accc[:, 2:3]], writes=[accc[:, 1:2]], tag="q_thr")
    wq = ctx.ab.alloc("q_wq", DT * D)
    t_abs = ctx.af.alloc("q_tabs", D)
    t_sgn = ctx.ab.alloc("q_tsgn", D)
    for mt in range(DT):
        sl = fslot[0]
        src = ctx.drams["l0f_w"][mt]
        p.op("sync", lambda e, d=sl[:, 0:512], s=src:
             e.dma_start(out=d.ap, in_=s[:, 0:512]),
             writes=[sl[:, 0:512]], dma="l0fw0", tag="qw_load2")
        p.op("scalar", lambda e, d=sl[:, 512:1024], s=src:
             e.dma_start(out=d.ap, in_=s[:, 512:1024]),
             writes=[sl[:, 512:1024]], dma="l0fw1", tag="qw_load2")
        p.op("scalar", lambda e, o=t_abs, i=sl: e.activation(o.ap, i.ap, AF.Abs),
             reads=[sl], writes=[t_abs], tag="q_abs2")
        p.op("scalar", lambda e, o=t_sgn, i=sl: e.activation(o.ap, i.ap, AF.Sign),
             reads=[sl], writes=[t_sgn], tag="q_sgn")
        p.op("vector", lambda e, o=t_abs, i=t_abs, th=accc[:, 1:2]:
             e.tensor_scalar(o.ap, i.ap, th.ap, None, ALU.is_ge),
             reads=[t_abs, accc[:, 1:2]], writes=[t_abs], tag="q_ge")
        o = wq[:, mt * D:(mt + 1) * D]
        p.op("vector", lambda e, o=o, a=t_abs, b=t_sgn:
             e.tensor_tensor(o.ap, a.ap, b.ap, ALU.mult),
             reads=[t_abs, t_sgn], writes=[o], tag="q_mul")
    for mt in range(DT):
        for n in range(NT):
            ps = ctx.bank()
            pv = ps.tv[:, 0:512]
            pairs = [(wq[:, mt * D + kt * 128:mt * D + (kt + 1) * 128],
                      xnb[kt][:, n * 512:(n + 1) * 512]) for kt in range(DT)]
            matmul_acc(ctx, pv, pairs, tag="mm_tern")
            xr = xres[mt][:, n * 512:(n + 1) * 512]
            p.op("vector", lambda e, o=xr, i=pv, s=accc[:, 2:3]:
                 e.scalar_tensor_tensor(o.ap, i.ap, s.ap, o.ap, ALU.mult, ALU.add),
                 reads=[pv, accc[:, 2:3], xr], writes=[xr], tag="res_tern")
    ctx.ab.free("q_wq")
    ctx.af.free("q_tabs")
    ctx.ab.free("q_tsgn")
    ctx.af.free("q_acc")
    ctx.af.free("fw0")
    for i in range(DT):
        ctx.ab.free(f"xnb{i}")


MAMBA_DEBUG = False


def dbg_dump(ctx, name, tv, parts, free):
    if not MAMBA_DEBUG:
        return
    t = ctx.p.nc.dram_tensor("dbg_" + name, [parts, free], F32,
                             kind="ExternalOutput").ap()
    ctx.p.op("gpsimd", lambda e, d=t, s=tv[0:parts, 0:free]:
             e.dma_start(out=d, in_=s.ap),
             reads=[tv[0:parts, 0:free]], dma="dbg", tag="dbg")


def emit_l1_mamba(ctx, st):
    p = ctx.p
    xres = st["xres"]
    xnb = [ctx.ab.alloc(f"xnb{i}", T) for i in range(DT)]
    emit_norm(ctx, xres, xnb, "ln", eps=1e-5)
    for i in range(DT):
        dst = ctx.drams["xspill"][i * 128:(i + 1) * 128, :]
        p.op("sync", lambda e, d=dst, s=xres[i]: e.dma_start(out=d, in_=s.ap),
             reads=[xres[i]], dma="xsp_st", tag="x_spill")
    for i in range(DT):
        ctx.af.free(f"xr{i}")
    del st["xres"]

    h1 = [ctx.ab.alloc(f"h1_{i}", T) for i in range(DT)]
    ws = WStream(ctx, "mam_w", None, 8, D, nslots=2)
    for mt in range(DT):
        wp = ws.load(ctx.drams["l1_fc1"][mt])
        for n in range(NT):
            ps = ctx.bank()
            pv = ps.tv[:, 0:512]
            pairs = [(wp[:, kt * 128:(kt + 1) * 128],
                      xnb[kt][:, n * 512:(n + 1) * 512]) for kt in range(DT)]
            matmul_acc(ctx, pv, pairs, tag="mm_fc1")
            o = h1[mt][:, n * 512:(n + 1) * 512]
            b = ctx.vcol("fc1_b", mt)
            p.op("vector", lambda e, o=o, i=pv, b=b:
                 e.tensor_scalar(o.ap, i.ap, b.ap, None, ALU.add),
                 reads=[pv, b], writes=[o], tag="ev_h1")
    for i in range(DT):
        ctx.ab.free(f"xnb{i}")

    xpd = ctx.ab.alloc("xpd", 16 * 64)
    xpb = ctx.ab.alloc("xpb", 16 * 16)
    xpc = ctx.ab.alloc("xpc", 16 * 16)
    dtpb = ctx.ab.alloc("dtpb", 16 * 128)
    for kt in range(16):
        for nm, dst, w in (("xpd", xpd[:, kt * 64:(kt + 1) * 64], 64),
                           ("xpb", xpb[0:128, kt * 16:(kt + 1) * 16], 16),
                           ("xpc", xpc[0:128, kt * 16:(kt + 1) * 16], 16)):
            src = ctx.drams["l1_" + nm][kt]
            p.op("gpsimd", lambda e, d=dst, s=src: e.dma_start(out=d.ap, in_=s),
                 writes=[dst], dma=f"ld_{nm}", tag="xp_load")
        dst = dtpb[0:64, kt * 128:(kt + 1) * 128]
        src = ctx.drams["l1_dtp"][kt]
        p.op("gpsimd", lambda e, d=dst, s=src: e.dma_start(out=d.ap, in_=s),
             writes=[dst], dma="ld_dtp", tag="dtp_load")

    Abuf = ctx.af.alloc("Abuf", 256)
    for dtile in range(16):
        src = ctx.vcols("Alog", dtile * 16, (dtile + 1) * 16)
        dst = Abuf[:, dtile * 16:(dtile + 1) * 16]
        p.op("scalar", lambda e, o=dst, i=src: e.activation(o.ap, i.ap, AF.Exp),
             reads=[src], writes=[dst], tag="A_exp")
        p.op("vector", lambda e, o=dst, i=dst:
             e.tensor_scalar(o.ap, i.ap, -1.0, None, ALU.mult),
             reads=[dst], writes=[dst], tag="A_neg")

    accd = ctx.af.alloc("accd", T)
    accb = ctx.af.alloc("accb", T)
    accc = ctx.af.alloc("accc", T)
    for nm, tv, pp in (("accd", accd, 64), ("accb", accb, 16), ("accc", accc, 16)):
        v = tv[0:pp, :]
        p.op("vector", lambda e, o=v: e.memset(o.ap, 0.0), writes=[v], tag="xdbl_z")
    xcpad = ctx.af.alloc("xcpad", T + 16)
    ctmp = ctx.af.alloc("ctmp", T)
    ug = [ctx.ab.alloc(f"ug{i}", T) for i in range(4)]

    def mm_stream(pv, wp, h1_src, n, tag):
        for kt in range(DT):
            lt = wp[:, kt * 128:(kt + 1) * 128]
            rt = h1_src(kt, n)
            p.op("tensor",
                 lambda e, o=pv, a=lt, b=rt, s=(kt == 0), t=(kt == DT - 1):
                     e.matmul(o.ap, a.ap, b.ap, start=s, stop=t),
                 reads=[lt, rt], writes=[pv], tag=tag)

    def make_u(dtile, udst, h1_src):
        wp = ws.load(ctx.drams["l1_inproj"][dtile])
        z3 = xcpad[:, 0:3]
        p.op("vector", lambda e, o=z3: e.memset(o.ap, 0.0), writes=[z3], tag="xc_z")
        for n in range(NT):
            ps = ctx.bank()
            pv = ps.tv[:, 0:512]
            mm_stream(pv, wp, h1_src, n, "mm_xc")
            o = xcpad[:, 3 + n * 512:3 + (n + 1) * 512]
            p.op("vector", lambda e, o=o, i=pv: e.tensor_copy(o.ap, i.ap),
                 reads=[pv], writes=[o], tag="ev_xc")
        c0 = ctx.vcol("cw0", dtile)
        p.op("vector", lambda e, o=ctmp, i=xcpad[:, 0:T], s=c0:
             e.tensor_scalar(o.ap, i.ap, s.ap, None, ALU.mult),
             reads=[xcpad[:, 0:T], c0], writes=[ctmp], tag="conv0")
        for w in range(1, 4):
            cw = ctx.vcol(f"cw{w}", dtile)
            sl = xcpad[:, w:w + T]
            p.op("vector", lambda e, o=ctmp, i=sl, s=cw, a=ctmp:
                 e.scalar_tensor_tensor(o.ap, i.ap, s.ap, a.ap, ALU.mult, ALU.add),
                 reads=[sl, cw, ctmp], writes=[ctmp], tag=f"conv{w}")
        cb = ctx.vcol("conv_b", dtile)
        p.op("scalar", lambda e, o=udst, i=ctmp, b=cb:
             e.activation(o.ap, i.ap, AF.Silu, bias=b.ap),
             reads=[ctmp, cb], writes=[udst], tag="u_silu")

    h1_res = lambda kt, n: h1[kt][:, n * 512:(n + 1) * 512]
    for g4 in range(4):
        for j in range(4):
            make_u(g4 * 4 + j, ug[j], h1_res)
        for (acc, pp, wsl, msz) in ((accd, 64, xpd, 64), (accb, 16, xpb, 16),
                                    (accc, 16, xpc, 16)):
            for n in range(NT):
                ps = ctx.bank()
                pv = ps.tv[0:pp, 0:512]
                pairs = [(wsl[0:128, (g4 * 4 + j) * msz:(g4 * 4 + j + 1) * msz],
                          ug[j][:, n * 512:(n + 1) * 512]) for j in range(4)]
                matmul_acc(ctx, pv, pairs, tag="mm_xdbl")
                o = acc[0:pp, n * 512:(n + 1) * 512]
                p.op("vector", lambda e, o=o, i=pv:
                     e.tensor_tensor(o.ap, o.ap, i.ap, ALU.add),
                     reads=[pv, o], writes=[o], tag="xdbl_add")
    for i in range(4):
        ctx.ab.free(f"ug{i}")

    dbg_dump(ctx, "accd", accd, 64, T)
    dbg_dump(ctx, "accb", accb, 16, T)
    dbg_dump(ctx, "accc", accc, 16, T)
    dtlow = ctx.ab.alloc("dtlow", T)
    Bbf = ctx.ab.alloc("Bbf", T)[0:16, :]
    Cbf_full = ctx.ab.alloc("Cbf", T)[0:16, :]
    for (src, d, pp) in ((accd[0:64, :], dtlow[0:64, :], 64),
                         (accb[0:16, :], Bbf, 16),
                         (accc[0:16, :], Cbf_full, 16)):
        s = src
        p.op("vector", lambda e, o=d, i=s: e.tensor_copy(o.ap, i.ap),
             reads=[s], writes=[d], tag="xdbl_cast")
    ctx.af.free("accd")
    ctx.af.free("accb")
    ctx.af.free("accc")

    u2 = [ctx.ab.alloc(f"u{i}", T) for i in range(2)]
    zs2 = [ctx.ab.alloc(f"zs{i}", T) for i in range(2)]
    X2 = [ctx.ab.alloc(f"X{i}", T) for i in range(2)]
    hb2 = [ctx.ab.alloc("hb0", T)] * 2
    Bbs2 = [ctx.ab.alloc(f"Bbs{i}", T) for i in range(2)]
    Cbs2 = [ctx.ab.alloc(f"Cbs{i}", T) for i in range(2)]
    dtu2 = [ctx.ab.alloc(f"dtu{i}", T) for i in range(2)]
    dtt2 = [ctx.af.alloc(f"dtt{i}", T) for i in range(2)]
    ysa2 = [ctx.af.alloc(f"ysa{i}", T) for i in range(2)]
    dA2 = [ctx.af.alloc(f"dA{i}", T) for i in range(2)]
    sp = [ctx.af.alloc(f"sp{i}", 512) for i in range(3)]

    def pre_tile(dtile, sub):
        u_t, zs, dtt, dtu, ysa = (u2[sub], zs2[sub], dtt2[sub], dtu2[sub],
                                  ysa2[sub])
        make_u(dtile, u_t, h1_res)
        wp = ws.load(ctx.drams["l1_inproj"][16 + dtile])
        for n in range(NT):
            ps = ctx.bank()
            pv = ps.tv[:, 0:512]
            mm_stream(pv, wp, h1_res, n, "mm_z")
            o = zs[:, n * 512:(n + 1) * 512]
            p.op("scalar", lambda e, o=o, i=pv: e.activation(o.ap, i.ap, AF.Silu),
                 reads=[pv], writes=[o], tag="ev_zs")
        bdt = ctx.vcol("dt_b", dtile)
        for n in range(NT):
            ps = ctx.bank()
            pv = ps.tv[:, 0:512]
            lt = dtpb[0:64, dtile * 128:(dtile + 1) * 128]
            rt = dtlow[0:64, n * 512:(n + 1) * 512]
            matmul_acc(ctx, pv, [(lt, rt)], tag="mm_dt")
            s0, s1, s2 = sp
            p.op("scalar", lambda e, o=s0, i=pv, b=bdt:
                 e.activation(o.ap, i.ap, AF.Abs, bias=b.ap),
                 reads=[pv, bdt], writes=[s0], tag="sp_abs")
            p.op("scalar", lambda e, o=s1, i=s0:
                 e.activation(o.ap, i.ap, AF.Exp, scale=-1.0),
                 reads=[s0], writes=[s1], tag="sp_exp")
            p.op("scalar", lambda e, o=s1, i=s1, b=ctx.vcol("one"):
                 e.activation(o.ap, i.ap, AF.Ln, bias=b.ap),
                 reads=[s1, ctx.vcol("one")], writes=[s1], tag="sp_ln")
            p.op("scalar", lambda e, o=s2, i=pv, b=bdt:
                 e.activation(o.ap, i.ap, AF.Relu, bias=b.ap),
                 reads=[pv, bdt], writes=[s2], tag="sp_relu")
            o = dtt[:, n * 512:(n + 1) * 512]
            p.op("vector", lambda e, o=o, a=s1, b=s2:
                 e.tensor_tensor(o.ap, a.ap, b.ap, ALU.add),
                 reads=[s1, s2], writes=[o], tag="sp_add")
        p.op("vector", lambda e, o=dtu, a=dtt, b=u_t:
             e.tensor_tensor(o.ap, a.ap, b.ap, ALU.mult),
             reads=[dtt, u_t], writes=[dtu], tag="dtu")
        p.op("vector", lambda e, o=ysa: e.memset(o.ap, 0.0), writes=[ysa],
             tag="ys_z")

    def post_tile(dtile, sub):
        u_t, zs, ysa = u2[sub], zs2[sub], ysa2[sub]
        Dc = ctx.vcol("Dvec", dtile)
        p.op("vector", lambda e, o=ysa, i=u_t, s=Dc, a=ysa:
             e.scalar_tensor_tensor(o.ap, i.ap, s.ap, a.ap, ALU.mult, ALU.add),
             reads=[u_t, Dc, ysa], writes=[ysa], tag="y_uD")
        yt = X2[sub]
        p.op("vector", lambda e, o=yt, a=ysa, b=zs:
             e.tensor_tensor(o.ap, a.ap, b.ap, ALU.mult),
             reads=[ysa, zs], writes=[yt], tag="y_z")
        dst = ctx.drams["ysp"][dtile * 128:(dtile + 1) * 128, :]
        p.op("sync", lambda e, d=dst, s=yt: e.dma_start(out=d, in_=s.ap),
             reads=[yt], dma=f"ysp{sub}", tag="y_spill")

    for dp in range(8):
        pre_tile(2 * dp, 0)
        pre_tile(2 * dp + 1, 1)
        for sidx in range(16):
            Bbs, Cbs = Bbs2[sidx % 2], Cbs2[sidx % 2]
            sel = ctx.sel16[0:16, sidx * 128:(sidx + 1) * 128]
            for n in range(NT):
                ps = ctx.bank()
                pv = ps.tv[:, 0:512]
                rt = Bbf[:, n * 512:(n + 1) * 512]
                matmul_acc(ctx, pv, [(sel, rt)], tag="mm_bb")
                o = Bbs[:, n * 512:(n + 1) * 512]
                p.op("scalar", lambda e, o=o, i=pv:
                     e.activation(o.ap, i.ap, AF.Identity),
                     reads=[pv], writes=[o], tag="ev_bb")
                ps2 = ctx.bank()
                pv2 = ps2.tv[:, 0:512]
                rt2 = Cbf_full[:, n * 512:(n + 1) * 512]
                matmul_acc(ctx, pv2, [(sel, rt2)], tag="mm_cb")
                o2 = Cbs[:, n * 512:(n + 1) * 512]
                p.op("scalar", lambda e, o=o2, i=pv2:
                     e.activation(o.ap, i.ap, AF.Identity),
                     reads=[pv2], writes=[o2], tag="ev_cb")
            for sub in range(2):
                dtile = 2 * dp + sub
                dtt, dtu, ysa = dtt2[sub], dtu2[sub], ysa2[sub]
                dAb, Xb, hb = dA2[sub], X2[sub], hb2[sub]
                acol = Abuf[:, dtile * 16 + sidx:dtile * 16 + sidx + 1]
                p.op("scalar", lambda e, o=dAb, i=dtt, s=acol:
                     e.activation(o.ap, i.ap, AF.Exp, scale=s.ap),
                     reads=[dtt, acol], writes=[dAb], tag="dA")
                p.op("vector", lambda e, o=Xb, a=dtu, b=Bbs:
                     e.tensor_tensor(o.ap, a.ap, b.ap, ALU.mult),
                     reads=[dtu, Bbs], writes=[Xb], tag="X_mul")
                p.op("vector", lambda e, o=hb, a=dAb, x=Xb:
                     e.tensor_tensor_scan(o.ap, a.ap, x.ap, 0.0, ALU.mult,
                                          ALU.add),
                     reads=[dAb, Xb], writes=[hb], tag="scan")
                p.op("vector", lambda e, o=Xb, a=hb, b=Cbs:
                     e.tensor_tensor(o.ap, a.ap, b.ap, ALU.mult),
                     reads=[hb, Cbs], writes=[Xb], tag="hC_mul")
                p.op("gpsimd", lambda e, o=ysa, a=ysa, b=Xb:
                     e.tensor_tensor(o.ap, a.ap, b.ap, ALU.add),
                     reads=[ysa, Xb], writes=[ysa], tag="ys_add")
        post_tile(2 * dp, 0)
        post_tile(2 * dp + 1, 1)

    for nm in ["zs0", "zs1", "u0", "u1", "X0", "X1", "hb0", "dtlow",
               "Bbf", "Cbf", "dtpb", "xpd", "xpb", "xpc", "Bbs0", "Bbs1",
               "Cbs0", "Cbs1", "dtu0", "dtu1"]:
        ctx.ab.free(nm)
    for nm in ["dtt0", "dtt1", "ysa0", "ysa1", "dA0", "dA1", "ctmp", "Abuf",
               "xcpad"]:
        ctx.af.free(nm)
    for i in range(3):
        ctx.af.free(f"sp{i}")
    for i in range(DT):
        ctx.ab.free(f"h1_{i}")

    opw = ctx.ab.alloc("opw", DT * 2048)
    for mt in range(DT):
        dst = opw[:, mt * 2048:(mt + 1) * 2048]
        srcw = ctx.drams["l1_outproj"][mt]
        p.op("gpsimd", lambda e, d=dst, s=srcw: e.dma_start(out=d.ap, in_=s),
             writes=[dst], dma=f"opw{mt}", tag="opw_load")
    ych = ctx.ab.alloc("ych", 16 * 512)
    oacc = [ctx.ab.alloc(f"oac{i}", T) for i in range(DT)]
    for n in range(NT):
        for kt in range(16):
            dst = ych[:, kt * 512:(kt + 1) * 512]
            srcy = ctx.drams["ysp"][kt * 128:(kt + 1) * 128,
                                    n * 512:(n + 1) * 512]
            p.op("sync", lambda e, d=dst, s=srcy: e.dma_start(out=d.ap, in_=s),
                 writes=[dst], dma=f"ych{kt}", tag="ych_load")
        for mt in range(DT):
            ps = ctx.bank()
            pv = ps.tv[:, 0:512]
            pairs = [(opw[:, mt * 2048 + kt * 128:mt * 2048 + (kt + 1) * 128],
                      ych[:, kt * 512:(kt + 1) * 512]) for kt in range(16)]
            matmul_acc(ctx, pv, pairs, tag="mm_op")
            o = oacc[mt][:, n * 512:(n + 1) * 512]
            p.op("vector", lambda e, o=o, i=pv: e.tensor_copy(o.ap, i.ap),
                 reads=[pv], writes=[o], tag="ev_op")
    ctx.ab.free("opw")
    ctx.ab.free("ych")

    dbg_dump(ctx, "oac0", oacc[0], 128, T)
    xres = [ctx.af.alloc(f"xr{i}", T) for i in range(DT)]
    st["xres"] = xres
    xsrc = ctx.drams["xspill"]
    for i in range(DT):
        src = xsrc[i * 128:(i + 1) * 128, :]
        p.op("sync", lambda e, d=xres[i], s=src: e.dma_start(out=d.ap, in_=s),
             writes=[xres[i]], dma=f"xrl{i}", tag="x_reload")
    for mt in range(DT):
        wp = ws.load(ctx.drams["l1_fc2"][mt])
        for n in range(NT):
            ps = ctx.bank()
            pv = ps.tv[:, 0:512]
            pairs = [(wp[:, kt * 128:(kt + 1) * 128],
                      oacc[kt][:, n * 512:(n + 1) * 512]) for kt in range(DT)]
            matmul_acc(ctx, pv, pairs, tag="mm_fc2")
            xr = xres[mt][:, n * 512:(n + 1) * 512]
            b2 = ctx.vcol("fc2_b", mt)
            p.op("vector", lambda e, o=xr, i=pv, s=b2:
                 e.scalar_tensor_tensor(o.ap, i.ap, s.ap, o.ap, ALU.add, ALU.add),
                 reads=[pv, b2, xr], writes=[xr], tag="res_fc2")
    for i in range(DT):
        ctx.ab.free(f"oac{i}")
    ws.release()


def emit_l1_moe(ctx, st):
    p = ctx.p
    xres = st["xres"]
    xnb = [ctx.ab.alloc(f"xnb{i}", T) for i in range(DT)]
    smalls = ctx.af.alloc("moe_sm", 64)
    mcols = smalls[:, 16:32]
    rcols = smalls[:, 32:48]
    emit_norm(ctx, xres, xnb, "ln", eps=1e-5, exact=True,
              keep_cols=(mcols, rcols))

    gwf = ctx.af.alloc("moe_gw", 8 * DT)
    for kt in range(DT):
        dst = gwf[:, kt * 8:(kt + 1) * 8]
        src = ctx.drams["moe_gate"][kt]
        p.op("sync", lambda e, d=dst, s=src: e.dma_start(out=d.ap, in_=s),
             writes=[dst], dma="moe_gw", tag="gw_load")
    ps = ctx.bank()
    cs_ps = ps.tv[0:1, 0:8]
    pairs = [(ctx.of32[:, 0:1], gwf[:, kt * 8:(kt + 1) * 8]) for kt in range(DT)]
    matmul_acc(ctx, cs_ps, pairs, tag="mm_cs")
    csrow = smalls[0:1, 56:64]
    p.op("vector", lambda e, o=csrow, i=cs_ps: e.tensor_copy(o.ap, i.ap),
         reads=[cs_ps], writes=[csrow], tag="ev_cs")
    csb_ps = ctx.bank()
    p.op("tensor", lambda e, o=csb_ps.tv[:, 0:8], a=ctx.of32[0:1, 0:128], b=csrow:
         e.matmul(o.ap, a.ap, b.ap, start=True, stop=True),
         reads=[ctx.of32[0:1, 0:128], csrow], writes=[csb_ps.tv[:, 0:8]],
         tag="mm_csb")
    csb = smalls[:, 0:8]
    p.op("vector", lambda e, o=csb, i=csb_ps.tv[:, 0:8]: e.tensor_copy(o.ap, i.ap),
         reads=[csb_ps.tv[:, 0:8]], writes=[csb], tag="ev_csb")
    mask_tm = ctx.af.alloc("moe_msk", 16)
    logit_t = ctx.af.alloc("moe_lg", 16)
    srow_b = ctx.rowb.tv[0:1, 1024:3072]
    mkb = ctx.ab.alloc("moe_mkb", 16)
    mask8 = ctx.ab.alloc("moe_mask8", T)
    for ti in range(16):
        psx = ctx.bank()
        raw = psx.tv[:, 0:8]
        pairs = [(xres[kt][:, ti * 128:(ti + 1) * 128],
                  gwf[:, kt * 8:(kt + 1) * 8]) for kt in range(DT)]
        matmul_acc(ctx, raw, pairs, tag="mm_lg")
        lg = logit_t[:, 0:8]
        mc = mcols[:, ti:ti + 1]
        rc = rcols[:, ti:ti + 1]
        t1 = mask_tm[:, 8:16]
        p.op("vector", lambda e, o=t1, i=csb, s=mc:
             e.tensor_scalar(o.ap, i.ap, s.ap, None, ALU.mult),
             reads=[csb, mc], writes=[t1], tag="lg_mcs")
        p.op("vector", lambda e, o=lg, a=raw, b=t1:
             e.tensor_tensor(o.ap, a.ap, b.ap, ALU.subtract),
             reads=[raw, t1], writes=[lg], tag="lg_sub")
        p.op("vector", lambda e, o=lg, i=lg, s=rc:
             e.tensor_scalar(o.ap, i.ap, s.ap, None, ALU.mult),
             reads=[lg, rc], writes=[lg], tag="lg_rs")
        nmax = smalls[:, 48:49]
        p.op("vector", lambda e, o=nmax, i=lg:
             e.reduce_max(o.ap, i.ap, AX.X, negate=True),
             reads=[lg], writes=[nmax], tag="lg_max")
        top = smalls[:, 49:50]
        p.op("vector", lambda e, o=top, i=nmax:
             e.tensor_scalar(o.ap, i.ap, -1.0, None, ALU.mult),
             reads=[nmax], writes=[top], tag="lg_top")
        mk = mask_tm[:, 0:8]
        p.op("vector", lambda e, o=mk, i=lg, s=top:
             e.tensor_scalar(o.ap, i.ap, s.ap, None, ALU.is_equal),
             reads=[lg, top], writes=[mk], tag="lg_eq")
        sc = smalls[:, 50:51]
        p.op("scalar", lambda e, o=sc, i=nmax:
             e.activation(o.ap, i.ap, AF.Sigmoid, scale=-1.0),
             reads=[nmax], writes=[sc], tag="lg_sig")
        psy = ctx.bank()
        pv = psy.tv[0:1, 0:128]
        p.op("tensor", lambda e, o=pv, a=sc, b=ctx.if32:
             e.matmul(o.ap, a.ap, b.ap, start=True, stop=True),
             reads=[sc, ctx.if32], writes=[pv], tag="mm_scT")
        o = srow_b[0:1, ti * 128:(ti + 1) * 128]
        p.op("vector", lambda e, o=o, i=pv: e.tensor_copy(o.ap, i.ap),
             reads=[pv], writes=[o], tag="ev_scT")
        mcb = mkb[:, 0:8]
        mk8 = mask_tm[:, 0:8]
        p.op("vector", lambda e, o=mcb, i=mk8: e.tensor_copy(o.ap, i.ap),
             reads=[mk8], writes=[mcb], tag="mk_cast")
        psz = ctx.bankbf()
        pv = psz.tv[0:8, 0:128]
        transpose128(ctx, pv, mcb)
        o = mask8[0:8, ti * 128:(ti + 1) * 128]
        p.op("vector", lambda e, o=o, i=pv: e.tensor_copy(o.ap, i.ap),
             reads=[pv], writes=[o], tag="ev_mkT")
    ctx.af.free("moe_msk")
    ctx.af.free("moe_lg")
    ctx.af.free("moe_gw")

    gbuf = ctx.ab.alloc("moe_gb", T)
    maskb = ctx.ab.alloc("moe_mb", T)
    act = [ctx.ab.alloc(f"moe_a{i}", T) for i in range(8)]
    ws = WStream(ctx, "moe_w", None, 8, 2 * D, nslots=2)

    def group4(wtile_fn, src_tiles, KT, evac_fn, tag):
        for n in range(NT):
            pv = ctx.bank().tv[:, 0:512]
            pairs = [(wtile_fn(kt), src_tiles[kt][:, n * 512:(n + 1) * 512])
                     for kt in range(KT)]
            matmul_acc(ctx, pv, pairs, tag=tag)
            evac_fn(n, pv)

    def expert(gu_dram, down_dram, src_tiles, masked):
        for g in range(2):
            for sub in range(8):
                mt = g * 8 + sub
                wgu = ws.load(gu_dram[mt])
                wp, wu = wgu[:, 0:D], wgu[:, D:2 * D]

                def ev_g(n, pv):
                    o = gbuf[:, n * 512:(n + 1) * 512]
                    p.op("scalar", lambda e, o=o, i=pv:
                         e.activation(o.ap, i.ap, AF.Silu),
                         reads=[pv], writes=[o], tag="ev_esilu")
                    if masked:
                        mb = maskb[:, n * 512:(n + 1) * 512]
                        p.op("vector", lambda e, o=o, a=o, b=mb:
                             e.tensor_tensor(o.ap, a.ap, b.ap, ALU.mult),
                             reads=[o, mb], writes=[o], tag="gm_mask")
                group4(lambda kt: wp[:, kt * 128:(kt + 1) * 128], src_tiles,
                       DT, ev_g, "mm_eg")

                def ev_u(n, pv):
                    oa = act[sub][:, n * 512:(n + 1) * 512]
                    o = gbuf[:, n * 512:(n + 1) * 512]
                    p.op("vector", lambda e, o=oa, a=pv, b=o:
                         e.tensor_tensor(o.ap, a.ap, b.ap, ALU.mult),
                         reads=[pv, o], writes=[oa], tag="ev_eact")
                group4(lambda kt: wu[:, kt * 128:(kt + 1) * 128], src_tiles,
                       DT, ev_u, "mm_eu")
            for mt in range(DT):
                wp = ws.load(down_dram[g][mt])

                def ev_d(n, pv, mt=mt):
                    xr = xres[mt][:, n * 512:(n + 1) * 512]
                    p.op("vector", lambda e, o=xr, i=pv:
                         e.tensor_tensor(o.ap, o.ap, i.ap, ALU.add),
                         reads=[pv, xr], writes=[xr], tag="res_moe")
                group4(lambda kt: wp[:, kt * 128:(kt + 1) * 128], act,
                       DT, ev_d, "mm_ed")

    expert(ctx.drams["moe_shgu"],
           [ctx.drams["moe_shd"][g] for g in range(2)], xnb, masked=False)
    xs = xnb
    for n in range(NT):
        psb_ = ctx.bank()
        pv = psb_.tv[:, 0:512]
        row = srow_b[0:1, n * 512:(n + 1) * 512]
        bcast_row(ctx, row, pv)
        for dt_ in range(DT):
            o = xs[dt_][:, n * 512:(n + 1) * 512]
            p.op("vector", lambda e, o=o, b=pv:
                 e.tensor_tensor(o.ap, o.ap, b.ap, ALU.mult),
                 reads=[o, pv], writes=[o], tag="xs_mul")
    for ei in range(8):
        sel = ctx.sel16[0:8, ei * 128:(ei + 1) * 128]
        for n in range(NT):
            psx = ctx.bank()
            pv = psx.tv[:, 0:512]
            rt = mask8[0:8, n * 512:(n + 1) * 512]
            matmul_acc(ctx, pv, [(sel, rt)], tag="mm_mb")
            o = maskb[:, n * 512:(n + 1) * 512]
            p.op("vector", lambda e, o=o, i=pv: e.tensor_copy(o.ap, i.ap),
                 reads=[pv], writes=[o], tag="ev_mb")
        expert(ctx.drams["moe_rtgu"][ei],
               [ctx.drams["moe_rtd"][ei][g] for g in range(2)], xs, masked=True)
    ws.release()
    for i in range(DT):
        ctx.ab.free(f"xnb{i}")
    for i in range(8):
        ctx.ab.free(f"moe_a{i}")
    for nm in ["moe_gb", "moe_mb", "moe_mkb", "moe_mask8"]:
        ctx.ab.free(nm)
    ctx.af.free("moe_sm")


def build_core_program(phases, host):
    nc = bass.Bass("TRN2", target_bir_lowering=False)
    p = Prog(nc)
    ctx = Ctx(p)

    def din(name, arr):
        t = nc.dram_tensor(name, list(arr.shape), F32, kind="ExternalInput").ap()
        ctx.drams[name] = t
        return t

    out_names = []

    def dout(name, shape):
        t = nc.dram_tensor(name, list(shape), F32, kind="ExternalOutput").ap()
        ctx.drams[name] = t
        out_names.append(name)
        return t

    wnames = set()
    for ph in phases:
        wnames.update(PHASE_WEIGHTS[ph])
    for name in sorted(wnames):
        din(name, host["dram_inputs"][name])
    din("xfm", np.zeros((D, T), np.float32))
    din("vp", host["vecpack"])

    need_xi_in = ("l0_attn" in phases or "l0_glu" in phases) \
        and "l0_mix" not in phases
    need_xi_out = ("l0_mix" in phases or "l0_attn" in phases) \
        and "l0_glu" not in phases
    need_x_out = "l0_glu" in phases or "l0_ffn" in phases \
        or "l1_mamba" in phases or "l1_moe" in phases
    if need_xi_in:
        din("xi_in", np.zeros((D, T), np.float32))
    if need_xi_out:
        dout("xi_out", (D, T))
    if need_x_out:
        dout("x_out", (D, T))

    AF_EL = max(PHASE_ARENA[ph][0] for ph in phases)
    AB_EL = max(PHASE_ARENA[ph][1] for ph in phases)
    af_buf = p.sbuf("arena_f32", 128, AF_EL, F32)
    ab_buf = p.sbuf("arena_bf16", 128, AB_EL, BF16)
    ctx.af = Arena(af_buf)
    ctx.ab = Arena(ab_buf)
    ctx.banks_f32 = [p.psum(f"pf{i}", 128, 512, F32) for i in range(6)]
    ctx.banks_bf = [p.psum(f"pb{i}", 128, 1024, BF16) for i in range(2)]
    vp_arr = host["vecpack"]
    ctx.vp = p.sbuf("vecpack", 128, vp_arr.shape[1], F32)
    ctx.vp_index = host["vp_index"]
    ctx.rowb = p.sbuf("rowb", 1, 3072, BF16)
    ctx.rowf = p.sbuf("rowf", 1, 2048, F32)
    if "cmask" in wnames:
        ctx.masks = p.sbuf("c_masks", 128, 512, BF16).tv
    if "csel" in wnames:
        ctx.sel16 = p.sbuf("c_sel16", 16, 2048, BF16).tv
    if "l1_fc1" in wnames:
        ctx.drams["xspill"] = nc.dram_tensor("xspill", [D, T], F32,
                                             kind="Internal").ap()
        ctx.drams["ysp"] = nc.dram_tensor("ysp", [2048, T], BF16,
                                          kind="Internal").ap()

    with nc.Block() as block:
        p.op("sync", lambda e: e.dma_start(out=ctx.vp.tv.ap, in_=ctx.drams["vp"]),
             writes=[ctx.vp.tv], dma="vp")
        emit_consts(ctx)
        if ctx.masks is not None:
            p.op("gpsimd", lambda e: e.dma_start(out=ctx.masks.ap,
                                                 in_=ctx.drams["cmask"]),
                 writes=[ctx.masks], dma="cmask")
        if ctx.sel16 is not None:
            p.op("gpsimd", lambda e: e.dma_start(out=ctx.sel16.ap,
                                                 in_=ctx.drams["csel"]),
                 writes=[ctx.sel16], dma="csel")

        st = {}
        xres = [ctx.af.alloc(f"xr{i}", T) for i in range(DT)]
        st["xres"] = xres
        for i in range(DT):
            src = ctx.drams["xfm"][i * 128:(i + 1) * 128, :]
            p.op("sync", lambda e, d=xres[i], s=src: e.dma_start(out=d.ap, in_=s),
                 writes=[xres[i]], dma=f"xload{i}")
        if need_xi_in:
            xinner = [ctx.ab.alloc(f"xi{i}", T) for i in range(DT)]
            st["xinner"] = xinner
            for i in range(DT):
                src = ctx.drams["xi_in"][i * 128:(i + 1) * 128, :]
                p.op("gpsimd", lambda e, d=xinner[i], s=src: e.dma_start(out=d.ap, in_=s),
                     writes=[xinner[i]], dma=f"xiload{i}")

        for ph in phases:
            PHASE_EMITTERS[ph](ctx, st)

        final_tvs = []
        if need_xi_out:
            for i in range(DT):
                dst = ctx.drams["xi_out"][i * 128:(i + 1) * 128, :]
                tv = st["xinner"][i]
                p.op("gpsimd", lambda e, d=dst, s=tv: e.dma_start(out=d, in_=s.ap),
                     reads=[tv], dma="st_out")
                final_tvs.append(tv)
        if need_x_out:
            for i in range(DT):
                dst = ctx.drams["x_out"][i * 128:(i + 1) * 128, :]
                tv = st["xres"][i]
                p.op("sync", lambda e, d=dst, s=tv: e.dma_start(out=d, in_=s.ap),
                     reads=[tv], dma="st_out")
                final_tvs.append(tv)
        n_st = sum(1 for o in p.ops if o.dma_sem == "st_out")
        p.op("sync", lambda e, n=n_st: e.wait_ge(SEMS["st_out"], 16 * n),
             tag="final_wait")

        global SEMS
        SEMS = {}
        sem_cms = []
        names = list(ENGINES) + p.dma_sems_used()
        for nm in names:
            cm = nc.semaphore(f"s_{nm}")
            SEMS[nm] = cm.__enter__()
            sem_cms.append(cm)
        p.emit(block, SEMS)
        for cm in reversed(sem_cms):
            cm.__exit__(None, None, None)
    p.close_bufs()
    return nc, out_names


SEMS = {}
PHASE_EMITTERS = {"l0_mix": emit_l0_mix, "l0_attn": emit_l0_attn,
                  "l0_glu": emit_l0_glu, "l0_ffn": emit_l0_ffn,
                  "l1_mamba": emit_l1_mamba, "l1_moe": emit_l1_moe}
PHASE_ARENA = {
    "l0_mix": (19000, 54000),
    "l0_attn": (17500, 53000),
    "l0_glu": (17500, 56000),
    "l0_ffn": (19000, 50000),
    "l1_mamba": (18500, 55000),
    "l1_moe": (19000, 47000),
}
PHASE_WEIGHTS = {
    "l0_mix": ["l0_gvd"],
    "l0_attn": ["l0_qkv3", "l0_wo", "cmask"],
    "l0_glu": ["l0_gu", "l0_wout0", "l0_wout1", "l0_wout2", "l0_wout3"],
    "l0_ffn": ["l0f_w"],
    "l1_mamba": ["l1_fc1", "l1_inproj", "l1_xpd", "l1_xpb", "l1_xpc",
                 "l1_dtp", "l1_outproj", "l1_fc2", "csel"],
    "l1_moe": ["moe_gate", "moe_shgu", "moe_shd", "moe_rtgu", "moe_rtd",
               "csel"],
}


def prep_host(inp):
    vp = VecPack()
    vp.add("w_rms_mix", inp["L0_attn_w_rms_mix"])
    vp.add("bg", inp["L0_attn_bg"])
    vp.add("bv", inp["L0_attn_bv"])
    vp.add("bd", inp["L0_attn_bd"])
    vp.add("w_rms_attn", inp["L0_attn_w_rms_attn"])
    vp.add("w_rms_ffn", inp["L0_attn_w_rms_ffn"])
    vp.add("eps_1e5", np.full(128, 1e-5, np.float32))
    vp.add("eps_1e6", np.full(128, 1e-6, np.float32))
    vp.add("one", np.full(128, 1.0, np.float32))
    vp.add("fc1_b", inp["L1_attn_fc1_b"])
    for w in range(4):
        vp.add(f"cw{w}", inp["L1_attn_conv_w"][:, w])
    vp.add("conv_b", inp["L1_attn_conv_b"])
    vp.add("dt_b", inp["L1_attn_dt_proj_b"])
    vp.add("Dvec", inp["L1_attn_D"])
    vp.add("fc2_b", inp["L1_attn_fc2_b"])
    al = np.asarray(inp["L1_attn_A_log"], np.float32)
    vp.add_image("Alog", al.reshape(16, 128, 16).transpose(1, 0, 2).reshape(128, 256))

    j = np.arange(128)[:, None]
    kk = np.arange(256)[None, :]
    maskN = np.where(kk < 128, (kk > j), (kk - 128) <= j).astype(np.float32)
    mask0 = np.where(kk < 128, False, (kk - 128) <= j).astype(np.float32)
    cmask = np.concatenate([maskN, mask0], axis=1)

    wout = inp["L0_attn_Wout"]
    _g = lhsT_panels(inp["L0_attn_Wg"])
    _v = lhsT_panels(inp["L0_attn_Wv"])
    _d = lhsT_panels(inp["L0_attn_Wd"])
    _qkv = lhsT_panels(inp["L0_attn_Wqkv"])
    _ga = lhsT_panels(inp["L0_attn_Wgate"])
    _up = lhsT_panels(inp["L0_attn_Wup"])
    dram = {
        "l0_gvd": np.ascontiguousarray(np.concatenate([_g, _v, _d], axis=2)),
        "l0_qkv3": np.ascontiguousarray(np.concatenate(
            [_qkv[0:8], _qkv[8:16], _qkv[16:24]], axis=2)),
        "l0_wo": lhsT_panels(inp["L0_attn_Wo"]),
        "l0_gu": np.ascontiguousarray(np.concatenate([_ga, _up], axis=2)),
        "l0_wout0": lhsT_panels(wout[:, 0:1024]),
        "l0_wout1": lhsT_panels(wout[:, 1024:2048]),
        "l0_wout2": lhsT_panels(wout[:, 2048:3072]),
        "l0_wout3": lhsT_panels(wout[:, 3072:4096]),
        "l0f_w": lhsT_panels(inp["L0_ffn_weight"]),
        "cmask": np.ascontiguousarray(cmask),
    }
    xpw = np.asarray(inp["L1_attn_x_proj_w"], np.float32)
    dtpw = np.asarray(inp["L1_attn_dt_proj_w"], np.float32)
    sel = np.zeros((16, 16, 128), np.float32)
    for i in range(16):
        sel[i, i, :] = 1.0
    dram.update({
        "l1_fc1": lhsT_panels(inp["L1_attn_fc1_w"]),
        "l1_inproj": lhsT_panels(inp["L1_attn_in_proj_w"]),
        "l1_xpd": np.ascontiguousarray(
            xpw[0:64].reshape(64, 16, 128).transpose(1, 2, 0)),
        "l1_xpb": np.ascontiguousarray(
            xpw[64:80].reshape(16, 16, 128).transpose(1, 2, 0)),
        "l1_xpc": np.ascontiguousarray(
            xpw[80:96].reshape(16, 16, 128).transpose(1, 2, 0)),
        "l1_dtp": np.ascontiguousarray(
            dtpw.reshape(16, 128, 64).transpose(0, 2, 1)),
        "l1_outproj": lhsT_panels(inp["L1_attn_out_proj_w"]),
        "l1_fc2": lhsT_panels(inp["L1_attn_fc2_w"]),
        "csel": np.ascontiguousarray(sel.reshape(16, 2048)),
    })
    shd = np.asarray(inp["L1_ffn_sh_down"], np.float32).T
    dram.update({
        "moe_gate": np.ascontiguousarray(
            np.asarray(inp["L1_ffn_gate_w"], np.float32).reshape(8, 128, 8)),
        "moe_shgu": np.concatenate(
            [lhsT_panels(np.asarray(inp["L1_ffn_sh_gate"], np.float32).T),
             lhsT_panels(np.asarray(inp["L1_ffn_sh_up"], np.float32).T)], axis=2),
        "moe_shd": np.stack([lhsT_panels(shd[:, g * 1024:(g + 1) * 1024])
                             for g in range(2)]),
        "moe_rtgu": np.stack([np.concatenate(
            [lhsT_panels(np.asarray(inp["L1_ffn_rt_gate"][e], np.float32).T),
             lhsT_panels(np.asarray(inp["L1_ffn_rt_up"][e], np.float32).T)],
            axis=2) for e in range(8)]),
        "moe_rtd": np.stack([np.stack([
            lhsT_panels(np.asarray(inp["L1_ffn_rt_down"][e], np.float32)
                        .T[:, g * 1024:(g + 1) * 1024]) for g in range(2)])
            for e in range(8)]),
    })
    host = {
        "dram_inputs": dram,
        "vecpack": vp.array(),
        "vp_index": vp.index,
    }
    return host


def run_launch(phases, host, per_core_io):
    nc, out_names = build_core_program(phases, host)
    wnames = set()
    for ph in phases:
        wnames.update(PHASE_WEIGHTS[ph])
    in_maps = []
    for c in range(NCORES):
        m = {name: host["dram_inputs"][name] for name in wnames}
        m["vp"] = host["vecpack"]
        m.update(per_core_io[c])
        in_maps.append(m)
    res = run_bass_kernel_spmd(nc, in_maps, core_ids=list(range(NCORES)))
    return res.results, out_names


NCORES = 8

LAUNCHES = [ALL_PHASES]


def kernel(**inputs):
    host = prep_host(inputs)
    x = np.asarray(inputs["L0_attn_x"], np.float32)
    xfm = np.ascontiguousarray(x.transpose(0, 2, 1))
    cur_x = [xfm[c] for c in range(NCORES)]
    cur_xi = [None] * NCORES
    for phases in LAUNCHES:
        io = []
        for c in range(NCORES):
            m = {"xfm": cur_x[c]}
            if ("l0_attn" in phases or "l0_glu" in phases)                     and "l0_mix" not in phases:
                m["xi_in"] = cur_xi[c]
            io.append(m)
        results, outs = run_launch(phases, host, io)
        for c in range(NCORES):
            if "xi_out" in outs:
                cur_xi[c] = results[c]["xi_out"]
            if "x_out" in outs:
                cur_x[c] = results[c]["x_out"]
    out = np.stack(cur_x)
    return np.ascontiguousarray(out.transpose(0, 2, 1))


# version 31 (final)
# speedup vs baseline: 1.1062x; 1.0374x over previous
import numpy as np
import ml_dtypes
import concourse.bass as bass
import concourse.mybir as mybir
from concourse.bass_utils import run_bass_kernel_spmd

F32 = mybir.dt.float32
BF16 = mybir.dt.bfloat16
AF = mybir.ActivationFunctionType
ALU = mybir.AluOpType
AX = mybir.AxisListType

B, T, D = 8, 2048, 1024
DT = D // 128
NT = T // 512
ENGINES = ("sync", "vector", "scalar", "tensor", "gpsimd")

ALL_PHASES = ("l0_mix", "l0_attn", "l0_glu", "l0_ffn", "l1_mamba", "l1_moe")


class Buf:
    __slots__ = ("name", "handle", "parts", "free", "dtype", "space")

    def __init__(self, name, handle, parts, free, dtype, space):
        self.name, self.handle = name, handle
        self.parts, self.free, self.dtype, self.space = parts, free, dtype, space

    @property
    def tv(self):
        return TV(self, 0, self.parts, 0, self.free)

    def __getitem__(self, sl):
        return self.tv[sl]


class TV:
    __slots__ = ("buf", "p0", "p1", "f0", "f1")

    def __init__(self, buf, p0, p1, f0, f1):
        self.buf, self.p0, self.p1, self.f0, self.f1 = buf, p0, p1, f0, f1
        assert 0 <= p0 < p1 <= buf.parts and 0 <= f0 < f1 <= buf.free, \
            f"bad TV {buf.name}[{p0}:{p1},{f0}:{f1}] of [{buf.parts},{buf.free}]"

    @property
    def ap(self):
        return self.buf.handle[self.p0:self.p1, self.f0:self.f1]

    @property
    def P(self):
        return self.p1 - self.p0

    @property
    def F(self):
        return self.f1 - self.f0

    def __getitem__(self, sl):
        psl, fsl = sl if isinstance(sl, tuple) else (sl, slice(None))
        p0, p1, st = psl.indices(self.P)
        assert st == 1
        f0, f1, st = fsl.indices(self.F)
        assert st == 1
        return TV(self.buf, self.p0 + p0, self.p0 + p1, self.f0 + f0, self.f0 + f1)

    def overlaps(self, o):
        return (self.buf is o.buf and self.p0 < o.p1 and o.p0 < self.p1
                and self.f0 < o.f1 and o.f0 < self.f1)

    def covers(self, o):
        return (self.buf is o.buf and self.p0 <= o.p0 and self.p1 >= o.p1
                and self.f0 <= o.f0 and self.f1 >= o.f1)


class Op:
    __slots__ = ("i", "eng", "fn", "reads", "writes", "dma_sem", "need_inc", "tag")

    def __init__(self, i, eng, fn, reads, writes, dma_sem, tag):
        self.i, self.eng, self.fn = i, eng, fn
        self.reads, self.writes, self.dma_sem = reads, writes, dma_sem
        self.need_inc = False
        self.tag = tag


class Arena:

    def __init__(self, buf, align=16):
        self.buf = buf
        self.align = align
        self.free_list = [(0, buf.free)]
        self.live = {}

    def alloc(self, name, elems):
        a = self.align
        elems = -(-elems // a) * a
        for idx, (lo, hi) in enumerate(self.free_list):
            lo_a = -(-lo // a) * a
            if hi - lo_a >= elems:
                if lo_a > lo:
                    self.free_list[idx] = (lo, lo_a)
                    self.free_list.insert(idx + 1, (lo_a + elems, hi))
                else:
                    self.free_list[idx] = (lo_a + elems, hi)
                    if self.free_list[idx][0] >= self.free_list[idx][1]:
                        del self.free_list[idx]
                self.live[name] = (lo_a, lo_a + elems)
                return TV(self.buf, 0, self.buf.parts, lo_a, lo_a + elems)
        raise MemoryError(
            f"arena {self.buf.name}: cannot alloc {elems} for {name}; "
            f"free={self.free_list} live={self.live}")

    def free(self, name):
        lo, hi = self.live.pop(name)
        self.free_list.append((lo, hi))
        self.free_list.sort()
        merged = []
        for seg in self.free_list:
            if merged and merged[-1][1] >= seg[0]:
                merged[-1] = (merged[-1][0], max(merged[-1][1], seg[1]))
            else:
                merged.append(list(seg))
        self.free_list = [tuple(s) for s in merged]


class Prog:
    def __init__(self, nc):
        self.nc = nc
        self.ops = []
        self.deps = []
        self.records = {}
        self._ctxs = []

    def sbuf(self, name, parts, free, dtype):
        cm = self.nc.sbuf_tensor(name, [parts, free], dtype)
        h = cm.__enter__()
        self._ctxs.append(cm)
        return Buf(name, h, parts, free, dtype, "sbuf")

    def psum(self, name, parts, free, dtype):
        cm = self.nc.psum_tensor(name, [parts, free], dtype)
        h = cm.__enter__()
        self._ctxs.append(cm)
        return Buf(name, h, parts, free, dtype, "psum")

    def close_bufs(self):
        for cm in reversed(self._ctxs):
            cm.__exit__(None, None, None)
        self._ctxs = []

    def op(self, eng, fn, reads=(), writes=(), dma=None, tag=""):
        i = len(self.ops)
        reads = tuple(tv.buf.tv if tv.buf.space == "psum" else tv for tv in reads)
        writes = tuple(tv.buf.tv if tv.buf.space == "psum" else tv for tv in writes)
        o = Op(i, eng, fn, tuple(reads), tuple(writes), dma, tag)
        dep = set()
        for tv in o.reads:
            for (rtv, w, _rd) in self.records.get(tv.buf.name, ()):
                if w is not None and tv.overlaps(rtv):
                    dep.add(w)
        for tv in o.writes:
            for (rtv, w, rd) in self.records.get(tv.buf.name, ()):
                if tv.overlaps(rtv):
                    if w is not None:
                        dep.add(w)
                    dep.update(rd)
        for tv in o.reads:
            recs = self.records.setdefault(tv.buf.name, [])
            hit = False
            for r in recs:
                if r[0].covers(tv) and r[1] is not None:
                    r[2].append(i)
                    hit = True
            if not hit:
                recs.append([tv, None, [i]])
        for tv in o.writes:
            recs = self.records.setdefault(tv.buf.name, [])
            recs[:] = [r for r in recs if not tv.covers(r[0])]
            recs.append([tv, i, []])
        dep.discard(i)
        self.ops.append(o)
        self.deps.append(dep)
        return i

    def dep(self, a, b):
        self.deps[b].add(a)

    def barrier_read(self, tvs, eng="sync"):
        return self.op(eng, lambda e: e.nop(), reads=tvs, tag="barrier")

    def emit(self, block, sems):
        ops, deps = self.ops, self.deps
        for o in ops:
            for d in deps[o.i]:
                p = ops[d]
                if p.dma_sem is None and p.eng == "tensor" and o.eng == "tensor":
                    continue
                if p.dma_sem is None:
                    p.need_inc = True
        inc_count = {e: 0 for e in ENGINES}
        dma_count = {}
        prod_val = {}
        for o in ops:
            if o.dma_sem is not None:
                c = dma_count.get(o.dma_sem, 0) + 1
                dma_count[o.dma_sem] = c
                prod_val[o.i] = (o.dma_sem, 16 * c)
            elif o.need_inc:
                inc_count[o.eng] += 1
                prod_val[o.i] = (o.eng, inc_count[o.eng])
            else:
                prod_val[o.i] = None
        waits = []
        for o in ops:
            w = {}
            for d in deps[o.i]:
                p = ops[d]
                if p.dma_sem is None and p.eng == "tensor" and o.eng == "tensor":
                    continue
                s, v = prod_val[d]
                if w.get(s, -1) < v:
                    w[s] = v
            waits.append(w)
        self.n_incs = dict(inc_count)
        self.n_dma = dict(dma_count)

        def body_for(engine):
            def body(eng):
                hw = {}
                for o in ops:
                    if o.eng != engine:
                        continue
                    for s, v in sorted(waits[o.i].items()):
                        if hw.get(s, 0) >= v:
                            continue
                        eng.wait_ge(sems[s], v)
                        hw[s] = v
                    inst = o.fn(eng)
                    if o.dma_sem is not None:
                        inst.then_inc(sems[o.dma_sem], 16)
                    elif o.need_inc:
                        inst.then_inc(sems[o.eng], 1)
            return body

        for e in ENGINES:
            getattr(block, e)(body_for(e))

    def dma_sems_used(self):
        return sorted({o.dma_sem for o in self.ops if o.dma_sem is not None})


def lhsT_panels(w_out_in):
    O, I = w_out_in.shape
    assert O % 128 == 0 and I % 128 == 0, (O, I)
    MT, KT = O // 128, I // 128
    a = w_out_in.reshape(MT, 128, KT, 128).transpose(0, 3, 2, 1)
    return np.ascontiguousarray(a.reshape(MT, 128, KT * 128), dtype=np.float32)


def lhsT_panels_rows(w_out_in, row_lo, row_hi):
    w = w_out_in[row_lo:row_hi]
    M, I = w.shape
    KT = I // 128
    a = w.reshape(M, KT, 128).transpose(2, 1, 0)
    return np.ascontiguousarray(a.reshape(1, 128, KT * M), dtype=np.float32)


def rhs_panels(w_out_in):
    O, I = w_out_in.shape
    assert I % 128 == 0
    KT = I // 128
    return np.ascontiguousarray(w_out_in.T.reshape(KT, 128, O), dtype=np.float32)


def vec_image(v):
    v = np.asarray(v, dtype=np.float32).ravel()
    n = v.size // 128
    assert v.size == n * 128, v.size
    return np.ascontiguousarray(v.reshape(n, 128).T)


class VecPack:
    def __init__(self):
        self.cols = []
        self.index = {}
        self.n = 0

    def add(self, name, v):
        img = vec_image(v)
        self.index[name] = (self.n, img.shape[1])
        self.cols.append(img)
        self.n += img.shape[1]

    def add_image(self, name, img):
        img = np.ascontiguousarray(np.asarray(img, np.float32))
        assert img.shape[0] == 128
        self.index[name] = (self.n, img.shape[1])
        self.cols.append(img)
        self.n += img.shape[1]

    def add_scalar_slot(self, name):
        self.index[name] = (self.n, 1)
        self.cols.append(np.zeros((128, 1), np.float32))
        self.n += 1

    def array(self):
        return np.ascontiguousarray(np.concatenate(self.cols, axis=1))


class Ctx:
    def __init__(self, prog):
        self.p = prog
        self.af = None
        self.ab = None
        self.banks_f32 = []
        self.banks_bf = []
        self._rrf = 0
        self._rrb = 0
        self.vp = None
        self.of32 = None
        self.if32 = None
        self.sel16 = None
        self.masks = None
        self.vp_index = {}
        self.ones = None
        self.ident = None
        self.rowb = None
        self.rowf = None
        self.drams = {}

    def bank(self):
        b = self.banks_f32[self._rrf % len(self.banks_f32)]
        self._rrf += 1
        return b

    def bankbf(self):
        b = self.banks_bf[self._rrb % len(self.banks_bf)]
        self._rrb += 1
        return b

    def vcol(self, name, j=0):
        lo, n = self.vp_index[name]
        assert j < n, (name, j, n)
        return self.vp[:, lo + j:lo + j + 1]

    def vcols(self, name, j0, j1):
        lo, n = self.vp_index[name]
        assert j1 <= n
        return self.vp[:, lo + j0:lo + j1]


def emit_consts(ctx):
    p = ctx.p
    ones = p.sbuf("c_ones", 128, 128, BF16)
    ident = p.sbuf("c_ident", 128, 128, BF16)
    of32 = p.sbuf("c_of32", 128, 128, F32)
    ctx.ones, ctx.ident, ctx.of32 = ones, ident, of32.tv
    p.op("vector", lambda e: e.memset(ones.tv.ap, 1.0), writes=[ones.tv])
    p.op("vector", lambda e: e.memset(of32.tv.ap, 1.0), writes=[of32.tv])
    p.op("vector", lambda e: e.memset(ident.tv.ap, 1.0), writes=[ident.tv])
    p.op("gpsimd",
         lambda e: e.affine_select(ident.tv.ap, ident.tv.ap, [[1, 128]],
                                   ALU.is_equal, 0.0, base=0,
                                   channel_multiplier=-1),
         reads=[ident.tv], writes=[ident.tv])
    if32 = p.sbuf("c_if32", 128, 128, F32)
    ctx.if32 = if32.tv
    p.op("vector", lambda e: e.memset(if32.tv.ap, 1.0), writes=[if32.tv])
    p.op("gpsimd",
         lambda e: e.affine_select(if32.tv.ap, if32.tv.ap, [[1, 128]],
                                   ALU.is_equal, 0.0, base=0,
                                   channel_multiplier=-1),
         reads=[if32.tv], writes=[if32.tv])


class WStream:

    def __init__(self, ctx, name, dram_ap, n_panels, panel_free, nslots=4):
        self.ctx, self.name = ctx, name
        self.dram = dram_ap
        self.n_panels, self.pf = n_panels, panel_free
        self.slots = [ctx.ab.alloc(f"{name}_s{i}", panel_free) for i in range(nslots)]
        self.nslots = nslots
        self.count = 0

    def panel(self, i):
        return self.load(self.dram[i])

    def load(self, src_ap):
        ctx = self.ctx
        slot = self.slots[self.count % self.nslots]
        sem = f"w_{self.name}_{self.count % self.nslots}"
        self.count += 1
        shp = list(src_ap.shape)
        fw = int(np.prod(shp[1:])) if len(shp) > 1 else int(shp[0])
        assert fw <= slot.F, (self.name, fw, slot.F)
        dst = slot[:, 0:fw] if fw < slot.F else slot
        ctx.p.op("gpsimd", lambda e, d=dst, s=src_ap: e.dma_start(out=d.ap, in_=s),
                 writes=[dst], dma=sem, tag="wload")
        return dst

    def release(self):
        for i in range(self.nslots):
            self.ctx.ab.free(f"{self.name}_s{i}")


def matmul_acc(ctx, psum_tv, pairs, tag="mm"):
    p = ctx.p
    n = len(pairs)
    for k, (lt, rt) in enumerate(pairs):
        p.op("tensor",
             lambda e, o=psum_tv, a=lt, b=rt, s=(k == 0), t=(k == n - 1):
                 e.matmul(o.ap, a.ap, b.ap, start=s, stop=t),
             reads=[lt, rt], writes=[psum_tv], tag=tag)


def transpose128(ctx, psum_bf_tv, src_tv):
    ident = ctx.ident
    ctx.p.op("tensor",
             lambda e, o=psum_bf_tv, i=src_tv, idt=ident.tv:
                 e.transpose(o.ap, i.ap, idt.ap),
             reads=[src_tv, ident.tv], writes=[psum_bf_tv], tag="transp")


def bcast_row(ctx, row_tv, out_psum_tv, tag="bcast"):
    lhsT = ctx.ones[0:1, 0:out_psum_tv.P]
    ctx.p.op("tensor",
             lambda e, o=out_psum_tv, l=lhsT, r=row_tv:
                 e.matmul(o.ap, l.ap, r.ap, start=True, stop=True),
             reads=[row_tv, lhsT], writes=[out_psum_tv], tag=tag)


def emit_norm(ctx, src_tiles, out_tiles, kind, gamma=None, eps=1e-5,
              out2_tiles=None, eng_apply="vector", exact=False, keep_rows=None,
              keep_cols=None):
    p = ctx.p
    DTl = len(src_tiles)
    Dl = DTl * 128
    src_bf = src_tiles[0].buf.dtype == BF16
    tb = [ctx.ab.alloc(f"nrm_tb{i}", 512) for i in range(2)]
    if exact:
        sq = [ctx.af.alloc(f"nrm_sq{i}", 512) for i in range(2)]
        onecol = ctx.of32[:, 0:1]
    else:
        sq = [ctx.ab.alloc(f"nrm_sq{i}", 512) for i in range(2)]
        onecol = ctx.ones[:, 0:1]
    d1 = [ctx.af.alloc(f"nrm_d1{i}", 512) for i in range(2)]
    rowb, rowf = ctx.rowb, ctx.rowf
    ebias = ctx.vcol("eps_1e5" if eps == 1e-5 else "eps_1e6")[0:1, :]
    Tl = src_tiles[0].F
    NTl = Tl // 512
    for n in range(NTl):
        t0, t1 = n * 512, (n + 1) * 512
        ps_sum = ctx.bank()
        ps_sumsq = ctx.bank()
        sum_tv = ps_sum.tv[0:1, 0:512]
        sumsq_tv = ps_sumsq.tv[0:1, 0:512]
        for dt_ in range(DTl):
            s = src_tiles[dt_][:, t0:t1]
            if kind == "ln":
                if exact:
                    xb = s
                elif src_bf:
                    xb = s
                else:
                    xb = tb[dt_ % 2]
                    p.op("vector", lambda e, o=xb, i=s: e.tensor_copy(o.ap, i.ap),
                         reads=[s], writes=[xb], tag="nrm_cast")
                p.op("tensor",
                     lambda e, o=sum_tv, a=onecol, b=xb, s_=(dt_ == 0),
                            t_=(dt_ == DTl - 1):
                         e.matmul(o.ap, a.ap, b.ap, start=s_, stop=t_),
                     reads=[onecol, xb], writes=[sum_tv], tag="nrm_mmx")
            sqt = sq[dt_ % 2]
            p.op("scalar", lambda e, o=sqt, i=s: e.activation(o.ap, i.ap, AF.Square),
                 reads=[s], writes=[sqt], tag="nrm_sq")
            p.op("tensor",
                 lambda e, o=sumsq_tv, a=onecol, b=sqt, s_=(dt_ == 0),
                        t_=(dt_ == DTl - 1):
                     e.matmul(o.ap, a.ap, b.ap, start=s_, stop=t_),
                 reads=[onecol, sqt], writes=[sumsq_tv], tag="nrm_mmsq")
        r0 = rowf.tv[0:1, 0:512]
        r1 = rowf.tv[0:1, 512:1024]
        r2 = rowf.tv[0:1, 1024:1536]
        r3 = rowf.tv[0:1, 1536:2048]
        if kind == "ln":
            p.op("vector", lambda e, o=r2, i=sum_tv:
                 e.tensor_scalar(o.ap, i.ap, 1.0 / Dl, None, ALU.mult),
                 reads=[sum_tv], writes=[r2], tag="nrm_m")
            p.op("scalar", lambda e, o=r3, i=r2: e.activation(o.ap, i.ap, AF.Square),
                 reads=[r2], writes=[r3], tag="nrm_msq")
            p.op("vector", lambda e, o=r0, i=sumsq_tv, s2=r3:
                 e.scalar_tensor_tensor(o.ap, i.ap, 1.0 / Dl, s2.ap,
                                        ALU.mult, ALU.subtract),
                 reads=[sumsq_tv, r3], writes=[r0], tag="nrm_var")
        else:
            p.op("vector", lambda e, o=r0, i=sumsq_tv:
                 e.tensor_scalar(o.ap, i.ap, 1.0 / Dl, None, ALU.mult),
                 reads=[sumsq_tv], writes=[r0], tag="nrm_var")
        p.op("scalar", lambda e, o=r1, i=r0, b=ebias:
             e.activation(o.ap, i.ap, AF.Sqrt, bias=b.ap),
             reads=[r0, ebias], writes=[r1], tag="nrm_sd")
        p.op("vector", lambda e, o=r3, i=r1: e.reciprocal(o.ap, i.ap),
             reads=[r1], writes=[r3], tag="nrm_rstd")
        rs_b = rowb.tv[0:1, 0:512]
        p.op("vector", lambda e, o=rs_b, i=r3: e.tensor_copy(o.ap, i.ap),
             reads=[r3], writes=[rs_b], tag="nrm_rb")
        if keep_rows is not None:
            km, kr = keep_rows
            o_ = kr[0:1, t0:t1]
            p.op("vector", lambda e, o=o_, i=r3: e.tensor_copy(o.ap, i.ap),
                 reads=[r3], writes=[o_], tag="nrm_keepr")
            o_ = km[0:1, t0:t1]
            p.op("vector", lambda e, o=o_, i=r2: e.tensor_copy(o.ap, i.ap),
                 reads=[r2], writes=[o_], tag="nrm_keepm")
        if keep_cols is not None:
            kmc, krc = keep_cols
            for ti in range(4):
                for (row, cols) in ((r2, kmc), (r3, krc)):
                    psy = ctx.bank()
                    pc = psy.tv[:, 0:1]
                    lt = row[0:1, ti * 128:(ti + 1) * 128]
                    p.op("tensor", lambda e, o=pc, a=lt, b=ctx.of32[0:1, 0:1]:
                         e.matmul(o.ap, a.ap, b.ap, start=True, stop=True),
                         reads=[lt, ctx.of32[0:1, 0:1]], writes=[pc], tag="mm_kc")
                    o_ = cols[:, n * 4 + ti:n * 4 + ti + 1]
                    p.op("vector", lambda e, o=o_, i=pc: e.tensor_copy(o.ap, i.ap),
                         reads=[pc], writes=[o_], tag="ev_kc")
        ps_r = ctx.bank()
        rB = ps_r.tv[:, 0:512]
        bcast_row(ctx, rs_b, rB)
        if kind == "ln":
            m_b = rowb.tv[0:1, 512:1024]
            p.op("vector", lambda e, o=m_b, i=r2: e.tensor_copy(o.ap, i.ap),
                 reads=[r2], writes=[m_b], tag="nrm_mb")
            ps_m = ctx.bank()
            mB = ps_m.tv[:, 0:512]
            bcast_row(ctx, m_b, mB)
        for dt_ in range(DTl):
            s = src_tiles[dt_][:, t0:t1]
            o = out_tiles[dt_][:, t0:t1]
            if kind == "ln":
                dd = d1[dt_ % 2]
                p.op(eng_apply, lambda e, a=s, b=mB, oo=dd:
                     e.tensor_tensor(oo.ap, a.ap, b.ap, ALU.subtract),
                     reads=[s, mB], writes=[dd], tag="nrm_sub")
                p.op(eng_apply, lambda e, a=dd, b=rB, oo=o:
                     e.tensor_tensor(oo.ap, a.ap, b.ap, ALU.mult),
                     reads=[dd, rB], writes=[o], tag="nrm_mul")
                if out2_tiles is not None:
                    o2 = out2_tiles[dt_][:, t0:t1]
                    p.op(eng_apply, lambda e, a=dd, b=rB, oo=o2:
                         e.tensor_tensor(oo.ap, a.ap, b.ap, ALU.mult),
                         reads=[dd, rB], writes=[o2], tag="nrm_mul2")
            else:
                g = ctx.vcol(gamma, dt_)
                p.op(eng_apply, lambda e, a=s, sc=g, b=rB, oo=o:
                     e.scalar_tensor_tensor(oo.ap, a.ap, sc.ap, b.ap,
                                            ALU.mult, ALU.mult),
                     reads=[s, rB, g], writes=[o], tag="nrm_app")
                if out2_tiles is not None:
                    o2 = out2_tiles[dt_][:, t0:t1]
                    p.op(eng_apply, lambda e, a=s, sc=g, b=rB, oo=o2:
                         e.scalar_tensor_tensor(oo.ap, a.ap, sc.ap, b.ap,
                                                ALU.mult, ALU.mult),
                         reads=[s, rB, g], writes=[o2], tag="nrm_app2")
    for i in range(2):
        ctx.ab.free(f"nrm_tb{i}")
        (ctx.af if exact else ctx.ab).free(f"nrm_sq{i}")
        ctx.af.free(f"nrm_d1{i}")


def proj_fm(ctx, in_tiles, wstream, MT, evac, panel_order=None, tag="proj"):
    KT = len(in_tiles)
    order = panel_order if panel_order is not None else range(MT)
    for mt in order:
        w = wstream.panel(mt)
        for n in range(NT):
            ps = ctx.bank()
            pv = ps.tv[:, 0:512]
            pairs = [(w[:, kt * 128:(kt + 1) * 128],
                      in_tiles[kt][:, n * 512:(n + 1) * 512]) for kt in range(KT)]
            matmul_acc(ctx, pv, pairs, tag=tag)
            evac(mt, n, pv)


def emit_l0_mix(ctx, st):
    p = ctx.p
    xres = st["xres"]
    xinner = [ctx.ab.alloc(f"xi{i}", T) for i in range(DT)]
    st["xinner"] = xinner
    emit_norm(ctx, xres, xinner, "ln", eps=1e-5)

    xnb = [ctx.ab.alloc(f"xnb{i}", T) for i in range(DT)]
    emit_norm(ctx, xinner, xnb, "rms", gamma="w_rms_mix", eps=1e-6)

    sig = ctx.ab.alloc("sig_t", T)
    tnh = ctx.ab.alloc("tanh_t", T)
    xs = [ctx.ab.alloc(f"xs{i}", T) for i in range(2)]
    a_f = [ctx.af.alloc("a0", T)]
    h = [ctx.ab.alloc(f"h{i}", T) for i in range(2)]
    ws = WStream(ctx, "ws", None, DT, 3 * D, nslots=2)

    def mm_evac(wpanel, src_tiles, n, act, bname, mt, out_tile):
        ps = ctx.bank()
        pv = ps.tv[:, 0:512]
        pairs = [(wpanel[:, kt * 128:(kt + 1) * 128],
                  src_tiles[kt][:, n * 512:(n + 1) * 512]) for kt in range(DT)]
        matmul_acc(ctx, pv, pairs)
        o = out_tile[:, n * 512:(n + 1) * 512]
        bcol = ctx.vcol(bname, mt)
        p.op("scalar", lambda e, oo=o, i=pv, b=bcol:
             e.activation(oo.ap, i.ap, act, bias=b.ap),
             reads=[pv, bcol], writes=[o], tag="evac")

    for mt in range(DT):
        wall = ws.load(ctx.drams["l0_gvd"][mt])
        wgp, wvp, wdp = wall[:, 0:D], wall[:, D:2 * D], wall[:, 2 * D:3 * D]
        for n in range(NT):
            mm_evac(wgp, xnb, n, AF.Sigmoid, "bg", mt, sig)
        for n in range(NT):
            mm_evac(wvp, xnb, n, AF.Tanh, "bv", mt, tnh)
        xst = xs[mt % 2]
        p.op("vector", lambda e, oo=xst, x1=sig, x2=tnh:
             e.tensor_tensor(oo.ap, x1.ap, x2.ap, ALU.mult),
             reads=[sig, tnh], writes=[xst], tag="xs_mul")
        for n in range(NT):
            mm_evac(wdp, xnb, n, AF.Sigmoid, "bd", mt, sig)
        at = a_f[0]
        p.op("vector", lambda e, oo=at, i=sig:
             e.tensor_scalar(oo.ap, i.ap, 0.998, 0.001, ALU.mult, ALU.add),
             reads=[sig], writes=[at], tag="a_ts")
        ht = h[mt % 2]
        p.op("vector", lambda e, o=ht, a=at, x=xst:
             e.tensor_tensor_scan(o.ap, a.ap, x.ap, 0.0, ALU.mult, ALU.add),
             reads=[at, xst], writes=[ht], tag="scan")
        xi = xinner[mt]
        p.op("vector", lambda e, o=xi, a=xi, b=ht:
             e.tensor_tensor(o.ap, a.ap, b.ap, ALU.add),
             reads=[xi, ht], writes=[xi], tag="res_add")
    ws.release()
    ctx.ab.free("sig_t")
    ctx.ab.free("tanh_t")
    ctx.af.free("a0")
    for i in range(2):
        ctx.ab.free(f"xs{i}")
        ctx.ab.free(f"h{i}")
    for i in range(DT):
        ctx.ab.free(f"xnb{i}")


def emit_l0_attn(ctx, st):
    p = ctx.p
    xinner = st["xinner"]
    xnb = [ctx.ab.alloc(f"xnb{i}", T) for i in range(DT)]
    emit_norm(ctx, xinner, xnb, "rms", gamma="w_rms_attn", eps=1e-6)

    qb2 = [ctx.ab.alloc(f"at_q{i}", 512) for i in range(2)]
    kb2 = [ctx.ab.alloc(f"at_k{i}", 640) for i in range(2)]
    vb2 = [ctx.ab.alloc(f"at_v{i}", 640) for i in range(2)]
    prob = [ctx.ab.alloc(f"at_pr{i}", 256) for i in range(8)]
    ao_tm = ctx.ab.alloc("at_ao", 4096)
    ao_fm = ctx.ab.alloc("at_af", 4096)
    attT = [ao_fm[:, i * 256:(i + 1) * 256] for i in range(8)]
    acol = ctx.af.alloc("at_acol", 64)
    wqkv = WStream(ctx, "wqkv", None, 8, 3 * D, nslots=2)
    wos = WStream(ctx, "wos", None, 8, D, nslots=2)
    masks = ctx.masks

    for tg in range(NT):
        t0 = tg * 512

        def emit_qkv(hp):
            qb, kb, vb = qb2[hp % 2], kb2[hp % 2], vb2[hp % 2]
            w3 = wqkv.load(ctx.drams["l0_qkv3"][hp])
            wq = w3[:, 0:D]
            ps = ctx.bank()
            pv = ps.tv[:, 0:512]
            pairs = [(wq[:, kt * 128:(kt + 1) * 128],
                      xnb[kt][:, t0:t0 + 512]) for kt in range(DT)]
            matmul_acc(ctx, pv, pairs, tag="mm_q")
            p.op("vector", lambda e, o=qb, i=pv: e.tensor_copy(o.ap, i.ap),
                 reads=[pv], writes=[qb], tag="ev_q")
            wk = w3[:, D:2 * D]
            if tg == 0:
                p.op("vector", lambda e, o=kb[:, 0:128]: e.memset(o.ap, 0.0),
                     writes=[kb[:, 0:128]], tag="kpad0")
                kranges = [(128, t0, 512)]
            else:
                kranges = [(0, t0 - 128, 512), (512, t0 + 384, 128)]
            for (df, src_t, ln) in kranges:
                ps = ctx.bank()
                pv = ps.tv[:, 0:ln]
                pairs = [(wk[:, kt * 128:(kt + 1) * 128],
                          xnb[kt][:, src_t:src_t + ln]) for kt in range(DT)]
                matmul_acc(ctx, pv, pairs, tag="mm_k")
                o = kb[:, df:df + ln]
                p.op("vector", lambda e, o=o, i=pv: e.tensor_copy(o.ap, i.ap),
                     reads=[pv], writes=[o], tag="ev_k")
            wv = w3[:, 2 * D:3 * D]
            if tg == 0:
                p.op("vector", lambda e, o=vb[:, 0:128]: e.memset(o.ap, 0.0),
                     writes=[vb[:, 0:128]], tag="vpad0")
                vtis = range(1, 5)
            else:
                vtis = range(0, 5)
            for vi in vtis:
                tt = t0 - 128 + vi * 128
                ps = ctx.bank()
                pv = ps.tv[:, 0:128]
                pairs = [(xnb[kt][:, tt:tt + 128],
                          wv[:, kt * 128:(kt + 1) * 128]) for kt in range(DT)]
                matmul_acc(ctx, pv, pairs, tag="mm_v")
                o = vb[:, vi * 128:(vi + 1) * 128]
                p.op("vector", lambda e, o=o, i=pv: e.tensor_copy(o.ap, i.ap),
                     reads=[pv], writes=[o], tag="ev_v")
            return qb, kb, vb

        def emit_chains(hp, qb, kb, vb):
            chains = [(cl, hh) for cl in range(4) for hh in range(2)]
            scs = {}

            def stage_A(ci):
                cl, hh = chains[ci]
                h0 = hh * 64
                q_tv = qb[h0:h0 + 64, cl * 128:(cl + 1) * 128]
                k_tv = kb[h0:h0 + 64, cl * 128:cl * 128 + 256]
                ps = ctx.banks_f32[ci % 4]
                sc = ps.tv[:, (ci // 4) * 256:(ci // 4) * 256 + 256]
                scs[ci] = sc
                p.op("tensor", lambda e, o=sc, a=q_tv, b=k_tv:
                     e.matmul(o.ap, a.ap, b.ap, start=True, stop=True),
                     reads=[q_tv, k_tv], writes=[sc], tag="mm_sc")

            for ci in range(8):
                stage_A(ci)
            for grp in range(1):
                cis = list(range(8))
                meta = {}
                for ci in cis:
                    cl, hh = chains[ci]
                    skip_prev = (tg == 0 and cl == 0)
                    meta[ci] = (cl, hh * 64, ci,
                                [1] if skip_prev else [0, 1])
                for ci in cis:
                    _, _, gi, _ = meta[ci]
                    nmax = acol[:, gi * 4:gi * 4 + 1]
                    p.op("vector", lambda e, o=nmax, i=scs[ci]:
                         e.reduce_max(o.ap, i.ap, AX.X, negate=True),
                         reads=[scs[ci]], writes=[nmax], tag="sc_max")
                for ci in cis:
                    _, _, gi, _ = meta[ci]
                    nmax = acol[:, gi * 4:gi * 4 + 1]
                    bias = acol[:, gi * 4 + 1:gi * 4 + 2]
                    p.op("vector", lambda e, o=bias, i=nmax:
                         e.tensor_scalar(o.ap, i.ap, 0.125, None, ALU.mult),
                         reads=[nmax], writes=[bias], tag="sc_bias")
                for ci in cis:
                    _, _, gi, _ = meta[ci]
                    bias = acol[:, gi * 4 + 1:gi * 4 + 2]
                    p.op("scalar", lambda e, o=prob[gi], i=scs[ci], b=bias:
                         e.activation(o.ap, i.ap, AF.Exp, bias=b.ap,
                                      scale=0.125),
                         reads=[scs[ci], bias], writes=[prob[gi]],
                         tag="sc_exp")
                for ci in cis:
                    cl, _, gi, _ = meta[ci]
                    mk = masks[:, 256:512] if (tg == 0 and cl == 0) \
                        else masks[:, 0:256]
                    p.op("gpsimd", lambda e, o=prob[gi], m=mk:
                         e.tensor_tensor(o.ap, o.ap, m.ap, ALU.mult),
                         reads=[prob[gi], mk], writes=[prob[gi]],
                         tag="sc_mask")
                for ci in cis:
                    _, _, gi, _ = meta[ci]
                    s_ = acol[:, gi * 4 + 2:gi * 4 + 3]
                    p.op("vector", lambda e, o=s_, i=prob[gi]:
                         e.reduce_sum(o.ap, i.ap, AX.X),
                         reads=[prob[gi]], writes=[s_], tag="sc_sum")
                for ci in cis:
                    _, _, gi, _ = meta[ci]
                    s_ = acol[:, gi * 4 + 2:gi * 4 + 3]
                    r_ = acol[:, gi * 4 + 3:gi * 4 + 4]
                    p.op("vector", lambda e, o=r_, i=s_:
                         e.reciprocal(o.ap, i.ap),
                         reads=[s_], writes=[r_], tag="sc_rec")
                pbs = [ctx.bankbf(), ctx.bankbf()]
                for ci in cis:
                    _, _, gi, halves = meta[ci]
                    for hf in halves:
                        dst_ps = pbs[gi // 4].tv[:, (gi % 4) * 256 + hf * 128:
                                                 (gi % 4) * 256 + (hf + 1) * 128]
                        transpose128(ctx, dst_ps, prob[gi][:, hf * 128:
                                                           (hf + 1) * 128])
                for ci in cis:
                    _, _, gi, halves = meta[ci]
                    for hf in halves:
                        dst_ps = pbs[gi // 4].tv[:, (gi % 4) * 256 + hf * 128:
                                                 (gi % 4) * 256 + (hf + 1) * 128]
                        dtv = attT[gi][:, hf * 128:(hf + 1) * 128]
                        p.op("vector", lambda e, o=dtv, i=dst_ps:
                             e.tensor_copy(o.ap, i.ap),
                             reads=[dst_ps], writes=[dtv], tag="ev_aT")
                po = ctx.banks_f32[5]
                for ci in cis:
                    cl, h0, gi, halves = meta[ci]
                    ov = po.tv[:, gi * 64:(gi + 1) * 64]
                    mms = []
                    for hf in halves:
                        lt = attT[gi][:, hf * 128:(hf + 1) * 128]
                        rt = vb[:, cl * 128 + hf * 128:
                                cl * 128 + hf * 128 + 128][:, h0:h0 + 64]
                        mms.append((lt, rt))
                    matmul_acc(ctx, ov, mms, tag="mm_pv")
                for ci in cis:
                    cl, h0, gi, _ = meta[ci]
                    ov = po.tv[:, gi * 64:(gi + 1) * 64]
                    r_ = acol[:, gi * 4 + 3:gi * 4 + 4]
                    o = ao_tm[:, cl * 1024 + hp * 128 + h0:
                              cl * 1024 + hp * 128 + h0 + 64]
                    p.op("vector", lambda e, o=o, i=ov, r=r_:
                         e.tensor_scalar(o.ap, i.ap, r.ap, None, ALU.mult),
                         reads=[ov, r_], writes=[o], tag="ev_o")

        prev = None
        for hp in range(8):
            cur = emit_qkv(hp)
            if prev is not None:
                emit_chains(hp - 1, *prev)
            prev = cur
        emit_chains(7, *prev)

        for cl in range(4):
            for dt_ in range(DT):
                pb = ctx.bankbf()
                dst_ps = pb.tv[:, 0:128]
                src = ao_tm[:, cl * 1024 + dt_ * 128:cl * 1024 + (dt_ + 1) * 128]
                transpose128(ctx, dst_ps, src)
                dtv = ao_fm[:, dt_ * 512 + cl * 128:dt_ * 512 + (cl + 1) * 128]
                p.op("vector", lambda e, o=dtv, i=dst_ps: e.tensor_copy(o.ap, i.ap),
                     reads=[dst_ps], writes=[dtv], tag="ev_aofm")
        for mt in range(DT):
            wop = wos.load(ctx.drams["l0_wo"][mt])
            ps = ctx.bank()
            pv = ps.tv[:, 0:512]
            pairs = [(wop[:, kt * 128:(kt + 1) * 128],
                      ao_fm[:, kt * 512:kt * 512 + 512]) for kt in range(DT)]
            matmul_acc(ctx, pv, pairs, tag="mm_wo")
            xi = xinner[mt][:, t0:t0 + 512]
            p.op("vector", lambda e, o=xi, i=pv:
                 e.tensor_tensor(o.ap, o.ap, i.ap, ALU.add),
                 reads=[pv, xi], writes=[xi], tag="res_attn")
    wqkv.release()
    wos.release()
    for nm in (["at_q0", "at_q1", "at_k0", "at_k1", "at_v0", "at_v1"]
               + [f"at_pr{i}" for i in range(8)] + ["at_ao", "at_af"]):
        ctx.ab.free(nm)
    ctx.af.free("at_acol")
    for i in range(DT):
        ctx.ab.free(f"xnb{i}")


def emit_l0_glu(ctx, st):
    p = ctx.p
    xres, xinner = st["xres"], st["xinner"]
    xnb = [ctx.ab.alloc(f"xnb{i}", T) for i in range(DT)]
    emit_norm(ctx, xinner, xnb, "rms", gamma="w_rms_ffn", eps=1e-6)

    gbuf = ctx.ab.alloc("glu_g", 512)
    act = [ctx.ab.alloc(f"glu_a{i}", T) for i in range(8)]
    ws = WStream(ctx, "glu_w", None, 32, 2 * D, nslots=2)
    wd_s = WStream(ctx, "glu_wd", None, 8, D, nslots=2)

    for g in range(4):
        for sub in range(8):
            mt = g * 8 + sub
            wgu = ws.load(ctx.drams["l0_gu"][mt])
            wgp, wup = wgu[:, 0:D], wgu[:, D:2 * D]
            for n in range(NT):
                ps = ctx.bank()
                pv = ps.tv[:, 0:512]
                pairs = [(wgp[:, kt * 128:(kt + 1) * 128],
                          xnb[kt][:, n * 512:(n + 1) * 512]) for kt in range(DT)]
                matmul_acc(ctx, pv, pairs, tag="mm_gate")
                o = gbuf[:, 0:512]
                p.op("scalar", lambda e, o=o, i=pv:
                     e.activation(o.ap, i.ap, AF.Silu),
                     reads=[pv], writes=[o], tag="ev_silu")
                ps2 = ctx.bank()
                pv2 = ps2.tv[:, 0:512]
                pairs = [(wup[:, kt * 128:(kt + 1) * 128],
                          xnb[kt][:, n * 512:(n + 1) * 512]) for kt in range(DT)]
                matmul_acc(ctx, pv2, pairs, tag="mm_up")
                oa = act[sub][:, n * 512:(n + 1) * 512]
                p.op("vector", lambda e, o=oa, i=pv2, gg=o:
                     e.tensor_tensor(o.ap, i.ap, gg.ap, ALU.mult),
                     reads=[pv2, o], writes=[oa], tag="ev_act")
        for mt in range(DT):
            wop = wd_s.load(ctx.drams[f"l0_wout{g}"][mt])
            for n in range(NT):
                ps = ctx.bank()
                pv = ps.tv[:, 0:512]
                pairs = [(wop[:, kt * 128:(kt + 1) * 128],
                          act[kt][:, n * 512:(n + 1) * 512]) for kt in range(DT)]
                matmul_acc(ctx, pv, pairs, tag="mm_down")
                xi = xinner[mt][:, n * 512:(n + 1) * 512]
                p.op("vector", lambda e, o=xi, i=pv:
                     e.tensor_tensor(o.ap, o.ap, i.ap, ALU.add),
                     reads=[pv, xi], writes=[xi], tag="res_glu")
    ws.release()
    wd_s.release()
    ctx.ab.free("glu_g")
    for i in range(8):
        ctx.ab.free(f"glu_a{i}")
    for i in range(DT):
        ctx.ab.free(f"xnb{i}")
    for i in range(DT):
        p.op("vector", lambda e, o=xres[i], b=xinner[i]:
             e.tensor_tensor(o.ap, o.ap, b.ap, ALU.add),
             reads=[xres[i], xinner[i]], writes=[xres[i]], tag="res_mod")
    for i in range(DT):
        ctx.ab.free(f"xi{i}")
    del st["xinner"]


def emit_l0_ffn(ctx, st):
    p = ctx.p
    xres = st["xres"]
    xnb = [ctx.ab.alloc(f"xnb{i}", T) for i in range(DT)]
    emit_norm(ctx, xres, xnb, "ln", eps=1e-5)

    fslot = [ctx.af.alloc("fw0", D)]
    accc = ctx.af.alloc("q_acc", 8)
    onef = ctx.of32
    p.op("vector", lambda e, o=accc[:, 0:1]: e.memset(o.ap, 0.0),
         writes=[accc[:, 0:1]], tag="q_z")
    for mt in range(DT):
        sl = fslot[0]
        src = ctx.drams["l0f_w"][mt]
        p.op("sync", lambda e, d=sl[:, 0:512], s=src:
             e.dma_start(out=d.ap, in_=s[:, 0:512]),
             writes=[sl[:, 0:512]], dma="l0fw0", tag="qw_load")
        p.op("scalar", lambda e, d=sl[:, 512:1024], s=src:
             e.dma_start(out=d.ap, in_=s[:, 512:1024]),
             writes=[sl[:, 512:1024]], dma="l0fw1", tag="qw_load")
        c = accc[:, 3:4]
        p.op("vector", lambda e, o=c, i=sl:
             e.tensor_reduce(o.ap, i.ap, AX.X, ALU.add, apply_absolute_value=True),
             reads=[sl], writes=[c], tag="q_abs")
        p.op("vector", lambda e, o=accc[:, 0:1], i=c:
             e.tensor_tensor(o.ap, o.ap, i.ap, ALU.add),
             reads=[accc[:, 0:1], c], writes=[accc[:, 0:1]], tag="q_accadd")
    ps = ctx.bank()
    tot = ps.tv[0:1, 0:1]
    p.op("tensor", lambda e, o=tot, a=onef[:, 0:1], b=accc[:, 0:1]:
         e.matmul(o.ap, a.ap, b.ap, start=True, stop=True),
         reads=[onef[:, 0:1], accc[:, 0:1]], writes=[tot], tag="q_tot")
    sc1 = ctx.rowf.tv[0:1, 0:1]
    p.op("vector", lambda e, o=sc1, i=tot:
         e.tensor_scalar(o.ap, i.ap, 1.0 / (D * D), 1e-5, ALU.mult, ALU.max),
         reads=[tot], writes=[sc1], tag="q_scale")
    ps2 = ctx.bank()
    p.op("tensor", lambda e, o=ps2.tv[:, 0:1], a=onef[0:1, 0:128], b=sc1:
         e.matmul(o.ap, a.ap, b.ap, start=True, stop=True),
         reads=[onef[0:1, 0:128], sc1], writes=[ps2.tv[:, 0:1]], tag="q_bs")
    p.op("vector", lambda e, o=accc[:, 2:3], i=ps2.tv[:, 0:1]:
         e.tensor_copy(o.ap, i.ap),
         reads=[ps2.tv[:, 0:1]], writes=[accc[:, 2:3]], tag="q_scol")
    p.op("vector", lambda e, o=accc[:, 1:2], i=accc[:, 2:3]:
         e.tensor_scalar(o.ap, i.ap, 0.5, None, ALU.mult),
         reads=[accc[:, 2:3]], writes=[accc[:, 1:2]], tag="q_thr")
    wq = ctx.ab.alloc("q_wq", DT * D)
    t_abs = ctx.af.alloc("q_tabs", D)
    t_sgn = ctx.ab.alloc("q_tsgn", D)
    for mt in range(DT):
        sl = fslot[0]
        src = ctx.drams["l0f_w"][mt]
        p.op("sync", lambda e, d=sl[:, 0:512], s=src:
             e.dma_start(out=d.ap, in_=s[:, 0:512]),
             writes=[sl[:, 0:512]], dma="l0fw0", tag="qw_load2")
        p.op("scalar", lambda e, d=sl[:, 512:1024], s=src:
             e.dma_start(out=d.ap, in_=s[:, 512:1024]),
             writes=[sl[:, 512:1024]], dma="l0fw1", tag="qw_load2")
        p.op("scalar", lambda e, o=t_abs, i=sl: e.activation(o.ap, i.ap, AF.Abs),
             reads=[sl], writes=[t_abs], tag="q_abs2")
        p.op("scalar", lambda e, o=t_sgn, i=sl: e.activation(o.ap, i.ap, AF.Sign),
             reads=[sl], writes=[t_sgn], tag="q_sgn")
        p.op("vector", lambda e, o=t_abs, i=t_abs, th=accc[:, 1:2]:
             e.tensor_scalar(o.ap, i.ap, th.ap, None, ALU.is_ge),
             reads=[t_abs, accc[:, 1:2]], writes=[t_abs], tag="q_ge")
        o = wq[:, mt * D:(mt + 1) * D]
        p.op("vector", lambda e, o=o, a=t_abs, b=t_sgn:
             e.tensor_tensor(o.ap, a.ap, b.ap, ALU.mult),
             reads=[t_abs, t_sgn], writes=[o], tag="q_mul")
    for mt in range(DT):
        for n in range(NT):
            ps = ctx.bank()
            pv = ps.tv[:, 0:512]
            pairs = [(wq[:, mt * D + kt * 128:mt * D + (kt + 1) * 128],
                      xnb[kt][:, n * 512:(n + 1) * 512]) for kt in range(DT)]
            matmul_acc(ctx, pv, pairs, tag="mm_tern")
            xr = xres[mt][:, n * 512:(n + 1) * 512]
            p.op("vector", lambda e, o=xr, i=pv, s=accc[:, 2:3]:
                 e.scalar_tensor_tensor(o.ap, i.ap, s.ap, o.ap, ALU.mult, ALU.add),
                 reads=[pv, accc[:, 2:3], xr], writes=[xr], tag="res_tern")
    ctx.ab.free("q_wq")
    ctx.af.free("q_tabs")
    ctx.ab.free("q_tsgn")
    ctx.af.free("q_acc")
    ctx.af.free("fw0")
    for i in range(DT):
        ctx.ab.free(f"xnb{i}")


MAMBA_DEBUG = False


def dbg_dump(ctx, name, tv, parts, free):
    if not MAMBA_DEBUG:
        return
    t = ctx.p.nc.dram_tensor("dbg_" + name, [parts, free], F32,
                             kind="ExternalOutput").ap()
    ctx.p.op("gpsimd", lambda e, d=t, s=tv[0:parts, 0:free]:
             e.dma_start(out=d, in_=s.ap),
             reads=[tv[0:parts, 0:free]], dma="dbg", tag="dbg")


def emit_l1_mamba(ctx, st):
    p = ctx.p
    xres = st["xres"]
    xnb = [ctx.ab.alloc(f"xnb{i}", T) for i in range(DT)]
    emit_norm(ctx, xres, xnb, "ln", eps=1e-5)
    for i in range(DT):
        dst = ctx.drams["xspill"][i * 128:(i + 1) * 128, :]
        p.op("sync", lambda e, d=dst, s=xres[i]: e.dma_start(out=d, in_=s.ap),
             reads=[xres[i]], dma="xsp_st", tag="x_spill")
    for i in range(DT):
        ctx.af.free(f"xr{i}")
    del st["xres"]

    h1 = [ctx.ab.alloc(f"h1_{i}", T) for i in range(DT)]
    ws = WStream(ctx, "mam_w", None, 8, D, nslots=2)
    for mt in range(DT):
        wp = ws.load(ctx.drams["l1_fc1"][mt])
        for n in range(NT):
            ps = ctx.bank()
            pv = ps.tv[:, 0:512]
            pairs = [(wp[:, kt * 128:(kt + 1) * 128],
                      xnb[kt][:, n * 512:(n + 1) * 512]) for kt in range(DT)]
            matmul_acc(ctx, pv, pairs, tag="mm_fc1")
            o = h1[mt][:, n * 512:(n + 1) * 512]
            b = ctx.vcol("fc1_b", mt)
            p.op("vector", lambda e, o=o, i=pv, b=b:
                 e.tensor_scalar(o.ap, i.ap, b.ap, None, ALU.add),
                 reads=[pv, b], writes=[o], tag="ev_h1")
    for i in range(DT):
        ctx.ab.free(f"xnb{i}")

    xpd = ctx.ab.alloc("xpd", 16 * 64)
    xpb = ctx.ab.alloc("xpb", 16 * 16)
    xpc = ctx.ab.alloc("xpc", 16 * 16)
    dtpb = ctx.ab.alloc("dtpb", 16 * 128)
    for kt in range(16):
        for nm, dst, w in (("xpd", xpd[:, kt * 64:(kt + 1) * 64], 64),
                           ("xpb", xpb[0:128, kt * 16:(kt + 1) * 16], 16),
                           ("xpc", xpc[0:128, kt * 16:(kt + 1) * 16], 16)):
            src = ctx.drams["l1_" + nm][kt]
            p.op("gpsimd", lambda e, d=dst, s=src: e.dma_start(out=d.ap, in_=s),
                 writes=[dst], dma=f"ld_{nm}", tag="xp_load")
        dst = dtpb[0:64, kt * 128:(kt + 1) * 128]
        src = ctx.drams["l1_dtp"][kt]
        p.op("gpsimd", lambda e, d=dst, s=src: e.dma_start(out=d.ap, in_=s),
             writes=[dst], dma="ld_dtp", tag="dtp_load")

    Abuf = ctx.af.alloc("Abuf", 256)
    for dtile in range(16):
        src = ctx.vcols("Alog", dtile * 16, (dtile + 1) * 16)
        dst = Abuf[:, dtile * 16:(dtile + 1) * 16]
        p.op("scalar", lambda e, o=dst, i=src: e.activation(o.ap, i.ap, AF.Exp),
             reads=[src], writes=[dst], tag="A_exp")
        p.op("vector", lambda e, o=dst, i=dst:
             e.tensor_scalar(o.ap, i.ap, -1.0, None, ALU.mult),
             reads=[dst], writes=[dst], tag="A_neg")

    accd = ctx.af.alloc("accd", T)
    accb = ctx.af.alloc("accb", T)
    accc = ctx.af.alloc("accc", T)
    for nm, tv, pp in (("accd", accd, 64), ("accb", accb, 16), ("accc", accc, 16)):
        v = tv[0:pp, :]
        p.op("vector", lambda e, o=v: e.memset(o.ap, 0.0), writes=[v], tag="xdbl_z")
    xcpad = ctx.af.alloc("xcpad", T + 16)
    ctmp = ctx.af.alloc("ctmp", T)
    ug = [ctx.ab.alloc(f"ug{i}", T) for i in range(4)]

    def mm_stream(pv, wp, h1_src, n, tag):
        for kt in range(DT):
            lt = wp[:, kt * 128:(kt + 1) * 128]
            rt = h1_src(kt, n)
            p.op("tensor",
                 lambda e, o=pv, a=lt, b=rt, s=(kt == 0), t=(kt == DT - 1):
                     e.matmul(o.ap, a.ap, b.ap, start=s, stop=t),
                 reads=[lt, rt], writes=[pv], tag=tag)

    def make_u(dtile, udst, h1_src):
        wp = ws.load(ctx.drams["l1_inproj"][dtile])
        z3 = xcpad[:, 0:3]
        p.op("vector", lambda e, o=z3: e.memset(o.ap, 0.0), writes=[z3], tag="xc_z")
        for n in range(NT):
            ps = ctx.bank()
            pv = ps.tv[:, 0:512]
            mm_stream(pv, wp, h1_src, n, "mm_xc")
            o = xcpad[:, 3 + n * 512:3 + (n + 1) * 512]
            p.op("vector", lambda e, o=o, i=pv: e.tensor_copy(o.ap, i.ap),
                 reads=[pv], writes=[o], tag="ev_xc")
        c0 = ctx.vcol("cw0", dtile)
        p.op("vector", lambda e, o=ctmp, i=xcpad[:, 0:T], s=c0:
             e.tensor_scalar(o.ap, i.ap, s.ap, None, ALU.mult),
             reads=[xcpad[:, 0:T], c0], writes=[ctmp], tag="conv0")
        for w in range(1, 4):
            cw = ctx.vcol(f"cw{w}", dtile)
            sl = xcpad[:, w:w + T]
            p.op("vector", lambda e, o=ctmp, i=sl, s=cw, a=ctmp:
                 e.scalar_tensor_tensor(o.ap, i.ap, s.ap, a.ap, ALU.mult, ALU.add),
                 reads=[sl, cw, ctmp], writes=[ctmp], tag=f"conv{w}")
        cb = ctx.vcol("conv_b", dtile)
        p.op("scalar", lambda e, o=udst, i=ctmp, b=cb:
             e.activation(o.ap, i.ap, AF.Silu, bias=b.ap),
             reads=[ctmp, cb], writes=[udst], tag="u_silu")

    h1_res = lambda kt, n: h1[kt][:, n * 512:(n + 1) * 512]
    for g4 in range(4):
        for j in range(4):
            make_u(g4 * 4 + j, ug[j], h1_res)
        for (acc, pp, wsl, msz) in ((accd, 64, xpd, 64), (accb, 16, xpb, 16),
                                    (accc, 16, xpc, 16)):
            for n in range(NT):
                ps = ctx.bank()
                pv = ps.tv[0:pp, 0:512]
                pairs = [(wsl[0:128, (g4 * 4 + j) * msz:(g4 * 4 + j + 1) * msz],
                          ug[j][:, n * 512:(n + 1) * 512]) for j in range(4)]
                matmul_acc(ctx, pv, pairs, tag="mm_xdbl")
                o = acc[0:pp, n * 512:(n + 1) * 512]
                p.op("vector", lambda e, o=o, i=pv:
                     e.tensor_tensor(o.ap, o.ap, i.ap, ALU.add),
                     reads=[pv, o], writes=[o], tag="xdbl_add")
    for i in range(4):
        ctx.ab.free(f"ug{i}")

    dbg_dump(ctx, "accd", accd, 64, T)
    dbg_dump(ctx, "accb", accb, 16, T)
    dbg_dump(ctx, "accc", accc, 16, T)
    dtlow = ctx.ab.alloc("dtlow", T)
    Bbf = ctx.ab.alloc("Bbf", T)[0:16, :]
    Cbf_full = ctx.ab.alloc("Cbf", T)[0:16, :]
    for (src, d, pp) in ((accd[0:64, :], dtlow[0:64, :], 64),
                         (accb[0:16, :], Bbf, 16),
                         (accc[0:16, :], Cbf_full, 16)):
        s = src
        p.op("vector", lambda e, o=d, i=s: e.tensor_copy(o.ap, i.ap),
             reads=[s], writes=[d], tag="xdbl_cast")
    ctx.af.free("accd")
    ctx.af.free("accb")
    ctx.af.free("accc")

    u2 = [ctx.ab.alloc(f"u{i}", T) for i in range(2)]
    zs2 = [ctx.ab.alloc(f"zs{i}", T) for i in range(2)]
    X2 = [ctx.ab.alloc(f"X{i}", T) for i in range(2)]
    hb2 = [ctx.ab.alloc("hb0", T)] * 2
    Bbs2 = [ctx.ab.alloc(f"Bbs{i}", T) for i in range(2)]
    Cbs2 = [ctx.ab.alloc(f"Cbs{i}", T) for i in range(2)]
    dtu2 = [ctx.ab.alloc(f"dtu{i}", T) for i in range(2)]
    dtt2 = [ctx.af.alloc(f"dtt{i}", T) for i in range(2)]
    ysa2 = [ctx.af.alloc(f"ysa{i}", T) for i in range(2)]
    dA2 = [ctx.af.alloc(f"dA{i}", T) for i in range(2)]
    sp = [ctx.af.alloc(f"sp{i}", 512) for i in range(3)]

    def pre_tile(dtile, sub):
        u_t, zs, dtt, dtu, ysa = (u2[sub], zs2[sub], dtt2[sub], dtu2[sub],
                                  ysa2[sub])
        make_u(dtile, u_t, h1_res)
        wp = ws.load(ctx.drams["l1_inproj"][16 + dtile])
        for n in range(NT):
            ps = ctx.bank()
            pv = ps.tv[:, 0:512]
            mm_stream(pv, wp, h1_res, n, "mm_z")
            o = zs[:, n * 512:(n + 1) * 512]
            p.op("scalar", lambda e, o=o, i=pv: e.activation(o.ap, i.ap, AF.Silu),
                 reads=[pv], writes=[o], tag="ev_zs")
        bdt = ctx.vcol("dt_b", dtile)
        for n in range(NT):
            ps = ctx.bank()
            pv = ps.tv[:, 0:512]
            lt = dtpb[0:64, dtile * 128:(dtile + 1) * 128]
            rt = dtlow[0:64, n * 512:(n + 1) * 512]
            matmul_acc(ctx, pv, [(lt, rt)], tag="mm_dt")
            s0, s1, s2 = sp
            p.op("scalar", lambda e, o=s0, i=pv, b=bdt:
                 e.activation(o.ap, i.ap, AF.Abs, bias=b.ap),
                 reads=[pv, bdt], writes=[s0], tag="sp_abs")
            p.op("scalar", lambda e, o=s1, i=s0:
                 e.activation(o.ap, i.ap, AF.Exp, scale=-1.0),
                 reads=[s0], writes=[s1], tag="sp_exp")
            p.op("scalar", lambda e, o=s1, i=s1, b=ctx.vcol("one"):
                 e.activation(o.ap, i.ap, AF.Ln, bias=b.ap),
                 reads=[s1, ctx.vcol("one")], writes=[s1], tag="sp_ln")
            p.op("scalar", lambda e, o=s2, i=pv, b=bdt:
                 e.activation(o.ap, i.ap, AF.Relu, bias=b.ap),
                 reads=[pv, bdt], writes=[s2], tag="sp_relu")
            o = dtt[:, n * 512:(n + 1) * 512]
            p.op("vector", lambda e, o=o, a=s1, b=s2:
                 e.tensor_tensor(o.ap, a.ap, b.ap, ALU.add),
                 reads=[s1, s2], writes=[o], tag="sp_add")
        p.op("vector", lambda e, o=dtu, a=dtt, b=u_t:
             e.tensor_tensor(o.ap, a.ap, b.ap, ALU.mult),
             reads=[dtt, u_t], writes=[dtu], tag="dtu")
        p.op("vector", lambda e, o=ysa: e.memset(o.ap, 0.0), writes=[ysa],
             tag="ys_z")

    def post_tile(dtile, sub):
        u_t, zs, ysa = u2[sub], zs2[sub], ysa2[sub]
        Dc = ctx.vcol("Dvec", dtile)
        p.op("vector", lambda e, o=ysa, i=u_t, s=Dc, a=ysa:
             e.scalar_tensor_tensor(o.ap, i.ap, s.ap, a.ap, ALU.mult, ALU.add),
             reads=[u_t, Dc, ysa], writes=[ysa], tag="y_uD")
        yt = X2[sub]
        p.op("vector", lambda e, o=yt, a=ysa, b=zs:
             e.tensor_tensor(o.ap, a.ap, b.ap, ALU.mult),
             reads=[ysa, zs], writes=[yt], tag="y_z")
        dst = ctx.drams["ysp"][dtile * 128:(dtile + 1) * 128, :]
        p.op("sync", lambda e, d=dst, s=yt: e.dma_start(out=d, in_=s.ap),
             reads=[yt], dma=f"ysp{sub}", tag="y_spill")

    for dp in range(8):
        pre_tile(2 * dp, 0)
        pre_tile(2 * dp + 1, 1)
        for sidx in range(16):
            Bbs, Cbs = Bbs2[sidx % 2], Cbs2[sidx % 2]
            sel = ctx.sel16[0:16, sidx * 128:(sidx + 1) * 128]
            for n in range(NT):
                ps = ctx.bank()
                pv = ps.tv[:, 0:512]
                rt = Bbf[:, n * 512:(n + 1) * 512]
                matmul_acc(ctx, pv, [(sel, rt)], tag="mm_bb")
                o = Bbs[:, n * 512:(n + 1) * 512]
                p.op("scalar", lambda e, o=o, i=pv:
                     e.activation(o.ap, i.ap, AF.Identity),
                     reads=[pv], writes=[o], tag="ev_bb")
                ps2 = ctx.bank()
                pv2 = ps2.tv[:, 0:512]
                rt2 = Cbf_full[:, n * 512:(n + 1) * 512]
                matmul_acc(ctx, pv2, [(sel, rt2)], tag="mm_cb")
                o2 = Cbs[:, n * 512:(n + 1) * 512]
                p.op("scalar", lambda e, o=o2, i=pv2:
                     e.activation(o.ap, i.ap, AF.Identity),
                     reads=[pv2], writes=[o2], tag="ev_cb")
            for sub in range(2):
                dtile = 2 * dp + sub
                dtt, dtu, ysa = dtt2[sub], dtu2[sub], ysa2[sub]
                dAb, Xb, hb = dA2[sub], X2[sub], hb2[sub]
                acol = Abuf[:, dtile * 16 + sidx:dtile * 16 + sidx + 1]
                p.op("scalar", lambda e, o=dAb, i=dtt, s=acol:
                     e.activation(o.ap, i.ap, AF.Exp, scale=s.ap),
                     reads=[dtt, acol], writes=[dAb], tag="dA")
                p.op("vector", lambda e, o=Xb, a=dtu, b=Bbs:
                     e.tensor_tensor(o.ap, a.ap, b.ap, ALU.mult),
                     reads=[dtu, Bbs], writes=[Xb], tag="X_mul")
                p.op("vector", lambda e, o=hb, a=dAb, x=Xb:
                     e.tensor_tensor_scan(o.ap, a.ap, x.ap, 0.0, ALU.mult,
                                          ALU.add),
                     reads=[dAb, Xb], writes=[hb], tag="scan")
                p.op("vector", lambda e, o=Xb, a=hb, b=Cbs:
                     e.tensor_tensor(o.ap, a.ap, b.ap, ALU.mult),
                     reads=[hb, Cbs], writes=[Xb], tag="hC_mul")
                p.op("gpsimd", lambda e, o=ysa, a=ysa, b=Xb:
                     e.tensor_tensor(o.ap, a.ap, b.ap, ALU.add),
                     reads=[ysa, Xb], writes=[ysa], tag="ys_add")
        post_tile(2 * dp, 0)
        post_tile(2 * dp + 1, 1)

    for nm in ["zs0", "zs1", "u0", "u1", "X0", "X1", "hb0", "dtlow",
               "Bbf", "Cbf", "dtpb", "xpd", "xpb", "xpc", "Bbs0", "Bbs1",
               "Cbs0", "Cbs1", "dtu0", "dtu1"]:
        ctx.ab.free(nm)
    for nm in ["dtt0", "dtt1", "ysa0", "ysa1", "dA0", "dA1", "ctmp", "Abuf",
               "xcpad"]:
        ctx.af.free(nm)
    for i in range(3):
        ctx.af.free(f"sp{i}")
    for i in range(DT):
        ctx.ab.free(f"h1_{i}")

    opw = ctx.ab.alloc("opw", DT * 2048)
    for mt in range(DT):
        dst = opw[:, mt * 2048:(mt + 1) * 2048]
        srcw = ctx.drams["l1_outproj"][mt]
        p.op("gpsimd", lambda e, d=dst, s=srcw: e.dma_start(out=d.ap, in_=s),
             writes=[dst], dma=f"opw{mt}", tag="opw_load")
    ych = ctx.ab.alloc("ych", 16 * 512)
    oacc = [ctx.ab.alloc(f"oac{i}", T) for i in range(DT)]
    for n in range(NT):
        for kt in range(16):
            dst = ych[:, kt * 512:(kt + 1) * 512]
            srcy = ctx.drams["ysp"][kt * 128:(kt + 1) * 128,
                                    n * 512:(n + 1) * 512]
            p.op("sync", lambda e, d=dst, s=srcy: e.dma_start(out=d.ap, in_=s),
                 writes=[dst], dma=f"ych{kt}", tag="ych_load")
        for mt in range(DT):
            ps = ctx.bank()
            pv = ps.tv[:, 0:512]
            pairs = [(opw[:, mt * 2048 + kt * 128:mt * 2048 + (kt + 1) * 128],
                      ych[:, kt * 512:(kt + 1) * 512]) for kt in range(16)]
            matmul_acc(ctx, pv, pairs, tag="mm_op")
            o = oacc[mt][:, n * 512:(n + 1) * 512]
            p.op("vector", lambda e, o=o, i=pv: e.tensor_copy(o.ap, i.ap),
                 reads=[pv], writes=[o], tag="ev_op")
    ctx.ab.free("opw")
    ctx.ab.free("ych")

    dbg_dump(ctx, "oac0", oacc[0], 128, T)
    xres = [ctx.af.alloc(f"xr{i}", T) for i in range(DT)]
    st["xres"] = xres
    xsrc = ctx.drams["xspill"]
    for i in range(DT):
        src = xsrc[i * 128:(i + 1) * 128, :]
        p.op("sync", lambda e, d=xres[i], s=src: e.dma_start(out=d.ap, in_=s),
             writes=[xres[i]], dma=f"xrl{i}", tag="x_reload")
    for mt in range(DT):
        wp = ws.load(ctx.drams["l1_fc2"][mt])
        for n in range(NT):
            ps = ctx.bank()
            pv = ps.tv[:, 0:512]
            pairs = [(wp[:, kt * 128:(kt + 1) * 128],
                      oacc[kt][:, n * 512:(n + 1) * 512]) for kt in range(DT)]
            matmul_acc(ctx, pv, pairs, tag="mm_fc2")
            xr = xres[mt][:, n * 512:(n + 1) * 512]
            b2 = ctx.vcol("fc2_b", mt)
            p.op("vector", lambda e, o=xr, i=pv, s=b2:
                 e.scalar_tensor_tensor(o.ap, i.ap, s.ap, o.ap, ALU.add, ALU.add),
                 reads=[pv, b2, xr], writes=[xr], tag="res_fc2")
    for i in range(DT):
        ctx.ab.free(f"oac{i}")
    ws.release()


def emit_l1_moe(ctx, st):
    p = ctx.p
    xres = st["xres"]
    xnb = [ctx.ab.alloc(f"xnb{i}", T) for i in range(DT)]
    smalls = ctx.af.alloc("moe_sm", 64)
    mcols = smalls[:, 16:32]
    rcols = smalls[:, 32:48]
    emit_norm(ctx, xres, xnb, "ln", eps=1e-5, exact=True,
              keep_cols=(mcols, rcols))

    gwf = ctx.af.alloc("moe_gw", 8 * DT)
    for kt in range(DT):
        dst = gwf[:, kt * 8:(kt + 1) * 8]
        src = ctx.drams["moe_gate"][kt]
        p.op("sync", lambda e, d=dst, s=src: e.dma_start(out=d.ap, in_=s),
             writes=[dst], dma="moe_gw", tag="gw_load")
    ps = ctx.bank()
    cs_ps = ps.tv[0:1, 0:8]
    pairs = [(ctx.of32[:, 0:1], gwf[:, kt * 8:(kt + 1) * 8]) for kt in range(DT)]
    matmul_acc(ctx, cs_ps, pairs, tag="mm_cs")
    csrow = smalls[0:1, 56:64]
    p.op("vector", lambda e, o=csrow, i=cs_ps: e.tensor_copy(o.ap, i.ap),
         reads=[cs_ps], writes=[csrow], tag="ev_cs")
    csb_ps = ctx.bank()
    p.op("tensor", lambda e, o=csb_ps.tv[:, 0:8], a=ctx.of32[0:1, 0:128], b=csrow:
         e.matmul(o.ap, a.ap, b.ap, start=True, stop=True),
         reads=[ctx.of32[0:1, 0:128], csrow], writes=[csb_ps.tv[:, 0:8]],
         tag="mm_csb")
    csb = smalls[:, 0:8]
    p.op("vector", lambda e, o=csb, i=csb_ps.tv[:, 0:8]: e.tensor_copy(o.ap, i.ap),
         reads=[csb_ps.tv[:, 0:8]], writes=[csb], tag="ev_csb")
    mask_tm = ctx.af.alloc("moe_msk", 16)
    logit_t = ctx.af.alloc("moe_lg", 16)
    srow_b = ctx.rowb.tv[0:1, 1024:3072]
    mkb = ctx.ab.alloc("moe_mkb", 16)
    mask8 = ctx.ab.alloc("moe_mask8", T)
    for ti in range(16):
        psx = ctx.bank()
        raw = psx.tv[:, 0:8]
        pairs = [(xres[kt][:, ti * 128:(ti + 1) * 128],
                  gwf[:, kt * 8:(kt + 1) * 8]) for kt in range(DT)]
        matmul_acc(ctx, raw, pairs, tag="mm_lg")
        lg = logit_t[:, 0:8]
        mc = mcols[:, ti:ti + 1]
        rc = rcols[:, ti:ti + 1]
        t1 = mask_tm[:, 8:16]
        p.op("vector", lambda e, o=t1, i=csb, s=mc:
             e.tensor_scalar(o.ap, i.ap, s.ap, None, ALU.mult),
             reads=[csb, mc], writes=[t1], tag="lg_mcs")
        p.op("vector", lambda e, o=lg, a=raw, b=t1:
             e.tensor_tensor(o.ap, a.ap, b.ap, ALU.subtract),
             reads=[raw, t1], writes=[lg], tag="lg_sub")
        p.op("vector", lambda e, o=lg, i=lg, s=rc:
             e.tensor_scalar(o.ap, i.ap, s.ap, None, ALU.mult),
             reads=[lg, rc], writes=[lg], tag="lg_rs")
        nmax = smalls[:, 48:49]
        p.op("vector", lambda e, o=nmax, i=lg:
             e.reduce_max(o.ap, i.ap, AX.X, negate=True),
             reads=[lg], writes=[nmax], tag="lg_max")
        top = smalls[:, 49:50]
        p.op("vector", lambda e, o=top, i=nmax:
             e.tensor_scalar(o.ap, i.ap, -1.0, None, ALU.mult),
             reads=[nmax], writes=[top], tag="lg_top")
        mk = mask_tm[:, 0:8]
        p.op("vector", lambda e, o=mk, i=lg, s=top:
             e.tensor_scalar(o.ap, i.ap, s.ap, None, ALU.is_equal),
             reads=[lg, top], writes=[mk], tag="lg_eq")
        sc = smalls[:, 50:51]
        p.op("scalar", lambda e, o=sc, i=nmax:
             e.activation(o.ap, i.ap, AF.Sigmoid, scale=-1.0),
             reads=[nmax], writes=[sc], tag="lg_sig")
        psy = ctx.bank()
        pv = psy.tv[0:1, 0:128]
        p.op("tensor", lambda e, o=pv, a=sc, b=ctx.if32:
             e.matmul(o.ap, a.ap, b.ap, start=True, stop=True),
             reads=[sc, ctx.if32], writes=[pv], tag="mm_scT")
        o = srow_b[0:1, ti * 128:(ti + 1) * 128]
        p.op("vector", lambda e, o=o, i=pv: e.tensor_copy(o.ap, i.ap),
             reads=[pv], writes=[o], tag="ev_scT")
        mcb = mkb[:, 0:8]
        mk8 = mask_tm[:, 0:8]
        p.op("vector", lambda e, o=mcb, i=mk8: e.tensor_copy(o.ap, i.ap),
             reads=[mk8], writes=[mcb], tag="mk_cast")
        psz = ctx.bankbf()
        pv = psz.tv[0:8, 0:128]
        transpose128(ctx, pv, mcb)
        o = mask8[0:8, ti * 128:(ti + 1) * 128]
        p.op("vector", lambda e, o=o, i=pv: e.tensor_copy(o.ap, i.ap),
             reads=[pv], writes=[o], tag="ev_mkT")
    ctx.af.free("moe_msk")
    ctx.af.free("moe_lg")
    ctx.af.free("moe_gw")

    gbuf = ctx.ab.alloc("moe_gb", T)
    maskb = ctx.ab.alloc("moe_mb", T)
    act = [ctx.ab.alloc(f"moe_a{i}", T) for i in range(8)]
    ws = WStream(ctx, "moe_w", None, 8, 2 * D, nslots=2)

    def group4(wtile_fn, src_tiles, KT, evac_fn, tag):
        for n in range(NT):
            pv = ctx.bank().tv[:, 0:512]
            pairs = [(wtile_fn(kt), src_tiles[kt][:, n * 512:(n + 1) * 512])
                     for kt in range(KT)]
            matmul_acc(ctx, pv, pairs, tag=tag)
            evac_fn(n, pv)

    def expert(gu_dram, down_dram, src_tiles, masked):
        for g in range(2):
            for sub in range(8):
                mt = g * 8 + sub
                wgu = ws.load(gu_dram[mt])
                wp, wu = wgu[:, 0:D], wgu[:, D:2 * D]

                def ev_g(n, pv):
                    o = gbuf[:, n * 512:(n + 1) * 512]
                    p.op("scalar", lambda e, o=o, i=pv:
                         e.activation(o.ap, i.ap, AF.Silu),
                         reads=[pv], writes=[o], tag="ev_esilu")
                    if masked:
                        mb = maskb[:, n * 512:(n + 1) * 512]
                        p.op("vector", lambda e, o=o, a=o, b=mb:
                             e.tensor_tensor(o.ap, a.ap, b.ap, ALU.mult),
                             reads=[o, mb], writes=[o], tag="gm_mask")
                group4(lambda kt: wp[:, kt * 128:(kt + 1) * 128], src_tiles,
                       DT, ev_g, "mm_eg")

                def ev_u(n, pv):
                    oa = act[sub][:, n * 512:(n + 1) * 512]
                    o = gbuf[:, n * 512:(n + 1) * 512]
                    p.op("vector", lambda e, o=oa, a=pv, b=o:
                         e.tensor_tensor(o.ap, a.ap, b.ap, ALU.mult),
                         reads=[pv, o], writes=[oa], tag="ev_eact")
                group4(lambda kt: wu[:, kt * 128:(kt + 1) * 128], src_tiles,
                       DT, ev_u, "mm_eu")
            for mt in range(DT):
                wp = ws.load(down_dram[g][mt])

                def ev_d(n, pv, mt=mt):
                    xr = xres[mt][:, n * 512:(n + 1) * 512]
                    p.op("vector", lambda e, o=xr, i=pv:
                         e.tensor_tensor(o.ap, o.ap, i.ap, ALU.add),
                         reads=[pv, xr], writes=[xr], tag="res_moe")
                group4(lambda kt: wp[:, kt * 128:(kt + 1) * 128], act,
                       DT, ev_d, "mm_ed")

    expert(ctx.drams["moe_shgu"],
           [ctx.drams["moe_shd"][g] for g in range(2)], xnb, masked=False)
    xs = xnb
    for n in range(NT):
        psb_ = ctx.bank()
        pv = psb_.tv[:, 0:512]
        row = srow_b[0:1, n * 512:(n + 1) * 512]
        bcast_row(ctx, row, pv)
        for dt_ in range(DT):
            o = xs[dt_][:, n * 512:(n + 1) * 512]
            p.op("vector", lambda e, o=o, b=pv:
                 e.tensor_tensor(o.ap, o.ap, b.ap, ALU.mult),
                 reads=[o, pv], writes=[o], tag="xs_mul")
    for ei in range(8):
        sel = ctx.sel16[0:8, ei * 128:(ei + 1) * 128]
        for n in range(NT):
            psx = ctx.bank()
            pv = psx.tv[:, 0:512]
            rt = mask8[0:8, n * 512:(n + 1) * 512]
            matmul_acc(ctx, pv, [(sel, rt)], tag="mm_mb")
            o = maskb[:, n * 512:(n + 1) * 512]
            p.op("vector", lambda e, o=o, i=pv: e.tensor_copy(o.ap, i.ap),
                 reads=[pv], writes=[o], tag="ev_mb")
        expert(ctx.drams["moe_rtgu"][ei],
               [ctx.drams["moe_rtd"][ei][g] for g in range(2)], xs, masked=True)
    ws.release()
    for i in range(DT):
        ctx.ab.free(f"xnb{i}")
    for i in range(8):
        ctx.ab.free(f"moe_a{i}")
    for nm in ["moe_gb", "moe_mb", "moe_mkb", "moe_mask8"]:
        ctx.ab.free(nm)
    ctx.af.free("moe_sm")


def build_core_program(phases, host):
    nc = bass.Bass("TRN2", target_bir_lowering=False)
    p = Prog(nc)
    ctx = Ctx(p)

    def din(name, arr):
        t = nc.dram_tensor(name, list(arr.shape), F32, kind="ExternalInput").ap()
        ctx.drams[name] = t
        return t

    out_names = []

    def dout(name, shape):
        t = nc.dram_tensor(name, list(shape), F32, kind="ExternalOutput").ap()
        ctx.drams[name] = t
        out_names.append(name)
        return t

    wnames = set()
    for ph in phases:
        wnames.update(PHASE_WEIGHTS[ph])
    for name in sorted(wnames):
        din(name, host["dram_inputs"][name])
    din("xfm", np.zeros((D, T), np.float32))
    din("vp", host["vecpack"])

    need_xi_in = ("l0_attn" in phases or "l0_glu" in phases) \
        and "l0_mix" not in phases
    need_xi_out = ("l0_mix" in phases or "l0_attn" in phases) \
        and "l0_glu" not in phases
    need_x_out = "l0_glu" in phases or "l0_ffn" in phases \
        or "l1_mamba" in phases or "l1_moe" in phases
    if need_xi_in:
        din("xi_in", np.zeros((D, T), np.float32))
    if need_xi_out:
        dout("xi_out", (D, T))
    if need_x_out:
        dout("x_out", (D, T))

    AF_EL = max(PHASE_ARENA[ph][0] for ph in phases)
    AB_EL = max(PHASE_ARENA[ph][1] for ph in phases)
    af_buf = p.sbuf("arena_f32", 128, AF_EL, F32)
    ab_buf = p.sbuf("arena_bf16", 128, AB_EL, BF16)
    ctx.af = Arena(af_buf)
    ctx.ab = Arena(ab_buf)
    ctx.banks_f32 = [p.psum(f"pf{i}", 128, 512, F32) for i in range(6)]
    ctx.banks_bf = [p.psum(f"pb{i}", 128, 1024, BF16) for i in range(2)]
    vp_arr = host["vecpack"]
    ctx.vp = p.sbuf("vecpack", 128, vp_arr.shape[1], F32)
    ctx.vp_index = host["vp_index"]
    ctx.rowb = p.sbuf("rowb", 1, 3072, BF16)
    ctx.rowf = p.sbuf("rowf", 1, 2048, F32)
    if "cmask" in wnames:
        ctx.masks = p.sbuf("c_masks", 128, 512, BF16).tv
    if "csel" in wnames:
        ctx.sel16 = p.sbuf("c_sel16", 16, 2048, BF16).tv
    if "l1_fc1" in wnames:
        ctx.drams["xspill"] = nc.dram_tensor("xspill", [D, T], F32,
                                             kind="Internal").ap()
        ctx.drams["ysp"] = nc.dram_tensor("ysp", [2048, T], BF16,
                                          kind="Internal").ap()

    with nc.Block() as block:
        p.op("sync", lambda e: e.dma_start(out=ctx.vp.tv.ap, in_=ctx.drams["vp"]),
             writes=[ctx.vp.tv], dma="vp")
        emit_consts(ctx)
        if ctx.masks is not None:
            p.op("gpsimd", lambda e: e.dma_start(out=ctx.masks.ap,
                                                 in_=ctx.drams["cmask"]),
                 writes=[ctx.masks], dma="cmask")
        if ctx.sel16 is not None:
            p.op("gpsimd", lambda e: e.dma_start(out=ctx.sel16.ap,
                                                 in_=ctx.drams["csel"]),
                 writes=[ctx.sel16], dma="csel")

        st = {}
        xres = [ctx.af.alloc(f"xr{i}", T) for i in range(DT)]
        st["xres"] = xres
        for i in range(DT):
            src = ctx.drams["xfm"][i * 128:(i + 1) * 128, :]
            p.op("sync", lambda e, d=xres[i], s=src: e.dma_start(out=d.ap, in_=s),
                 writes=[xres[i]], dma=f"xload{i}")
        if need_xi_in:
            xinner = [ctx.ab.alloc(f"xi{i}", T) for i in range(DT)]
            st["xinner"] = xinner
            for i in range(DT):
                src = ctx.drams["xi_in"][i * 128:(i + 1) * 128, :]
                p.op("gpsimd", lambda e, d=xinner[i], s=src: e.dma_start(out=d.ap, in_=s),
                     writes=[xinner[i]], dma=f"xiload{i}")

        for ph in phases:
            PHASE_EMITTERS[ph](ctx, st)

        final_tvs = []
        if need_xi_out:
            for i in range(DT):
                dst = ctx.drams["xi_out"][i * 128:(i + 1) * 128, :]
                tv = st["xinner"][i]
                p.op("gpsimd", lambda e, d=dst, s=tv: e.dma_start(out=d, in_=s.ap),
                     reads=[tv], dma="st_out")
                final_tvs.append(tv)
        if need_x_out:
            for i in range(DT):
                dst = ctx.drams["x_out"][i * 128:(i + 1) * 128, :]
                tv = st["xres"][i]
                p.op("sync", lambda e, d=dst, s=tv: e.dma_start(out=d, in_=s.ap),
                     reads=[tv], dma="st_out")
                final_tvs.append(tv)
        n_st = sum(1 for o in p.ops if o.dma_sem == "st_out")
        p.op("sync", lambda e, n=n_st: e.wait_ge(SEMS["st_out"], 16 * n),
             tag="final_wait")

        global SEMS
        SEMS = {}
        sem_cms = []
        names = list(ENGINES) + p.dma_sems_used()
        for nm in names:
            cm = nc.semaphore(f"s_{nm}")
            SEMS[nm] = cm.__enter__()
            sem_cms.append(cm)
        p.emit(block, SEMS)
        for cm in reversed(sem_cms):
            cm.__exit__(None, None, None)
    p.close_bufs()
    return nc, out_names


SEMS = {}
PHASE_EMITTERS = {"l0_mix": emit_l0_mix, "l0_attn": emit_l0_attn,
                  "l0_glu": emit_l0_glu, "l0_ffn": emit_l0_ffn,
                  "l1_mamba": emit_l1_mamba, "l1_moe": emit_l1_moe}
PHASE_ARENA = {
    "l0_mix": (19000, 54000),
    "l0_attn": (17500, 53000),
    "l0_glu": (17500, 56000),
    "l0_ffn": (19000, 50000),
    "l1_mamba": (18500, 55000),
    "l1_moe": (19000, 47000),
}
PHASE_WEIGHTS = {
    "l0_mix": ["l0_gvd"],
    "l0_attn": ["l0_qkv3", "l0_wo", "cmask"],
    "l0_glu": ["l0_gu", "l0_wout0", "l0_wout1", "l0_wout2", "l0_wout3"],
    "l0_ffn": ["l0f_w"],
    "l1_mamba": ["l1_fc1", "l1_inproj", "l1_xpd", "l1_xpb", "l1_xpc",
                 "l1_dtp", "l1_outproj", "l1_fc2", "csel"],
    "l1_moe": ["moe_gate", "moe_shgu", "moe_shd", "moe_rtgu", "moe_rtd",
               "csel"],
}


def prep_host(inp):
    vp = VecPack()
    vp.add("w_rms_mix", inp["L0_attn_w_rms_mix"])
    vp.add("bg", inp["L0_attn_bg"])
    vp.add("bv", inp["L0_attn_bv"])
    vp.add("bd", inp["L0_attn_bd"])
    vp.add("w_rms_attn", inp["L0_attn_w_rms_attn"])
    vp.add("w_rms_ffn", inp["L0_attn_w_rms_ffn"])
    vp.add("eps_1e5", np.full(128, 1e-5, np.float32))
    vp.add("eps_1e6", np.full(128, 1e-6, np.float32))
    vp.add("one", np.full(128, 1.0, np.float32))
    vp.add("fc1_b", inp["L1_attn_fc1_b"])
    for w in range(4):
        vp.add(f"cw{w}", inp["L1_attn_conv_w"][:, w])
    vp.add("conv_b", inp["L1_attn_conv_b"])
    vp.add("dt_b", inp["L1_attn_dt_proj_b"])
    vp.add("Dvec", inp["L1_attn_D"])
    vp.add("fc2_b", inp["L1_attn_fc2_b"])
    al = np.asarray(inp["L1_attn_A_log"], np.float32)
    vp.add_image("Alog", al.reshape(16, 128, 16).transpose(1, 0, 2).reshape(128, 256))

    j = np.arange(128)[:, None]
    kk = np.arange(256)[None, :]
    maskN = np.where(kk < 128, (kk > j), (kk - 128) <= j).astype(np.float32)
    mask0 = np.where(kk < 128, False, (kk - 128) <= j).astype(np.float32)
    cmask = np.concatenate([maskN, mask0], axis=1)

    wout = inp["L0_attn_Wout"]
    _g = lhsT_panels(inp["L0_attn_Wg"])
    _v = lhsT_panels(inp["L0_attn_Wv"])
    _d = lhsT_panels(inp["L0_attn_Wd"])
    _qkv = lhsT_panels(inp["L0_attn_Wqkv"])
    _ga = lhsT_panels(inp["L0_attn_Wgate"])
    _up = lhsT_panels(inp["L0_attn_Wup"])
    dram = {
        "l0_gvd": np.ascontiguousarray(np.concatenate([_g, _v, _d], axis=2)),
        "l0_qkv3": np.ascontiguousarray(np.concatenate(
            [_qkv[0:8], _qkv[8:16], _qkv[16:24]], axis=2)),
        "l0_wo": lhsT_panels(inp["L0_attn_Wo"]),
        "l0_gu": np.ascontiguousarray(np.concatenate([_ga, _up], axis=2)),
        "l0_wout0": lhsT_panels(wout[:, 0:1024]),
        "l0_wout1": lhsT_panels(wout[:, 1024:2048]),
        "l0_wout2": lhsT_panels(wout[:, 2048:3072]),
        "l0_wout3": lhsT_panels(wout[:, 3072:4096]),
        "l0f_w": lhsT_panels(inp["L0_ffn_weight"]),
        "cmask": np.ascontiguousarray(cmask),
    }
    xpw = np.asarray(inp["L1_attn_x_proj_w"], np.float32)
    dtpw = np.asarray(inp["L1_attn_dt_proj_w"], np.float32)
    sel = np.zeros((16, 16, 128), np.float32)
    for i in range(16):
        sel[i, i, :] = 1.0
    dram.update({
        "l1_fc1": lhsT_panels(inp["L1_attn_fc1_w"]),
        "l1_inproj": lhsT_panels(inp["L1_attn_in_proj_w"]),
        "l1_xpd": np.ascontiguousarray(
            xpw[0:64].reshape(64, 16, 128).transpose(1, 2, 0)),
        "l1_xpb": np.ascontiguousarray(
            xpw[64:80].reshape(16, 16, 128).transpose(1, 2, 0)),
        "l1_xpc": np.ascontiguousarray(
            xpw[80:96].reshape(16, 16, 128).transpose(1, 2, 0)),
        "l1_dtp": np.ascontiguousarray(
            dtpw.reshape(16, 128, 64).transpose(0, 2, 1)),
        "l1_outproj": lhsT_panels(inp["L1_attn_out_proj_w"]),
        "l1_fc2": lhsT_panels(inp["L1_attn_fc2_w"]),
        "csel": np.ascontiguousarray(sel.reshape(16, 2048)),
    })
    shd = np.asarray(inp["L1_ffn_sh_down"], np.float32).T
    dram.update({
        "moe_gate": np.ascontiguousarray(
            np.asarray(inp["L1_ffn_gate_w"], np.float32).reshape(8, 128, 8)),
        "moe_shgu": np.concatenate(
            [lhsT_panels(np.asarray(inp["L1_ffn_sh_gate"], np.float32).T),
             lhsT_panels(np.asarray(inp["L1_ffn_sh_up"], np.float32).T)], axis=2),
        "moe_shd": np.stack([lhsT_panels(shd[:, g * 1024:(g + 1) * 1024])
                             for g in range(2)]),
        "moe_rtgu": np.stack([np.concatenate(
            [lhsT_panels(np.asarray(inp["L1_ffn_rt_gate"][e], np.float32).T),
             lhsT_panels(np.asarray(inp["L1_ffn_rt_up"][e], np.float32).T)],
            axis=2) for e in range(8)]),
        "moe_rtd": np.stack([np.stack([
            lhsT_panels(np.asarray(inp["L1_ffn_rt_down"][e], np.float32)
                        .T[:, g * 1024:(g + 1) * 1024]) for g in range(2)])
            for e in range(8)]),
    })
    host = {
        "dram_inputs": dram,
        "vecpack": vp.array(),
        "vp_index": vp.index,
    }
    return host


def run_launch(phases, host, per_core_io):
    nc, out_names = build_core_program(phases, host)
    wnames = set()
    for ph in phases:
        wnames.update(PHASE_WEIGHTS[ph])
    in_maps = []
    for c in range(NCORES):
        m = {name: host["dram_inputs"][name] for name in wnames}
        m["vp"] = host["vecpack"]
        m.update(per_core_io[c])
        in_maps.append(m)
    res = run_bass_kernel_spmd(nc, in_maps, core_ids=list(range(NCORES)))
    return res.results, out_names


NCORES = 8

LAUNCHES = [ALL_PHASES]


def kernel(**inputs):
    host = prep_host(inputs)
    x = np.asarray(inputs["L0_attn_x"], np.float32)
    xfm = np.ascontiguousarray(x.transpose(0, 2, 1))
    cur_x = [xfm[c] for c in range(NCORES)]
    cur_xi = [None] * NCORES
    for phases in LAUNCHES:
        io = []
        for c in range(NCORES):
            m = {"xfm": cur_x[c]}
            if ("l0_attn" in phases or "l0_glu" in phases)                     and "l0_mix" not in phases:
                m["xi_in"] = cur_xi[c]
            io.append(m)
        results, outs = run_launch(phases, host, io)
        for c in range(NCORES):
            if "xi_out" in outs:
                cur_xi[c] = results[c]["xi_out"]
            if "x_out" in outs:
                cur_x[c] = results[c]["x_out"]
    out = np.stack(cur_x)
    return np.ascontiguousarray(out.transpose(0, 2, 1))
